# version 12
# baseline (speedup 1.0000x reference)
import numpy as np
import concourse.bass as bass
import concourse.mybir as mybir
from concourse import library_config
from concourse.bass_utils import run_bass_kernel_spmd
from contextlib import ExitStack

F32 = mybir.dt.float32
F16 = mybir.dt.float16
AF = mybir.ActivationFunctionType
ALU = mybir.AluOpType
AX = mybir.AxisListType

S = 2048
D = 1024
NCH = 8
NTT = 16
NKT = 16
QT = 512
NQT = 4
FQT = 256
NFQT = S // FQT
LN_EPS = 1e-5
ATT_EPS = 1e-6


class Sem:
    __slots__ = ("h", "n")

    def __init__(self, h):
        self.h = h
        self.n = 0


class Inst:
    __slots__ = ("raw",)

    def __init__(self, raw):
        self.raw = raw

    def inc(self, sem, k=1):
        sem.n += k
        if self.raw is not None:
            self.raw.then_inc(sem.h, k)
        return self


class Eng:

    def __init__(self, real):
        self.real = real

    def wait(self, sem, n=None):
        v = sem.n if n is None else n
        if self.real is not None and v > 0:
            self.real.wait_ge(sem.h, v)

    def waits(self, marks):
        for (sem, n) in marks:
            self.wait(sem, n)

    def __getattr__(self, op):
        real = self.real

        def call(*a, **k):
            return Inst(getattr(real, op)(*a, **k) if real is not None else None)

        return call


def build_nc(taps=()):
    taps = tuple(t for t in taps)
    nc = bass.Bass("TRN2", target_bir_lowering=False)

    x_h = nc.dram_tensor("x", [S, D], F32, kind="ExternalInput")
    wqT = nc.dram_tensor("wqT", [D, D], F32, kind="ExternalInput")
    wkT = nc.dram_tensor("wkT", [D, D], F32, kind="ExternalInput")
    wvT = nc.dram_tensor("wvT", [D, D], F32, kind="ExternalInput")
    woT = nc.dram_tensor("woT", [D, D], F32, kind="ExternalInput")
    w1 = nc.dram_tensor("w1", [D, D], F32, kind="ExternalInput")
    w2 = nc.dram_tensor("w2", [D, 512], F32, kind="ExternalInput")
    w3 = nc.dram_tensor("w3", [512, 256], F32, kind="ExternalInput")
    gw1 = nc.dram_tensor("gw1", [D, 512], F32, kind="ExternalInput")
    gw2 = nc.dram_tensor("gw2", [512, 256], F32, kind="ExternalInput")
    ident_s = nc.dram_tensor("ident", [128, 128], F32, kind="ExternalInput")
    cnb1_s = nc.dram_tensor("cnb1", [128, 8], F32, kind="ExternalInput")
    cnb2_s = nc.dram_tensor("cnb2", [128, 4], F32, kind="ExternalInput")
    cnb3_s = nc.dram_tensor("cnb3", [128, 2], F32, kind="ExternalInput")
    gb1_s = nc.dram_tensor("gb1", [128, 4], F32, kind="ExternalInput")
    gb2_s = nc.dram_tensor("gb2", [128, 2], F32, kind="ExternalInput")
    sc3_s = nc.dram_tensor("sc3", [1, 3], F32, kind="ExternalInput")
    w4gc_s = nc.dram_tensor("w4gc", [128, 4], F32, kind="ExternalInput")
    out_h = nc.dram_tensor("out", [S, D], F32, kind="ExternalOutput")
    tap_shapes = {"xnraw": [128, 16384], "attn": [128, 16384], "acc": [128, 16384],
                  "gcin": [128, 8], "gf": [1, 8], "stream": [128, 16384]}
    tap_h = {t: nc.dram_tensor("tap_" + t, tap_shapes[t], F32, kind="ExternalOutput").ap()
             for t in taps}

    with ExitStack() as ctx:
        sbt = lambda name, shape, dt=F32: ctx.enter_context(nc.sbuf_tensor(name, shape, dt))
        A = sbt("slotA", [128, 16384])
        B = sbt("slotB", [128, 16384])
        C = sbt("slotC", [128, 16384])
        A16 = A.bitcast(F16)
        B16 = B.bitcast(F16)
        C16 = C.bitcast(F16)
        wblk0 = sbt("wblk0", [128, 1024])
        wblk1 = sbt("wblk1", [128, 1024])
        ident = sbt("identb", [128, 128])
        cnb1 = sbt("cnb1b", [128, 8])
        cnb2 = sbt("cnb2b", [128, 4])
        cnb3 = sbt("cnb3b", [128, 2])
        gb1 = sbt("gb1b", [128, 4])
        gb2 = sbt("gb2b", [128, 2])
        sc3 = sbt("sc3b", [1, 3])
        w4gc = sbt("w4gcb", [128, 4])
        ones64 = sbt("ones64", [1, 64])
        ones128 = sbt("ones128", [1, 128])
        gcp = sbt("gcp", [128, 128])
        gcin = sbt("gcin", [128, 8])
        gch1 = sbt("gch1", [128, 4])
        gch2 = sbt("gch2", [128, 2])
        gfb = sbt("gfb", [1, 8])
        tf_sb = sbt("tf_sb", [1, FQT])
        cp_sb = sbt("cp_sb", [1, FQT])
        g3b = sbt("g3b", [128, 512])
        psum = [ctx.enter_context(nc.psum_tensor(f"p{i}", [128, 512], F32))
                for i in range(8)]
        sems = {nm: Sem(ctx.enter_context(nc.semaphore(nm)))
                for nm in ["sc_", "sx", "swb0", "swb1", "sf", "sxr0", "sxr1",
                           "st0", "st1", "sT", "sV", "sS", "sG"]}
        tap_sems = {t: Sem(ctx.enter_context(nc.semaphore("stp_" + t)))
                    for t in taps}
        block = ctx.enter_context(nc.Block())

        xn_hi = A16[:, 0:16384]
        xn_lo = A16[:, 16384:32768]
        w1hi = A16[:, 0:8192]
        w1lo = A16[:, 8192:16384]
        w2res = A[:, 8192:12288]
        s16h = A16[:, 24576:26624]
        s16l = A16[:, 26624:28672]
        g1 = A[:, 14336:16384]
        xre = [A[:, 0:1024], A[:, 1024:2048]]
        outt = [A[:, 2048:3072], A[:, 3072:4096]]
        ln_dump = C[:, 0:1024]
        ln_xnt = [C[:, 1024:2048], C[:, 2048:3072]]
        st_ = C[:, 3072:3248]
        ln_s1, ln_s2, ln_mu, ln_var, ln_m2, ln_std, ln_r0, ln_t, ln_rstd, ln_nmr = (
            st_[:, i * 16:(i + 1) * 16] for i in range(10))
        k_stack = C16[:, 0:4096]
        q_stack = C16[:, 4096:8192]
        qrep_hi2 = [C16[:, 8192:8704], C16[:, 29760:30272]]
        qrep_lo2 = [C16[:, 8704:9216], C16[:, 30272:30784]]
        wq16 = (C16[:, 9216:10240], C16[:, 10240:11264])
        wk16 = (C16[:, 11264:12288], C16[:, 12288:13312])
        wv16 = (C16[:, 13312:14336], C16[:, 14336:15360])
        vaugP = C[:, 7680:9760]
        vfm = [C[:, 9760:10272], C[:, 10272:10784]]
        sig = [C[:, 10784:11296], C[:, 11296:11808], C[:, 15392:15904]]
        n_sb2 = [C[:, 11808:12320], C[:, 12832:13344]]
        recip = C[:, 12320:12832]

        def emit(which, eng):
            T = Eng(eng if which == "tensor" else None)
            V = Eng(eng if which == "vector" else None)
            Sc = Eng(eng if which == "scalar" else None)
            G = Eng(eng if which == "gpsimd" else None)
            Y = Eng(eng if which == "sync" else None)
            sc_, sx, sf = sems["sc_"], sems["sx"], sems["sf"]
            swb = [sems["swb0"], sems["swb1"]]
            sxr = [sems["sxr0"], sems["sxr1"]]
            stp = [sems["st0"], sems["st1"]]
            sT, sV, sS = sems["sT"], sems["sV"], sems["sS"]
            sG = sems["sG"]
            for s_ in list(sems.values()) + list(tap_sems.values()):
                s_.n = 0
            bank_free = {i: [] for i in range(8)}

            def bacq(i):
                T.waits(bank_free[i])
                return psum[i]

            Y.dma_start(B.ap().rearrange("p (t n) -> p t n", t=16),
                        bass.AP(x_h, 0, [[1024, 128], [128 * 1024, 16], [1, 1024]])
                        ).inc(sx, 16)
            for dst, src in [(ident, ident_s), (cnb1, cnb1_s), (cnb2, cnb2_s),
                             (cnb3, cnb3_s), (gb1, gb1_s), (gb2, gb2_s),
                             (sc3, sc3_s), (w4gc, w4gc_s)]:
                Y.dma_start(dst[:, :], src.ap()).inc(sc_, 16)
            consts_n = sc_.n
            V.memset(gfb[:, :], 0.0).inc(sV)
            V.memset(ones64[:, :], 1.0).inc(sV)
            V.memset(ones128[:, :], 1.0).inc(sV)
            V.memset(vaugP[:, :], 1.0).inc(sV)

            V.wait(sx, 16)
            Sc.wait(sx, 16)
            for t in range(NTT):
                xt = B[:, t * 1024:(t + 1) * 1024]
                V.reduce_sum(ln_s1[:, t:t + 1], xt, axis=AX.X).inc(sV)
                if t > 0:
                    Sc.wait(sS)
                Sc.activation(ln_dump[:, :], xt, AF.Square,
                              accum_out=ln_s2[:, t:t + 1]).inc(sS)
            V.wait(sS)
            V.wait(sV)
            V.tensor_scalar(ln_mu[:, :], ln_s1[:, :], 1.0 / D, None, ALU.mult).inc(sV)
            V.tensor_scalar(ln_var[:, :], ln_s2[:, :], 1.0 / D, None, ALU.mult).inc(sV)
            V.wait(sV)
            V.tensor_tensor(ln_m2[:, :], ln_mu[:, :], ln_mu[:, :], ALU.mult).inc(sV)
            V.wait(sV)
            V.tensor_tensor(ln_var[:, :], ln_var[:, :], ln_m2[:, :], ALU.subtract).inc(sV)
            V.wait(sV)
            V.tensor_scalar(ln_var[:, :], ln_var[:, :], LN_EPS, None, ALU.add).inc(sV)
            Sc.wait(sV)
            Sc.activation(ln_std[:, :], ln_var[:, :], AF.Sqrt).inc(sS)
            V.wait(sS)
            V.reciprocal(ln_r0[:, :], ln_std[:, :]).inc(sV)
            V.wait(sV)
            V.tensor_tensor(ln_t[:, :], ln_r0[:, :], ln_r0[:, :], ALU.mult).inc(sV)
            V.wait(sV)
            V.tensor_tensor(ln_t[:, :], ln_t[:, :], ln_var[:, :], ALU.mult).inc(sV)
            V.wait(sV)
            V.tensor_scalar(ln_t[:, :], ln_t[:, :], -0.5, 1.5, ALU.mult, ALU.add).inc(sV)
            V.wait(sV)
            V.tensor_tensor(ln_rstd[:, :], ln_r0[:, :], ln_t[:, :], ALU.mult).inc(sV)
            V.wait(sV)
            V.scalar_tensor_tensor(ln_nmr[:, :], ln_mu[:, :], -1.0, ln_rstd[:, :],
                                   ALU.mult, ALU.mult).inc(sV)
            ln_done = sV.n

            xnt_free = [[], []]
            for t in range(NTT):
                xt = B[:, t * 1024:(t + 1) * 1024]
                xnt = ln_xnt[t % 2]
                Sc.wait(sV, ln_done)
                Sc.waits(xnt_free[t % 2])
                Sc.activation(xnt[:, :], xt, AF.Identity,
                              bias=ln_nmr[:, t:t + 1], scale=ln_rstd[:, t:t + 1]).inc(sS)
                xnt_ready = sS.n
                for c in range(NCH):
                    bi = 6 + (t * NCH + c) % 2
                    b = bacq(bi)
                    T.wait(sS, xnt_ready)
                    T.wait(sc_, consts_n)
                    T.transpose(b[:, 0:128], xnt[:, c * 128:(c + 1) * 128],
                                ident[:, :]).inc(sT)
                    hi_sl = xn_hi[:, c * S + t * 128: c * S + (t + 1) * 128]
                    lo_sl = xn_lo[:, c * S + t * 128: c * S + (t + 1) * 128]
                    Sc.wait(sT)
                    Sc.copy(hi_sl, b[:, 0:128]).inc(sS)
                    V.wait(sS)
                    V.tensor_tensor(lo_sl, b[:, 0:128], hi_sl, ALU.subtract).inc(sV)
                    V.reduce_sum(gcp[:, c * NTT + t: c * NTT + t + 1], b[:, 0:128],
                                 axis=AX.X).inc(sV)
                    bank_free[bi] = [(sV, sV.n)]
                xnt_free[t % 2] = [(sT, sT.n)]
            V.wait(sV)
            V.reduce_sum(gcin[:, 0:NCH],
                         gcp.ap().rearrange("p (c t) -> p c t", c=NCH),
                         axis=AX.X).inc(sV)
            V.wait(sV)
            V.tensor_scalar(gcin[:, 0:NCH], gcin[:, 0:NCH], 1.0 / S, None,
                            ALU.mult).inc(sV)
            gcin_done = sV.n
            tap_marks = {}
            if "xnraw" in taps:
                Y.wait(sV, gcin_done)
                Y.dma_start(tap_h["xnraw"], A[:, :]).inc(tap_sems["xnraw"], 16)
                tap_marks["xnraw"] = (tap_sems["xnraw"], 16)

            wblks = [wblk0, wblk1]
            wblk_free = [[], []]
            nblk = [0]

            def load_block(src_handle, col, nk, row_len):
                i = nblk[0] % 2
                buf = wblks[i]
                Y.waits(wblk_free[i])
                ap = bass.AP(src_handle, col * 128,
                             [[row_len, 128], [128 * row_len, nk], [1, 128]])
                Y.dma_start(buf.ap()[:, 0:nk * 128].rearrange("p (kc o) -> p kc o", kc=nk),
                            ap).inc(swb[i], 16)
                nblk[0] += 1
                return buf, i, (swb[i], swb[i].n)

            def split16(buf, i, wready, hi, lo, width=1024):
                V.waits([wready])
                V.tensor_copy(hi[:, 0:width], buf[:, 0:width]).inc(sV)
                V.wait(sV)
                V.tensor_tensor(lo[:, 0:width], buf[:, 0:width], hi[:, 0:width],
                                ALU.subtract).inc(sV)
                wblk_free[i] = [(sV, sV.n)]
                return sV.n

            for oc in range(4):
                buf, ib, wready = load_block(gw1, oc, 8, 512)
                b = bacq(7)
                T.waits([wready])
                T.wait(sV, gcin_done)
                for kc in range(NCH):
                    inst = T.matmul(b[:, 0:1], buf[:, kc * 128:(kc + 1) * 128],
                                    gcin[:, kc:kc + 1], start=(kc == 0),
                                    stop=(kc == NCH - 1))
                    if kc == NCH - 1:
                        inst.inc(sT)
                Sc.wait(sT)
                Sc.activation(gch1[:, oc:oc + 1], b[:, 0:1], AF.Gelu,
                              bias=gb1[:, oc:oc + 1]).inc(sS)
                bank_free[7] = [(sS, sS.n)]
                wblk_free[ib] = [(sT, sT.n)]
            for oc in range(2):
                buf, ib, wready = load_block(gw2, oc, 4, 256)
                b = bacq(7)
                T.waits([wready])
                T.wait(sS)
                for kc in range(4):
                    inst = T.matmul(b[:, 0:1], buf[:, kc * 128:(kc + 1) * 128],
                                    gch1[:, kc:kc + 1], start=(kc == 0), stop=(kc == 3))
                    if kc == 3:
                        inst.inc(sT)
                Sc.wait(sT)
                Sc.activation(gch2[:, oc:oc + 1], b[:, 0:1], AF.Gelu,
                              bias=gb2[:, oc:oc + 1]).inc(sS)
                bank_free[7] = [(sS, sS.n)]
                wblk_free[ib] = [(sT, sT.n)]
            b = bacq(7)
            T.wait(sS)
            T.wait(sc_, consts_n)
            for kc in range(2):
                inst = T.matmul(b[0:1, 0:1], w4gc[:, 2 + kc:3 + kc],
                                gch2[:, kc:kc + 1], start=(kc == 0), stop=(kc == 1))
                if kc == 1:
                    inst.inc(sT)
            Sc.wait(sT)
            Sc.activation(gfb[0:1, 0:1], b[0:1, 0:1], AF.Sigmoid,
                          bias=sc3[0:1, 1:2]).inc(sS)
            bank_free[7] = [(sS, sS.n)]
            V.wait(sS)
            V.tensor_scalar(gfb[0:1, 1:2], sc3[0:1, 2:3], 0.6, None, ALU.mult).inc(sV)
            V.tensor_scalar(gfb[0:1, 3:4], sc3[0:1, 2:3], 1.4, None, ALU.mult).inc(sV)
            V.tensor_scalar(gfb[0:1, 4:5], sc3[0:1, 2:3], -1.0, 0.1, ALU.mult,
                            ALU.add).inc(sV)
            V.wait(sV)
            V.tensor_tensor(gfb[0:1, 5:6], gfb[0:1, 0:1], gfb[0:1, 3:4], ALU.mult).inc(sV)
            V.wait(sV)
            V.tensor_tensor(gfb[0:1, 2:3], gfb[0:1, 4:5], gfb[0:1, 5:6], ALU.add).inc(sV)
            kconst_done = sV.n
            if "gcin" in taps:
                Y.wait(sV, gcin_done)
                Y.dma_start(tap_h["gcin"], gcin[:, :]).inc(tap_sems["gcin"], 16)
            if "gf" in taps:
                Y.wait(sV, kconst_done)
                Y.dma_start(tap_h["gf"], gfb[:, :]).inc(tap_sems["gf"], 16)

            state = {"g": 0, "pending": None, "qrep_free": [[], []],
                     "qrep_done": [0, 0], "qrep_next": None, "carry": None,
                     "tail": None, "sig_free": [[], [], []]}
            def load_split_pair(pair, guard_mark):
                bq, iq, rq = load_block(wqT, pair, 8, D)
                V.wait(sT, guard_mark)
                sqm = split16(bq, iq, rq, *wq16)
                bk, ik, rk = load_block(wkT, pair, 8, D)
                skm = split16(bk, ik, rk, *wk16)
                bv, iv, rv = load_block(wvT, pair, 8, D)
                svm = split16(bv, iv, rv, *wv16)
                return [sqm, skm, svm]

            state["splits"] = load_split_pair(0, 0)
            for pair in range(NCH):
                splits = state["splits"]
                w16 = [wq16, wk16, wv16]

                for proj in range(3):
                    whi, wlo = w16[proj]
                    for qt in range(NQT):
                        bi = [4, 5, 0, 1][(proj * NQT + qt) % 4]
                        b = bacq(bi)
                        T.wait(sV, splits[proj])
                        passes = []
                        for kc in range(NCH):
                            xh = xn_hi[:, kc * S + qt * QT: kc * S + qt * QT + QT]
                            xl = xn_lo[:, kc * S + qt * QT: kc * S + qt * QT + QT]
                            lh = whi[:, kc * 128:(kc + 1) * 128]
                            ll = wlo[:, kc * 128:(kc + 1) * 128]
                            passes += [(lh, xh), (lh, xl), (ll, xh)]
                        n = len(passes)
                        for ip, (lT, rhs) in enumerate(passes):
                            inst = T.matmul(b[:, :], lT, rhs, start=(ip == 0),
                                            stop=(ip == n - 1))
                            if ip == n - 1:
                                inst.inc(sT)
                        mmdone = sT.n
                        if proj < 2:
                            stk = q_stack if proj == 0 else k_stack
                            dA = stk[:, qt * QT: qt * QT + QT]
                            dB = stk[:, S + qt * QT: S + qt * QT + QT]
                            Sc.wait(sT, mmdone)
                            Sc.copy(dA[0:64, :], b[0:64, :]).inc(sS)
                            Sc.copy(dB[0:64, :], b[64:128, :]).inc(sS)
                            V.wait(sS)
                            V.tensor_tensor(dA[64:128, :], b[0:64, :], dA[0:64, :],
                                            ALU.subtract).inc(sV)
                            V.tensor_tensor(dB[64:128, :], b[64:128, :], dB[0:64, :],
                                            ALU.subtract).inc(sV)
                            bank_free[bi] = [(sV, sV.n)]
                        else:
                            vf = vfm[qt % 2]
                            V.wait(sT, mmdone)
                            V.tensor_copy(vf[:, :], b[:, :]).inc(sV)
                            bank_free[bi] = [(sV, sV.n)]
                            vf_ready = (sV, sV.n)
                            for j in range(4):
                                tau = qt * 4 + j
                                ti_ = 6 + tau % 2
                                tb = bacq(ti_)
                                T.waits([vf_ready])
                                T.transpose(tb[:, 0:128], vf[:, j * 128:(j + 1) * 128],
                                            ident[:, :]).inc(sT)
                                Sc.wait(sT)
                                Sc.copy(bass.AP(vaugP.tensor, vaugP.offset + tau * 130,
                                                [[16384, 128], [65, 2], [1, 64]]),
                                        tb[:, 0:128].rearrange("p (h d) -> p h d", h=2)
                                        ).inc(sS)
                                bank_free[ti_] = [(sS, sS.n)]
                att_S = sS.n
                state["proj_done_mark"] = sT.n

                def mk_qrep(hi_, qc, g):
                    par = g % 2

                    def build():
                        V.wait(sS, att_S)
                        V.wait(sV)
                        V.waits(state["qrep_free"][par])
                        so = hi_ * S
                        qh = q_stack[0:64, so + qc * QT: so + qc * QT + QT]
                        ql = q_stack[64:128, so + qc * QT: so + qc * QT + QT]
                        V.tensor_copy(qrep_hi2[par][0:64, :], qh).inc(sV)
                        V.tensor_copy(qrep_hi2[par][64:128, :], qh).inc(sV)
                        V.tensor_copy(qrep_lo2[par][0:64, :], ql).inc(sV)
                        V.tensor_copy(qrep_lo2[par][64:128, :], ql).inc(sV)
                        state["qrep_done"][par] = sV.n
                    return build

                for hi_ in range(2):
                    stk_off = hi_ * S
                    for qc in range(NQT):
                        g = state["g"]
                        state["g"] += 1
                        par = g % 2
                        qrep_hi, qrep_lo = qrep_hi2[par], qrep_lo2[par]
                        n_sb = n_sb2[par]
                        wvb = [2, 3, 6][g % 3]
                        if state["qrep_next"] is None:
                            mk_qrep(hi_, qc, g)()
                        if (hi_, qc) == (0, 1) and pair + 1 < NCH:
                            state["splits"] = load_split_pair(
                                pair + 1, state["proj_done_mark"])
                        if qc < NQT - 1:
                            state["qrep_next"] = mk_qrep(hi_, qc + 1, g + 1)
                        elif hi_ == 0:
                            state["qrep_next"] = mk_qrep(1, 0, g + 1)
                        else:
                            state["qrep_next"] = None
                        qrep_done = state["qrep_done"][par]
                        b2 = bacq(wvb)
                        sig_marks = {}
                        wv_marks = {}

                        def emit_scores(kt, qrep_done=qrep_done):
                            bsc = bacq(kt % 2)
                            lT = k_stack[:, stk_off + kt * 128: stk_off + (kt + 1) * 128]
                            T.wait(sV, qrep_done)
                            T.matmul(bsc[:, :], lT, qrep_hi[:, :], start=True, stop=False)
                            T.matmul(bsc[:, :], lT, qrep_lo[:, :], start=False,
                                     stop=True).inc(sT)
                            return sT.n

                        def emit_sig(kt, sc_mark, g=g):
                            bufi = (g * NKT + kt) % 3
                            sgb = sig[bufi]
                            Sc.wait(sT, sc_mark)
                            Sc.waits(state["sig_free"][bufi])
                            Sc.activation(sgb[:, :], psum[kt % 2][:, :], AF.Sigmoid,
                                          scale=0.125).inc(sS)
                            bank_free[kt % 2] = [(sS, sS.n)]
                            sig_marks[kt] = sS.n

                        def emit_wv(kt, g=g, b2=b2, hi_=hi_, sig_marks=sig_marks,
                                    wv_marks=wv_marks):
                            bufi = (g * NKT + kt) % 3
                            T.wait(sS, sig_marks[kt])
                            lhs = bass.AP(vaugP.tensor,
                                          vaugP.offset + kt * 130 + hi_ * 65,
                                          [[16384, 128], [1, 65]])
                            T.matmul(b2[0:65, :], lhs,
                                     sig[bufi][:, :], start=(kt == 0),
                                     stop=(kt == NKT - 1)).inc(sT)
                            wv_marks[kt] = sT.n
                            state["sig_free"][bufi] = [(sT, sT.n)]

                        def mk_tail(g=g, b2=b2, n_sb=n_sb, wvb=wvb, hi_=hi_, qc=qc,
                                    pair=pair, wv_marks=wv_marks):
                            dst = B[hi_ * 64: hi_ * 64 + 64,
                                    pair * S + qc * QT: pair * S + qc * QT + QT]

                            def tail_norm():
                                V.wait(sT, wv_marks[NKT - 1])
                                V.tensor_scalar(n_sb[0:1, 0:512], b2[64:65, :],
                                                ATT_EPS, None, ALU.add).inc(sV)
                                nx = sV.n

                                def pending():
                                    b3 = bacq(4)
                                    T.wait(sV, nx)
                                    T.matmul(b3[0:64, :], ones64[:, :],
                                             n_sb[0:1, 0:512], start=True,
                                             stop=True).inc(sT)
                                    V.wait(sT)
                                    V.wait(sV)
                                    V.reciprocal(recip[0:64, 0:512],
                                                 b3[0:64, :]).inc(sV)
                                    bank_free[4] = [(sV, sV.n)]
                                    V.wait(sV)
                                    V.tensor_tensor(dst, b2[0:64, :],
                                                    recip[0:64, 0:512],
                                                    ALU.mult).inc(sV)
                                    bank_free[wvb] = [(sV, sV.n)]
                                state["pending"] = pending
                            return tail_norm

                        carry = state["carry"]
                        for kt in range(NKT):
                            sc_mark = emit_scores(kt)
                            if kt <= 1 and carry is not None:
                                carry(kt)
                            emit_sig(kt, sc_mark)
                            if kt == 1 and state["tail"] is not None:
                                state["tail"]()
                                state["tail"] = None
                            if kt >= 2:
                                emit_wv(kt - 2)
                            if kt == 4 and state["qrep_next"] is not None:
                                state["qrep_next"]()
                            if kt == 6 and state["pending"] is not None:
                                state["pending"]()
                                state["pending"] = None
                        state["qrep_free"][par] = [(sT, sT.n)]
                        if (hi_, qc) == (1, NQT - 1):
                            emit_wv(NKT - 2)
                            emit_wv(NKT - 1)
                            if state["pending"] is not None:
                                state["pending"]()
                                state["pending"] = None
                            mk_tail()()
                            state["pending"]()
                            state["pending"] = None
                            state["carry"] = None
                            state["tail"] = None
                        else:
                            def mk_carry(emit_wv=emit_wv):
                                def carry_fn(k):
                                    emit_wv(NKT - 2 + k)
                                return carry_fn
                            state["carry"] = mk_carry()
                            state["tail"] = mk_tail()
            if state["pending"] is not None:
                state["pending"]()
                state["pending"] = None
            attn_V = sV.n
            if "attn" in taps:
                Y.wait(sV, attn_V)
                Y.dma_start(tap_h["attn"], B[:, :]).inc(tap_sems["attn"], 16)
                tap_marks["attn"] = (tap_sems["attn"], 16)

            for oc in range(NCH):
                buf, ib, wready = load_block(woT, oc, 8, D)
                for qc in range(NQT):
                    bi = 4 + qc % 2
                    b = bacq(bi)
                    T.waits([wready])
                    T.wait(sV, attn_V)
                    for kc in range(NCH):
                        rhs = B[:, kc * S + qc * QT: kc * S + qc * QT + QT]
                        inst = T.matmul(b[:, :], buf[:, kc * 128:(kc + 1) * 128], rhs,
                                        start=(kc == 0), stop=(kc == NCH - 1))
                        if kc == NCH - 1:
                            inst.inc(sT)
                    Sc.wait(sT)
                    Sc.copy(C[:, oc * S + qc * QT: oc * S + qc * QT + QT],
                            b[:, :]).inc(sS)
                    bank_free[bi] = [(sS, sS.n)]
                wblk_free[ib] = [(sT, sT.n)]
            wo_T, wo_S = sT.n, sS.n
            if "acc" in taps:
                Y.wait(sS, wo_S)
                Y.dma_start(tap_h["acc"], C[:, :]).inc(tap_sems["acc"], 16)
                tap_marks["acc"] = (tap_sems["acc"], 16)

            V.wait(sT, wo_T)
            if "attn" in tap_marks:
                V.waits([tap_marks["attn"]])
            Bv = B.ap().rearrange("p (c t) -> p c t", c=NCH)
            xhv = xn_hi.rearrange("p (c t) -> p c t", c=NCH)
            xlv = xn_lo.rearrange("p (c t) -> p c t", c=NCH)
            V.tensor_tensor(Bv[:, :, 0:FQT], xhv[:, :, 0:FQT], xlv[:, :, 0:FQT],
                            ALU.add).inc(sV)
            sinit0 = (sV, sV.n)
            G.wait(sT, wo_T)
            if "attn" in tap_marks:
                G.waits([tap_marks["attn"]])
            sinit = []
            for c in range(4):
                E_ = V if c < 2 else G
                sm = sV if c < 2 else sG
                for cc in (c, c + 4):
                    E_.tensor_tensor(B[:, cc * S + FQT:(cc + 1) * S],
                                     xn_hi[:, cc * S + FQT:(cc + 1) * S],
                                     xn_lo[:, cc * S + FQT:(cc + 1) * S],
                                     ALU.add).inc(sm)
                sinit.append((sm, sm.n))
            sinit_g = sG.n
            stream_ready = sV.n
            if "xnraw" in tap_marks:
                V.waits([tap_marks["xnraw"]])
            w1_marks = []
            for oc in range(NCH):
                buf, ib, wready = load_block(w1, oc, 8, D)
                E_ = V if oc < 4 else G
                sm = sV if oc < 4 else sG
                E_.waits([sinit0, sinit[oc // 2]])
                E_.wait(sG, sinit_g)
                E_.waits([wready])
                hi = w1hi.rearrange("p (oc w) -> p oc w", oc=NCH)[:, oc, :]
                lo = w1lo.rearrange("p (oc w) -> p oc w", oc=NCH)[:, oc, :]
                E_.tensor_copy(hi, buf[:, 0:1024]).inc(sm)
                E_.wait(sm)
                E_.tensor_tensor(lo, buf[:, 0:1024], hi, ALU.subtract).inc(sm)
                wblk_free[ib] = [(sm, sm.n)]
                w1_marks.append((sm, sm.n))
            for oc in range(4):
                Y.waits([sinit0, sinit[oc]])
                ap = bass.AP(w2, oc * 128, [[512, 128], [128 * 512, 8], [1, 128]])
                Y.dma_start(w2res.rearrange("p (oc w) -> p oc w", oc=4)[:, oc, :]
                            .rearrange("p (kc o) -> p kc o", kc=8), ap).inc(sf, 16)
            for oc in range(2):
                Y.waits(wblk_free[0])
                Y.waits(wblk_free[1])
                ap = bass.AP(w3, oc * 128, [[256, 128], [128 * 256, 4], [1, 128]])
                Y.dma_start(wblk0.ap()[:, oc * 512:(oc + 1) * 512]
                            .rearrange("p (kc o) -> p kc o", kc=4), ap).inc(sf, 16)
            ffn_w_ready = sf.n
            w1_ready = sV.n

            strv = Bv
            Cv = C.ap().rearrange("p (c t) -> p c t", c=NCH)
            acc_marks = [0] * NTT

            def emit_s16(cy, ti, first=False):
                t0 = ti * FQT
                if first:
                    V.waits([sinit0])
                else:
                    V.wait(sT, state["cn1_done"])
                    V.wait(sV)
                    V.wait(sG, sinit_g)
                V.tensor_copy(s16h.rearrange("p (c t) -> p c t", c=NCH),
                              strv[:, :, t0:t0 + FQT]).inc(sV)
                V.wait(sV)
                V.tensor_tensor(s16l.rearrange("p (c t) -> p c t", c=NCH),
                                strv[:, :, t0:t0 + FQT],
                                s16h.rearrange("p (c t) -> p c t", c=NCH),
                                ALU.subtract).inc(sV)
                state["s16_ready"] = sV.n

            def emit_cn1(cy, ti):
                for oc in range(NCH):
                    bi = 4 + oc % 2
                    b = psum[bi]
                    T.waits(bank_free[bi])
                    T.waits([w1_marks[oc]])
                    T.wait(sV, state["s16_ready"])
                    passes = []
                    for kc in range(NCH):
                        lh = w1hi[:, oc * 1024 + kc * 128: oc * 1024 + (kc + 1) * 128]
                        ll = w1lo[:, oc * 1024 + kc * 128: oc * 1024 + (kc + 1) * 128]
                        rh = s16h[:, kc * FQT:(kc + 1) * FQT]
                        rl = s16l[:, kc * FQT:(kc + 1) * FQT]
                        passes += [(lh, rh), (lh, rl), (ll, rh)]
                    n = len(passes)
                    for ip, (lT, rhs) in enumerate(passes):
                        inst = T.matmul(b[:, 0:FQT], lT, rhs, start=(ip == 0),
                                        stop=(ip == n - 1))
                        if ip == n - 1:
                            inst.inc(sT)
                    Sc.wait(sT)
                    Sc.wait(sc_, consts_n)
                    Sc.wait(sG, sinit_g)
                    Sc.wait(sV, stream_ready)
                    Sc.activation(g1[:, oc * FQT:(oc + 1) * FQT], b[:, 0:FQT],
                                  AF.Gelu, bias=cnb1[:, oc:oc + 1]).inc(sS)
                    bank_free[bi] = [(sS, sS.n)]
                state["g1_ready"] = sS.n
                state["cn1_done"] = sT.n

            state["g2_guard"] = list(wblk_free[1])
            emit_s16(0, 0, first=True)
            emit_cn1(0, 0)
            for cy in range(3):
                for ti in range(NFQT):
                    t0 = ti * FQT
                    g1_ready = state["g1_ready"]
                    if state.get("ffn_fin") is not None:
                        state["ffn_fin"]()
                        state["ffn_fin"] = None

                    for oc in range(4):
                        bi = 4 + oc % 2
                        b = bacq(bi)
                        T.wait(sf, ffn_w_ready)
                        T.wait(sS, g1_ready)
                        for kc in range(NCH):
                            inst = T.matmul(b[:, 0:FQT],
                                            w2res[:, oc * 1024 + kc * 128:
                                                  oc * 1024 + (kc + 1) * 128],
                                            g1[:, kc * FQT:(kc + 1) * FQT],
                                            start=(kc == 0), stop=(kc == NCH - 1))
                            if kc == NCH - 1:
                                inst.inc(sT)
                        Sc.wait(sT)
                        Sc.waits(state.get("g2_guard", []))
                        Sc.activation(wblk1[:, oc * FQT:(oc + 1) * FQT], b[:, 0:FQT],
                                      AF.Gelu, bias=cnb2[:, oc:oc + 1]).inc(sS)
                        bank_free[bi] = [(sS, sS.n)]
                    g2_ready = sS.n

                    for oc in range(2):
                        bi = 4 + oc % 2
                        b = bacq(bi)
                        T.wait(sS, g2_ready)
                        for kc in range(4):
                            inst = T.matmul(b[:, 0:FQT],
                                            wblk0[:, oc * 512 + kc * 128:
                                                  oc * 512 + (kc + 1) * 128],
                                            wblk1[:, kc * FQT:(kc + 1) * FQT],
                                            start=(kc == 0), stop=(kc == 3))
                            if kc == 3:
                                inst.inc(sT)
                        Sc.wait(sT)
                        Sc.activation(g3b[:, oc * FQT:(oc + 1) * FQT], b[:, 0:FQT],
                                      AF.Gelu, bias=cnb3[:, oc:oc + 1]).inc(sS)
                        bank_free[bi] = [(sS, sS.n)]
                    g3_ready = sS.n

                    nxt = (cy, ti + 1) if ti + 1 < NFQT else                           ((cy + 1, 0) if cy < 2 else None)
                    if nxt is not None:
                        emit_s16(*nxt)
                        emit_cn1(*nxt)

                    b7 = bacq(7)
                    T.wait(sS, g3_ready)
                    T.wait(sc_, consts_n)
                    for kc in range(2):
                        inst = T.matmul(b7[0:1, 0:FQT], w4gc[:, kc:kc + 1],
                                        g3b[:, kc * FQT:(kc + 1) * FQT],
                                        start=(kc == 0), stop=(kc == 1))
                        if kc == 1:
                            inst.inc(sT)
                    Sc.wait(sT)
                    Sc.wait(sV, kconst_done)
                    Sc.wait(sV)
                    Sc.activation(tf_sb[0:1, 0:FQT], b7[0:1, 0:FQT], AF.Sigmoid,
                                  bias=sc3[0:1, 0:1]).inc(sS)
                    bank_free[7] = [(sS, sS.n)]
                    V.wait(sS)
                    V.wait(sT)
                    V.tensor_scalar(cp_sb[0:1, 0:FQT], tf_sb[0:1, 0:FQT],
                                    gfb[0:1, 1:2], gfb[0:1, 2:3],
                                    ALU.mult, ALU.add).inc(sV)
                    cp_ready = sV.n

                    def mk_fin(cy=cy, ti=ti, t0=t0, cp_ready=cp_ready):
                        def fin():
                            b6 = bacq(6)
                            T.wait(sV, cp_ready)
                            T.matmul(b6[:, 0:FQT], ones128[:, :], cp_sb[0:1, 0:FQT],
                                     start=True, stop=True).inc(sT)
                            cp_bcast = sT.n
                            for (a, bb, sa, sb_) in [(0, 4, 1.0, -1.0),
                                                     (1, 5, 1.0, -1.0),
                                                     (2, 6, -1.0, 1.0),
                                                     (3, 7, -1.0, 1.0)]:
                                olda = strv[:, a, t0:t0 + FQT]
                                oldb = strv[:, bb, t0:t0 + FQT]
                                V.waits(bank_free[0])
                                V.waits(bank_free[1])
                                V.wait(sT, cp_bcast)
                                V.scalar_tensor_tensor(psum[0][:, 0:FQT], oldb, sa,
                                                       b6[:, 0:FQT], ALU.mult,
                                                       ALU.mult).inc(sV)
                                V.scalar_tensor_tensor(psum[1][:, 0:FQT], olda, sb_,
                                                       b6[:, 0:FQT], ALU.mult,
                                                       ALU.mult).inc(sV)
                                V.wait(sV)
                                V.tensor_tensor(olda, olda, psum[0][:, 0:FQT],
                                                ALU.add).inc(sV)
                                V.tensor_tensor(oldb, oldb, psum[1][:, 0:FQT],
                                                ALU.add).inc(sV)
                                bank_free[0] = [(sV, sV.n)]
                                bank_free[1] = [(sV, sV.n)]
                            bank_free[6] = [(sV, sV.n)]
                            if cy == 2:
                                V.wait(sS, wo_S)
                                if "acc" in tap_marks:
                                    V.waits([tap_marks["acc"]])
                                for half in range(2):
                                    t = 2 * ti + half
                                    V.wait(sV)
                                    V.tensor_tensor(Cv[:, :, t * 128:(t + 1) * 128],
                                                    Cv[:, :, t * 128:(t + 1) * 128],
                                                    Bv[:, :, t * 128:(t + 1) * 128],
                                                    ALU.add).inc(sV)
                                    acc_marks[t] = sV.n
                        return fin

                    state["ffn_fin"] = mk_fin()
            if state.get("ffn_fin") is not None:
                state["ffn_fin"]()
                state["ffn_fin"] = None
            ffn_done = sV.n
            if "stream" in taps:
                Y.wait(sV, ffn_done)
                Y.dma_start(tap_h["stream"], B[:, :]).inc(tap_sems["stream"], 16)

            xre_free = [[], []]
            outt_free = [[], []]
            xre_marks = {}

            def emit_xre(t):
                Sc.waits(xre_free[t % 2])
                Sc.wait(sV, ffn_done)
                Sc.dma_start(xre[t % 2][:, :],
                             bass.AP(x_h, t * 128 * 1024, [[1024, 128], [1, 1024]])
                             ).inc(sxr[t % 2], 16)
                xre_marks[t] = (sxr[t % 2], sxr[t % 2].n)

            emit_xre(0)
            emit_xre(1)
            for t in range(NTT):
                for c in range(NCH):
                    bi = 2 + (t * NCH + c) % 2
                    b = bacq(bi)
                    T.wait(sV, acc_marks[t])
                    T.transpose(b[:, 0:128],
                                C[:, c * S + t * 128: c * S + (t + 1) * 128],
                                ident[:, :]).inc(sT)
                    V.wait(sT)
                    V.waits([xre_marks[t]])
                    V.waits(outt_free[t % 2])
                    V.tensor_tensor(outt[t % 2][:, c * 128:(c + 1) * 128], b[:, 0:128],
                                    xre[t % 2][:, c * 128:(c + 1) * 128],
                                    ALU.add).inc(sV)
                    bank_free[bi] = [(sV, sV.n)]
                Y.wait(sV)
                Y.dma_start(bass.AP(out_h, t * 128 * 1024, [[1024, 128], [1, 1024]]),
                            outt[t % 2][:, :]).inc(stp[t % 2], 16)
                xre_free[t % 2] = [(sV, sV.n)]
                outt_free[t % 2] = [(stp[t % 2], stp[t % 2].n)]
                if t + 2 < NTT:
                    emit_xre(t + 2)
            Y.wait(stp[0], stp[0].n)
            Y.wait(stp[1], stp[1].n)
            for ts_ in tap_sems.values():
                Y.wait(ts_, ts_.n)
            return

        @block.sync
        def _(eng):
            emit("sync", eng)

        @block.tensor
        def _(eng):
            emit("tensor", eng)

        @block.vector
        def _(eng):
            emit("vector", eng)

        @block.scalar
        def _(eng):
            emit("scalar", eng)

        @block.gpsimd
        def _(eng):
            emit("gpsimd", eng)

    return nc


_NC_CACHE = {}


def _get_nc(taps=()):
    key = tuple(sorted(taps))
    if key not in _NC_CACHE:
        _NC_CACHE[key] = build_nc(taps)
    return _NC_CACHE[key]


def _prep_shared(inputs):
    f32 = np.float32
    g = lambda k: np.asarray(inputs[k], dtype=f32)
    shared = {
        "wqT": np.ascontiguousarray(g("L0_attn_Wq").T),
        "wkT": np.ascontiguousarray(g("L0_attn_Wk").T),
        "wvT": np.ascontiguousarray(g("L0_attn_Wv").T),
        "woT": np.ascontiguousarray(g("L0_attn_Wo").T),
        "w1": np.ascontiguousarray(g("L0_ffn_cn_w1")),
        "w2": np.ascontiguousarray(g("L0_ffn_cn_w2")),
        "w3": np.ascontiguousarray(g("L0_ffn_cn_w3")),
        "gw1": np.ascontiguousarray(g("L0_ffn_gc_w1")),
        "gw2": np.ascontiguousarray(g("L0_ffn_gc_w2")),
        "ident": np.eye(128, dtype=f32),
        "cnb1": np.ascontiguousarray(g("L0_ffn_cn_b1").reshape(8, 128).T),
        "cnb2": np.ascontiguousarray(g("L0_ffn_cn_b2").reshape(4, 128).T),
        "cnb3": np.ascontiguousarray(g("L0_ffn_cn_b3").reshape(2, 128).T),
        "gb1": np.ascontiguousarray(g("L0_ffn_gc_b1").reshape(4, 128).T),
        "gb2": np.ascontiguousarray(g("L0_ffn_gc_b2").reshape(2, 128).T),
        "sc3": np.stack([g("L0_ffn_cn_b4").reshape(()),
                         g("L0_ffn_gc_b3").reshape(()),
                         g("L0_ffn_adaptive_range").reshape(())]).reshape(1, 3)
        .astype(f32),
        "w4gc": np.concatenate([
            np.ascontiguousarray(g("L0_ffn_cn_w4").reshape(2, 128).T),
            np.ascontiguousarray(g("L0_ffn_gc_w3").reshape(2, 128).T)], axis=1)
        .astype(f32),
    }
    return shared


def kernel(**inputs):
    x = np.asarray(inputs["L0_attn_x"], dtype=np.float32)
    shared = _prep_shared(inputs)
    in_maps = []
    for i in range(8):
        m = dict(shared)
        m["x"] = np.ascontiguousarray(x[i])
        in_maps.append(m)
    nc = _get_nc()
    res = run_bass_kernel_spmd(nc, in_maps, core_ids=list(range(8)))
    return np.stack([np.asarray(res.results[i]["out"]) for i in range(8)], axis=0)


# version 13
# speedup vs baseline: 1.0064x; 1.0064x over previous
import numpy as np
import concourse.bass as bass
import concourse.mybir as mybir
from concourse import library_config
from concourse.bass_utils import run_bass_kernel_spmd
from contextlib import ExitStack

F32 = mybir.dt.float32
F16 = mybir.dt.float16
AF = mybir.ActivationFunctionType
ALU = mybir.AluOpType
AX = mybir.AxisListType

S = 2048
D = 1024
NCH = 8
NTT = 16
NKT = 16
QT = 512
NQT = 4
FQT = 256
NFQT = S // FQT
LN_EPS = 1e-5
ATT_EPS = 1e-6


class Sem:
    __slots__ = ("h", "n")

    def __init__(self, h):
        self.h = h
        self.n = 0


class Inst:
    __slots__ = ("raw",)

    def __init__(self, raw):
        self.raw = raw

    def inc(self, sem, k=1):
        sem.n += k
        if self.raw is not None:
            self.raw.then_inc(sem.h, k)
        return self


class Eng:

    def __init__(self, real):
        self.real = real

    def wait(self, sem, n=None):
        v = sem.n if n is None else n
        if self.real is not None and v > 0:
            self.real.wait_ge(sem.h, v)

    def waits(self, marks):
        for (sem, n) in marks:
            self.wait(sem, n)

    def __getattr__(self, op):
        real = self.real

        def call(*a, **k):
            return Inst(getattr(real, op)(*a, **k) if real is not None else None)

        return call


def build_nc(taps=()):
    taps = tuple(t for t in taps)
    nc = bass.Bass("TRN2", target_bir_lowering=False)

    x_h = nc.dram_tensor("x", [S, D], F32, kind="ExternalInput")
    wqT = nc.dram_tensor("wqT", [D, D], F32, kind="ExternalInput")
    wkT = nc.dram_tensor("wkT", [D, D], F32, kind="ExternalInput")
    wvT = nc.dram_tensor("wvT", [D, D], F32, kind="ExternalInput")
    woT = nc.dram_tensor("woT", [D, D], F32, kind="ExternalInput")
    w1 = nc.dram_tensor("w1", [D, D], F32, kind="ExternalInput")
    w2 = nc.dram_tensor("w2", [D, 512], F32, kind="ExternalInput")
    w3 = nc.dram_tensor("w3", [512, 256], F32, kind="ExternalInput")
    gw1 = nc.dram_tensor("gw1", [D, 512], F32, kind="ExternalInput")
    gw2 = nc.dram_tensor("gw2", [512, 256], F32, kind="ExternalInput")
    ident_s = nc.dram_tensor("ident", [128, 128], F32, kind="ExternalInput")
    cnb1_s = nc.dram_tensor("cnb1", [128, 8], F32, kind="ExternalInput")
    cnb2_s = nc.dram_tensor("cnb2", [128, 4], F32, kind="ExternalInput")
    cnb3_s = nc.dram_tensor("cnb3", [128, 2], F32, kind="ExternalInput")
    gb1_s = nc.dram_tensor("gb1", [128, 4], F32, kind="ExternalInput")
    gb2_s = nc.dram_tensor("gb2", [128, 2], F32, kind="ExternalInput")
    sc3_s = nc.dram_tensor("sc3", [1, 3], F32, kind="ExternalInput")
    w4gc_s = nc.dram_tensor("w4gc", [128, 4], F32, kind="ExternalInput")
    out_h = nc.dram_tensor("out", [S, D], F32, kind="ExternalOutput")
    tap_shapes = {"xnraw": [128, 16384], "attn": [128, 16384], "acc": [128, 16384],
                  "gcin": [128, 8], "gf": [1, 8], "stream": [128, 16384]}
    tap_h = {t: nc.dram_tensor("tap_" + t, tap_shapes[t], F32, kind="ExternalOutput").ap()
             for t in taps}

    with ExitStack() as ctx:
        sbt = lambda name, shape, dt=F32: ctx.enter_context(nc.sbuf_tensor(name, shape, dt))
        A = sbt("slotA", [128, 16384])
        B = sbt("slotB", [128, 16384])
        C = sbt("slotC", [128, 16384])
        A16 = A.bitcast(F16)
        B16 = B.bitcast(F16)
        C16 = C.bitcast(F16)
        wblk0 = sbt("wblk0", [128, 1024])
        wblk1 = sbt("wblk1", [128, 1024])
        ident = sbt("identb", [128, 128])
        cnb1 = sbt("cnb1b", [128, 8])
        cnb2 = sbt("cnb2b", [128, 4])
        cnb3 = sbt("cnb3b", [128, 2])
        gb1 = sbt("gb1b", [128, 4])
        gb2 = sbt("gb2b", [128, 2])
        sc3 = sbt("sc3b", [1, 3])
        w4gc = sbt("w4gcb", [128, 4])
        ones64 = sbt("ones64", [1, 64])
        ones128 = sbt("ones128", [1, 128])
        gcp = sbt("gcp", [128, 128])
        gcin = sbt("gcin", [128, 8])
        gch1 = sbt("gch1", [128, 4])
        gch2 = sbt("gch2", [128, 2])
        gfb = sbt("gfb", [1, 8])
        tf_sb = sbt("tf_sb", [1, FQT])
        cp_sb = sbt("cp_sb", [1, FQT])
        g3b = sbt("g3b", [128, 512])
        psum = [ctx.enter_context(nc.psum_tensor(f"p{i}", [128, 512], F32))
                for i in range(8)]
        sems = {nm: Sem(ctx.enter_context(nc.semaphore(nm)))
                for nm in ["sc_", "sx", "swb0", "swb1", "sf", "sxr0", "sxr1",
                           "st0", "st1", "sT", "sV", "sS", "sG"]}
        tap_sems = {t: Sem(ctx.enter_context(nc.semaphore("stp_" + t)))
                    for t in taps}
        block = ctx.enter_context(nc.Block())

        xn_hi = A16[:, 0:16384]
        xn_lo = A16[:, 16384:32768]
        w1hi = A16[:, 0:8192]
        w1lo = A16[:, 8192:16384]
        w2res = A[:, 8192:12288]
        s16h = A16[:, 24576:26624]
        s16l = A16[:, 26624:28672]
        g1 = A[:, 14336:16384]
        xre = [A[:, 0:1024], A[:, 1024:2048]]
        outt = [A[:, 2048:3072], A[:, 3072:4096]]
        ln_dump = C[:, 0:1024]
        ln_xnt = [C[:, 1024:2048], C[:, 2048:3072]]
        st_ = C[:, 3072:3248]
        ln_s1, ln_s2, ln_mu, ln_var, ln_m2, ln_std, ln_r0, ln_t, ln_rstd, ln_nmr = (
            st_[:, i * 16:(i + 1) * 16] for i in range(10))
        k_stack = C16[:, 0:4096]
        q_stack = C16[:, 4096:8192]
        qrep_hi2 = [C16[:, 8192:8704], C16[:, 29760:30272]]
        qrep_lo2 = [C16[:, 8704:9216], C16[:, 30272:30784]]
        wq16 = (C16[:, 9216:10240], C16[:, 10240:11264])
        wk16 = (C16[:, 11264:12288], C16[:, 12288:13312])
        wv16 = (C16[:, 13312:14336], C16[:, 14336:15360])
        vaugP = C[:, 7680:9760]
        vfm = [C[:, 9760:10272], C[:, 10272:10784]]
        sig = [C[:, 10784:11296], C[:, 11296:11808], C[:, 15392:15904]]
        n_sb2 = [C[:, 11808:12320], C[:, 12832:13344]]
        recip = C[:, 12320:12832]

        def emit(which, eng):
            T = Eng(eng if which == "tensor" else None)
            V = Eng(eng if which == "vector" else None)
            Sc = Eng(eng if which == "scalar" else None)
            G = Eng(eng if which == "gpsimd" else None)
            Y = Eng(eng if which == "sync" else None)
            sc_, sx, sf = sems["sc_"], sems["sx"], sems["sf"]
            swb = [sems["swb0"], sems["swb1"]]
            sxr = [sems["sxr0"], sems["sxr1"]]
            stp = [sems["st0"], sems["st1"]]
            sT, sV, sS = sems["sT"], sems["sV"], sems["sS"]
            sG = sems["sG"]
            for s_ in list(sems.values()) + list(tap_sems.values()):
                s_.n = 0
            bank_free = {i: [] for i in range(8)}

            def bacq(i):
                T.waits(bank_free[i])
                return psum[i]

            Y.dma_start(B.ap().rearrange("p (t n) -> p t n", t=16),
                        bass.AP(x_h, 0, [[1024, 128], [128 * 1024, 16], [1, 1024]])
                        ).inc(sx, 16)
            for dst, src in [(ident, ident_s), (cnb1, cnb1_s), (cnb2, cnb2_s),
                             (cnb3, cnb3_s), (gb1, gb1_s), (gb2, gb2_s),
                             (sc3, sc3_s), (w4gc, w4gc_s)]:
                Y.dma_start(dst[:, :], src.ap()).inc(sc_, 16)
            consts_n = sc_.n
            V.memset(gfb[:, :], 0.0).inc(sV)
            V.memset(ones64[:, :], 1.0).inc(sV)
            V.memset(ones128[:, :], 1.0).inc(sV)
            V.memset(vaugP[:, :], 1.0).inc(sV)

            V.wait(sx, 16)
            Sc.wait(sx, 16)
            for t in range(NTT):
                xt = B[:, t * 1024:(t + 1) * 1024]
                V.reduce_sum(ln_s1[:, t:t + 1], xt, axis=AX.X).inc(sV)
                if t > 0:
                    Sc.wait(sS)
                Sc.activation(ln_dump[:, :], xt, AF.Square,
                              accum_out=ln_s2[:, t:t + 1]).inc(sS)
            V.wait(sS)
            V.wait(sV)
            V.tensor_scalar(ln_mu[:, :], ln_s1[:, :], 1.0 / D, None, ALU.mult).inc(sV)
            V.tensor_scalar(ln_var[:, :], ln_s2[:, :], 1.0 / D, None, ALU.mult).inc(sV)
            V.wait(sV)
            V.tensor_tensor(ln_m2[:, :], ln_mu[:, :], ln_mu[:, :], ALU.mult).inc(sV)
            V.wait(sV)
            V.tensor_tensor(ln_var[:, :], ln_var[:, :], ln_m2[:, :], ALU.subtract).inc(sV)
            V.wait(sV)
            V.tensor_scalar(ln_var[:, :], ln_var[:, :], LN_EPS, None, ALU.add).inc(sV)
            Sc.wait(sV)
            Sc.activation(ln_std[:, :], ln_var[:, :], AF.Sqrt).inc(sS)
            V.wait(sS)
            V.reciprocal(ln_r0[:, :], ln_std[:, :]).inc(sV)
            V.wait(sV)
            V.tensor_tensor(ln_t[:, :], ln_r0[:, :], ln_r0[:, :], ALU.mult).inc(sV)
            V.wait(sV)
            V.tensor_tensor(ln_t[:, :], ln_t[:, :], ln_var[:, :], ALU.mult).inc(sV)
            V.wait(sV)
            V.tensor_scalar(ln_t[:, :], ln_t[:, :], -0.5, 1.5, ALU.mult, ALU.add).inc(sV)
            V.wait(sV)
            V.tensor_tensor(ln_rstd[:, :], ln_r0[:, :], ln_t[:, :], ALU.mult).inc(sV)
            V.wait(sV)
            V.scalar_tensor_tensor(ln_nmr[:, :], ln_mu[:, :], -1.0, ln_rstd[:, :],
                                   ALU.mult, ALU.mult).inc(sV)
            ln_done = sV.n

            wblks = [wblk0, wblk1]
            wblk_free = [[], []]
            nblk = [0]

            def load_block(src_handle, col, nk, row_len):
                i = nblk[0] % 2
                buf = wblks[i]
                Y.waits(wblk_free[i])
                ap = bass.AP(src_handle, col * 128,
                             [[row_len, 128], [128 * row_len, nk], [1, 128]])
                Y.dma_start(buf.ap()[:, 0:nk * 128].rearrange("p (kc o) -> p kc o", kc=nk),
                            ap).inc(swb[i], 16)
                nblk[0] += 1
                return buf, i, (swb[i], swb[i].n)

            def split16(buf, i, wready, hi, lo, width=1024):
                V.waits([wready])
                V.tensor_copy(hi[:, 0:width], buf[:, 0:width]).inc(sV)
                V.wait(sV)
                V.tensor_tensor(lo[:, 0:width], buf[:, 0:width], hi[:, 0:width],
                                ALU.subtract).inc(sV)
                wblk_free[i] = [(sV, sV.n)]
                return sV.n

            state = {"g": 0, "pending": None, "qrep_free": [[], []],
                     "qrep_done": [0, 0], "qrep_next": None, "carry": None,
                     "tail": None, "sig_free": [[], [], []]}
            def load_split_pair(pair, guard_mark):
                bq, iq, rq = load_block(wqT, pair, 8, D)
                V.wait(sT, guard_mark)
                sqm = split16(bq, iq, rq, *wq16)
                bk, ik, rk = load_block(wkT, pair, 8, D)
                skm = split16(bk, ik, rk, *wk16)
                bv, iv, rv = load_block(wvT, pair, 8, D)
                svm = split16(bv, iv, rv, *wv16)
                return [sqm, skm, svm]

            state["splits"] = load_split_pair(0, 0)
            xnt_free = [[], []]
            xn_marks = []
            for t in range(NTT):
                xt = B[:, t * 1024:(t + 1) * 1024]
                xnt = ln_xnt[t % 2]
                Sc.wait(sV, ln_done)
                Sc.waits(xnt_free[t % 2])
                Sc.activation(xnt[:, :], xt, AF.Identity,
                              bias=ln_nmr[:, t:t + 1], scale=ln_rstd[:, t:t + 1]).inc(sS)
                xnt_ready = sS.n
                for c in range(NCH):
                    bi = 6 + (t * NCH + c) % 2
                    b = bacq(bi)
                    T.wait(sS, xnt_ready)
                    T.wait(sc_, consts_n)
                    T.transpose(b[:, 0:128], xnt[:, c * 128:(c + 1) * 128],
                                ident[:, :]).inc(sT)
                    hi_sl = xn_hi[:, c * S + t * 128: c * S + (t + 1) * 128]
                    lo_sl = xn_lo[:, c * S + t * 128: c * S + (t + 1) * 128]
                    Sc.wait(sT)
                    Sc.copy(hi_sl, b[:, 0:128]).inc(sS)
                    V.wait(sS)
                    V.tensor_tensor(lo_sl, b[:, 0:128], hi_sl, ALU.subtract).inc(sV)
                    V.reduce_sum(gcp[:, c * NTT + t: c * NTT + t + 1], b[:, 0:128],
                                 axis=AX.X).inc(sV)
                    bank_free[bi] = [(sV, sV.n)]
                xnt_free[t % 2] = [(sT, sT.n)]
                xn_marks.append((sV, sV.n))
            V.wait(sV)
            V.reduce_sum(gcin[:, 0:NCH],
                         gcp.ap().rearrange("p (c t) -> p c t", c=NCH),
                         axis=AX.X).inc(sV)
            V.wait(sV)
            V.tensor_scalar(gcin[:, 0:NCH], gcin[:, 0:NCH], 1.0 / S, None,
                            ALU.mult).inc(sV)
            gcin_done = sV.n
            tap_marks = {}
            if "xnraw" in taps:
                Y.wait(sV, gcin_done)
                Y.dma_start(tap_h["xnraw"], A[:, :]).inc(tap_sems["xnraw"], 16)
                tap_marks["xnraw"] = (tap_sems["xnraw"], 16)

            for oc in range(4):
                buf, ib, wready = load_block(gw1, oc, 8, 512)
                b = bacq(7)
                T.waits([wready])
                T.wait(sV, gcin_done)
                for kc in range(NCH):
                    inst = T.matmul(b[:, 0:1], buf[:, kc * 128:(kc + 1) * 128],
                                    gcin[:, kc:kc + 1], start=(kc == 0),
                                    stop=(kc == NCH - 1))
                    if kc == NCH - 1:
                        inst.inc(sT)
                Sc.wait(sT)
                Sc.activation(gch1[:, oc:oc + 1], b[:, 0:1], AF.Gelu,
                              bias=gb1[:, oc:oc + 1]).inc(sS)
                bank_free[7] = [(sS, sS.n)]
                wblk_free[ib] = [(sT, sT.n)]
            for oc in range(2):
                buf, ib, wready = load_block(gw2, oc, 4, 256)
                b = bacq(7)
                T.waits([wready])
                T.wait(sS)
                for kc in range(4):
                    inst = T.matmul(b[:, 0:1], buf[:, kc * 128:(kc + 1) * 128],
                                    gch1[:, kc:kc + 1], start=(kc == 0), stop=(kc == 3))
                    if kc == 3:
                        inst.inc(sT)
                Sc.wait(sT)
                Sc.activation(gch2[:, oc:oc + 1], b[:, 0:1], AF.Gelu,
                              bias=gb2[:, oc:oc + 1]).inc(sS)
                bank_free[7] = [(sS, sS.n)]
                wblk_free[ib] = [(sT, sT.n)]
            b = bacq(7)
            T.wait(sS)
            T.wait(sc_, consts_n)
            for kc in range(2):
                inst = T.matmul(b[0:1, 0:1], w4gc[:, 2 + kc:3 + kc],
                                gch2[:, kc:kc + 1], start=(kc == 0), stop=(kc == 1))
                if kc == 1:
                    inst.inc(sT)
            Sc.wait(sT)
            Sc.activation(gfb[0:1, 0:1], b[0:1, 0:1], AF.Sigmoid,
                          bias=sc3[0:1, 1:2]).inc(sS)
            bank_free[7] = [(sS, sS.n)]
            V.wait(sS)
            V.tensor_scalar(gfb[0:1, 1:2], sc3[0:1, 2:3], 0.6, None, ALU.mult).inc(sV)
            V.tensor_scalar(gfb[0:1, 3:4], sc3[0:1, 2:3], 1.4, None, ALU.mult).inc(sV)
            V.tensor_scalar(gfb[0:1, 4:5], sc3[0:1, 2:3], -1.0, 0.1, ALU.mult,
                            ALU.add).inc(sV)
            V.wait(sV)
            V.tensor_tensor(gfb[0:1, 5:6], gfb[0:1, 0:1], gfb[0:1, 3:4], ALU.mult).inc(sV)
            V.wait(sV)
            V.tensor_tensor(gfb[0:1, 2:3], gfb[0:1, 4:5], gfb[0:1, 5:6], ALU.add).inc(sV)
            kconst_done = sV.n
            if "gcin" in taps:
                Y.wait(sV, gcin_done)
                Y.dma_start(tap_h["gcin"], gcin[:, :]).inc(tap_sems["gcin"], 16)
            if "gf" in taps:
                Y.wait(sV, kconst_done)
                Y.dma_start(tap_h["gf"], gfb[:, :]).inc(tap_sems["gf"], 16)

            for pair in range(NCH):
                splits = state["splits"]
                w16 = [wq16, wk16, wv16]

                for proj in range(3):
                    whi, wlo = w16[proj]
                    for qt in range(NQT):
                        bi = [4, 5, 0, 1][(proj * NQT + qt) % 4]
                        b = bacq(bi)
                        T.wait(sV, splits[proj])
                        if pair == 0:
                            T.waits([xn_marks[4 * qt + 3]])
                        passes = []
                        for kc in range(NCH):
                            xh = xn_hi[:, kc * S + qt * QT: kc * S + qt * QT + QT]
                            xl = xn_lo[:, kc * S + qt * QT: kc * S + qt * QT + QT]
                            lh = whi[:, kc * 128:(kc + 1) * 128]
                            ll = wlo[:, kc * 128:(kc + 1) * 128]
                            passes += [(lh, xh), (lh, xl), (ll, xh)]
                        n = len(passes)
                        for ip, (lT, rhs) in enumerate(passes):
                            inst = T.matmul(b[:, :], lT, rhs, start=(ip == 0),
                                            stop=(ip == n - 1))
                            if ip == n - 1:
                                inst.inc(sT)
                        mmdone = sT.n
                        if proj < 2:
                            stk = q_stack if proj == 0 else k_stack
                            dA = stk[:, qt * QT: qt * QT + QT]
                            dB = stk[:, S + qt * QT: S + qt * QT + QT]
                            Sc.wait(sT, mmdone)
                            Sc.copy(dA[0:64, :], b[0:64, :]).inc(sS)
                            Sc.copy(dB[0:64, :], b[64:128, :]).inc(sS)
                            V.wait(sS)
                            V.tensor_tensor(dA[64:128, :], b[0:64, :], dA[0:64, :],
                                            ALU.subtract).inc(sV)
                            V.tensor_tensor(dB[64:128, :], b[64:128, :], dB[0:64, :],
                                            ALU.subtract).inc(sV)
                            bank_free[bi] = [(sV, sV.n)]
                        else:
                            vf = vfm[qt % 2]
                            V.wait(sT, mmdone)
                            V.tensor_copy(vf[:, :], b[:, :]).inc(sV)
                            bank_free[bi] = [(sV, sV.n)]
                            vf_ready = (sV, sV.n)
                            for j in range(4):
                                tau = qt * 4 + j
                                ti_ = 6 + tau % 2
                                tb = bacq(ti_)
                                T.waits([vf_ready])
                                T.transpose(tb[:, 0:128], vf[:, j * 128:(j + 1) * 128],
                                            ident[:, :]).inc(sT)
                                Sc.wait(sT)
                                Sc.copy(bass.AP(vaugP.tensor, vaugP.offset + tau * 130,
                                                [[16384, 128], [65, 2], [1, 64]]),
                                        tb[:, 0:128].rearrange("p (h d) -> p h d", h=2)
                                        ).inc(sS)
                                bank_free[ti_] = [(sS, sS.n)]
                att_S = sS.n
                state["proj_done_mark"] = sT.n

                def mk_qrep(hi_, qc, g):
                    par = g % 2

                    def build():
                        V.wait(sS, att_S)
                        V.wait(sV)
                        V.waits(state["qrep_free"][par])
                        so = hi_ * S
                        qh = q_stack[0:64, so + qc * QT: so + qc * QT + QT]
                        ql = q_stack[64:128, so + qc * QT: so + qc * QT + QT]
                        V.tensor_copy(qrep_hi2[par][0:64, :], qh).inc(sV)
                        V.tensor_copy(qrep_hi2[par][64:128, :], qh).inc(sV)
                        V.tensor_copy(qrep_lo2[par][0:64, :], ql).inc(sV)
                        V.tensor_copy(qrep_lo2[par][64:128, :], ql).inc(sV)
                        state["qrep_done"][par] = sV.n
                    return build

                for hi_ in range(2):
                    stk_off = hi_ * S
                    for qc in range(NQT):
                        g = state["g"]
                        state["g"] += 1
                        par = g % 2
                        qrep_hi, qrep_lo = qrep_hi2[par], qrep_lo2[par]
                        n_sb = n_sb2[par]
                        wvb = [2, 3, 6][g % 3]
                        if state["qrep_next"] is None:
                            mk_qrep(hi_, qc, g)()
                        if (hi_, qc) == (0, 1) and pair + 1 < NCH:
                            state["splits"] = load_split_pair(
                                pair + 1, state["proj_done_mark"])
                        if qc < NQT - 1:
                            state["qrep_next"] = mk_qrep(hi_, qc + 1, g + 1)
                        elif hi_ == 0:
                            state["qrep_next"] = mk_qrep(1, 0, g + 1)
                        else:
                            state["qrep_next"] = None
                        qrep_done = state["qrep_done"][par]
                        b2 = bacq(wvb)
                        sig_marks = {}
                        wv_marks = {}

                        def emit_scores(kt, qrep_done=qrep_done):
                            bsc = bacq(kt % 2)
                            lT = k_stack[:, stk_off + kt * 128: stk_off + (kt + 1) * 128]
                            T.wait(sV, qrep_done)
                            T.matmul(bsc[:, :], lT, qrep_hi[:, :], start=True, stop=False)
                            T.matmul(bsc[:, :], lT, qrep_lo[:, :], start=False,
                                     stop=True).inc(sT)
                            return sT.n

                        def emit_sig(kt, sc_mark, g=g):
                            bufi = (g * NKT + kt) % 3
                            sgb = sig[bufi]
                            Sc.wait(sT, sc_mark)
                            Sc.waits(state["sig_free"][bufi])
                            Sc.activation(sgb[:, :], psum[kt % 2][:, :], AF.Sigmoid,
                                          scale=0.125).inc(sS)
                            bank_free[kt % 2] = [(sS, sS.n)]
                            sig_marks[kt] = sS.n

                        def emit_wv(kt, g=g, b2=b2, hi_=hi_, sig_marks=sig_marks,
                                    wv_marks=wv_marks):
                            bufi = (g * NKT + kt) % 3
                            T.wait(sS, sig_marks[kt])
                            lhs = bass.AP(vaugP.tensor,
                                          vaugP.offset + kt * 130 + hi_ * 65,
                                          [[16384, 128], [1, 65]])
                            T.matmul(b2[0:65, :], lhs,
                                     sig[bufi][:, :], start=(kt == 0),
                                     stop=(kt == NKT - 1)).inc(sT)
                            wv_marks[kt] = sT.n
                            state["sig_free"][bufi] = [(sT, sT.n)]

                        def mk_tail(g=g, b2=b2, n_sb=n_sb, wvb=wvb, hi_=hi_, qc=qc,
                                    pair=pair, wv_marks=wv_marks):
                            dst = B[hi_ * 64: hi_ * 64 + 64,
                                    pair * S + qc * QT: pair * S + qc * QT + QT]

                            def tail_norm():
                                V.wait(sT, wv_marks[NKT - 1])
                                V.tensor_scalar(n_sb[0:1, 0:512], b2[64:65, :],
                                                ATT_EPS, None, ALU.add).inc(sV)
                                nx = sV.n

                                def pending():
                                    b3 = bacq(4)
                                    T.wait(sV, nx)
                                    T.matmul(b3[0:64, :], ones64[:, :],
                                             n_sb[0:1, 0:512], start=True,
                                             stop=True).inc(sT)
                                    V.wait(sT)
                                    V.wait(sV)
                                    V.reciprocal(recip[0:64, 0:512],
                                                 b3[0:64, :]).inc(sV)
                                    bank_free[4] = [(sV, sV.n)]
                                    V.wait(sV)
                                    V.tensor_tensor(dst, b2[0:64, :],
                                                    recip[0:64, 0:512],
                                                    ALU.mult).inc(sV)
                                    bank_free[wvb] = [(sV, sV.n)]
                                state["pending"] = pending
                            return tail_norm

                        carry = state["carry"]
                        for kt in range(NKT):
                            sc_mark = emit_scores(kt)
                            if kt <= 1 and carry is not None:
                                carry(kt)
                            emit_sig(kt, sc_mark)
                            if kt == 1 and state["tail"] is not None:
                                state["tail"]()
                                state["tail"] = None
                            if kt >= 2:
                                emit_wv(kt - 2)
                            if kt == 4 and state["qrep_next"] is not None:
                                state["qrep_next"]()
                            if kt == 6 and state["pending"] is not None:
                                state["pending"]()
                                state["pending"] = None
                        state["qrep_free"][par] = [(sT, sT.n)]
                        if (hi_, qc) == (1, NQT - 1):
                            emit_wv(NKT - 2)
                            emit_wv(NKT - 1)
                            if state["pending"] is not None:
                                state["pending"]()
                                state["pending"] = None
                            mk_tail()()
                            state["pending"]()
                            state["pending"] = None
                            state["carry"] = None
                            state["tail"] = None
                        else:
                            def mk_carry(emit_wv=emit_wv):
                                def carry_fn(k):
                                    emit_wv(NKT - 2 + k)
                                return carry_fn
                            state["carry"] = mk_carry()
                            state["tail"] = mk_tail()
            if state["pending"] is not None:
                state["pending"]()
                state["pending"] = None
            attn_V = sV.n
            if "attn" in taps:
                Y.wait(sV, attn_V)
                Y.dma_start(tap_h["attn"], B[:, :]).inc(tap_sems["attn"], 16)
                tap_marks["attn"] = (tap_sems["attn"], 16)

            for oc in range(NCH):
                buf, ib, wready = load_block(woT, oc, 8, D)
                for qc in range(NQT):
                    bi = 4 + qc % 2
                    b = bacq(bi)
                    T.waits([wready])
                    T.wait(sV, attn_V)
                    for kc in range(NCH):
                        rhs = B[:, kc * S + qc * QT: kc * S + qc * QT + QT]
                        inst = T.matmul(b[:, :], buf[:, kc * 128:(kc + 1) * 128], rhs,
                                        start=(kc == 0), stop=(kc == NCH - 1))
                        if kc == NCH - 1:
                            inst.inc(sT)
                    Sc.wait(sT)
                    Sc.copy(C[:, oc * S + qc * QT: oc * S + qc * QT + QT],
                            b[:, :]).inc(sS)
                    bank_free[bi] = [(sS, sS.n)]
                wblk_free[ib] = [(sT, sT.n)]
            wo_T, wo_S = sT.n, sS.n
            if "acc" in taps:
                Y.wait(sS, wo_S)
                Y.dma_start(tap_h["acc"], C[:, :]).inc(tap_sems["acc"], 16)
                tap_marks["acc"] = (tap_sems["acc"], 16)

            V.wait(sT, wo_T)
            if "attn" in tap_marks:
                V.waits([tap_marks["attn"]])
            Bv = B.ap().rearrange("p (c t) -> p c t", c=NCH)
            xhv = xn_hi.rearrange("p (c t) -> p c t", c=NCH)
            xlv = xn_lo.rearrange("p (c t) -> p c t", c=NCH)
            V.tensor_tensor(Bv[:, :, 0:FQT], xhv[:, :, 0:FQT], xlv[:, :, 0:FQT],
                            ALU.add).inc(sV)
            sinit0 = (sV, sV.n)
            G.wait(sT, wo_T)
            if "attn" in tap_marks:
                G.waits([tap_marks["attn"]])
            sinit = []
            for c in range(4):
                E_ = V if c < 2 else G
                sm = sV if c < 2 else sG
                for cc in (c, c + 4):
                    E_.tensor_tensor(B[:, cc * S + FQT:(cc + 1) * S],
                                     xn_hi[:, cc * S + FQT:(cc + 1) * S],
                                     xn_lo[:, cc * S + FQT:(cc + 1) * S],
                                     ALU.add).inc(sm)
                sinit.append((sm, sm.n))
            sinit_g = sG.n
            stream_ready = sV.n
            if "xnraw" in tap_marks:
                V.waits([tap_marks["xnraw"]])
            w1_marks = []
            for oc in range(NCH):
                buf, ib, wready = load_block(w1, oc, 8, D)
                E_ = V if oc < 4 else G
                sm = sV if oc < 4 else sG
                E_.waits([sinit0, sinit[oc // 2]])
                E_.wait(sG, sinit_g)
                E_.waits([wready])
                hi = w1hi.rearrange("p (oc w) -> p oc w", oc=NCH)[:, oc, :]
                lo = w1lo.rearrange("p (oc w) -> p oc w", oc=NCH)[:, oc, :]
                E_.tensor_copy(hi, buf[:, 0:1024]).inc(sm)
                E_.wait(sm)
                E_.tensor_tensor(lo, buf[:, 0:1024], hi, ALU.subtract).inc(sm)
                wblk_free[ib] = [(sm, sm.n)]
                w1_marks.append((sm, sm.n))
            for oc in range(4):
                Y.waits([sinit0, sinit[oc]])
                ap = bass.AP(w2, oc * 128, [[512, 128], [128 * 512, 8], [1, 128]])
                Y.dma_start(w2res.rearrange("p (oc w) -> p oc w", oc=4)[:, oc, :]
                            .rearrange("p (kc o) -> p kc o", kc=8), ap).inc(sf, 16)
            for oc in range(2):
                Y.waits(wblk_free[0])
                Y.waits(wblk_free[1])
                ap = bass.AP(w3, oc * 128, [[256, 128], [128 * 256, 4], [1, 128]])
                Y.dma_start(wblk0.ap()[:, oc * 512:(oc + 1) * 512]
                            .rearrange("p (kc o) -> p kc o", kc=4), ap).inc(sf, 16)
            ffn_w_ready = sf.n
            w1_ready = sV.n

            strv = Bv
            Cv = C.ap().rearrange("p (c t) -> p c t", c=NCH)
            acc_marks = [0] * NTT

            def emit_s16(cy, ti, first=False):
                t0 = ti * FQT
                if first:
                    V.waits([sinit0])
                else:
                    V.wait(sT, state["cn1_done"])
                    V.wait(sV)
                    V.wait(sG, sinit_g)
                V.tensor_copy(s16h.rearrange("p (c t) -> p c t", c=NCH),
                              strv[:, :, t0:t0 + FQT]).inc(sV)
                V.wait(sV)
                V.tensor_tensor(s16l.rearrange("p (c t) -> p c t", c=NCH),
                                strv[:, :, t0:t0 + FQT],
                                s16h.rearrange("p (c t) -> p c t", c=NCH),
                                ALU.subtract).inc(sV)
                state["s16_ready"] = sV.n

            def emit_cn1(cy, ti):
                for oc in range(NCH):
                    bi = 4 + oc % 2
                    b = psum[bi]
                    T.waits(bank_free[bi])
                    T.waits([w1_marks[oc]])
                    T.wait(sV, state["s16_ready"])
                    passes = []
                    for kc in range(NCH):
                        lh = w1hi[:, oc * 1024 + kc * 128: oc * 1024 + (kc + 1) * 128]
                        ll = w1lo[:, oc * 1024 + kc * 128: oc * 1024 + (kc + 1) * 128]
                        rh = s16h[:, kc * FQT:(kc + 1) * FQT]
                        rl = s16l[:, kc * FQT:(kc + 1) * FQT]
                        passes += [(lh, rh), (lh, rl), (ll, rh)]
                    n = len(passes)
                    for ip, (lT, rhs) in enumerate(passes):
                        inst = T.matmul(b[:, 0:FQT], lT, rhs, start=(ip == 0),
                                        stop=(ip == n - 1))
                        if ip == n - 1:
                            inst.inc(sT)
                    Sc.wait(sT)
                    Sc.wait(sc_, consts_n)
                    Sc.wait(sG, sinit_g)
                    Sc.wait(sV, stream_ready)
                    Sc.activation(g1[:, oc * FQT:(oc + 1) * FQT], b[:, 0:FQT],
                                  AF.Gelu, bias=cnb1[:, oc:oc + 1]).inc(sS)
                    bank_free[bi] = [(sS, sS.n)]
                state["g1_ready"] = sS.n
                state["cn1_done"] = sT.n

            state["g2_guard"] = list(wblk_free[1])
            emit_s16(0, 0, first=True)
            emit_cn1(0, 0)
            for cy in range(3):
                for ti in range(NFQT):
                    t0 = ti * FQT
                    g1_ready = state["g1_ready"]
                    if state.get("ffn_fin") is not None:
                        state["ffn_fin"]()
                        state["ffn_fin"] = None

                    for oc in range(4):
                        bi = 4 + oc % 2
                        b = bacq(bi)
                        T.wait(sf, ffn_w_ready)
                        T.wait(sS, g1_ready)
                        for kc in range(NCH):
                            inst = T.matmul(b[:, 0:FQT],
                                            w2res[:, oc * 1024 + kc * 128:
                                                  oc * 1024 + (kc + 1) * 128],
                                            g1[:, kc * FQT:(kc + 1) * FQT],
                                            start=(kc == 0), stop=(kc == NCH - 1))
                            if kc == NCH - 1:
                                inst.inc(sT)
                        Sc.wait(sT)
                        Sc.waits(state.get("g2_guard", []))
                        Sc.activation(wblk1[:, oc * FQT:(oc + 1) * FQT], b[:, 0:FQT],
                                      AF.Gelu, bias=cnb2[:, oc:oc + 1]).inc(sS)
                        bank_free[bi] = [(sS, sS.n)]
                    g2_ready = sS.n

                    for oc in range(2):
                        bi = 4 + oc % 2
                        b = bacq(bi)
                        T.wait(sS, g2_ready)
                        for kc in range(4):
                            inst = T.matmul(b[:, 0:FQT],
                                            wblk0[:, oc * 512 + kc * 128:
                                                  oc * 512 + (kc + 1) * 128],
                                            wblk1[:, kc * FQT:(kc + 1) * FQT],
                                            start=(kc == 0), stop=(kc == 3))
                            if kc == 3:
                                inst.inc(sT)
                        Sc.wait(sT)
                        Sc.activation(g3b[:, oc * FQT:(oc + 1) * FQT], b[:, 0:FQT],
                                      AF.Gelu, bias=cnb3[:, oc:oc + 1]).inc(sS)
                        bank_free[bi] = [(sS, sS.n)]
                    g3_ready = sS.n

                    nxt = (cy, ti + 1) if ti + 1 < NFQT else                           ((cy + 1, 0) if cy < 2 else None)
                    if nxt is not None:
                        emit_s16(*nxt)
                        emit_cn1(*nxt)

                    b7 = bacq(7)
                    T.wait(sS, g3_ready)
                    T.wait(sc_, consts_n)
                    for kc in range(2):
                        inst = T.matmul(b7[0:1, 0:FQT], w4gc[:, kc:kc + 1],
                                        g3b[:, kc * FQT:(kc + 1) * FQT],
                                        start=(kc == 0), stop=(kc == 1))
                        if kc == 1:
                            inst.inc(sT)
                    Sc.wait(sT)
                    Sc.wait(sV, kconst_done)
                    Sc.wait(sV)
                    Sc.activation(tf_sb[0:1, 0:FQT], b7[0:1, 0:FQT], AF.Sigmoid,
                                  bias=sc3[0:1, 0:1]).inc(sS)
                    bank_free[7] = [(sS, sS.n)]
                    V.wait(sS)
                    V.wait(sT)
                    V.tensor_scalar(cp_sb[0:1, 0:FQT], tf_sb[0:1, 0:FQT],
                                    gfb[0:1, 1:2], gfb[0:1, 2:3],
                                    ALU.mult, ALU.add).inc(sV)
                    cp_ready = sV.n

                    def mk_fin(cy=cy, ti=ti, t0=t0, cp_ready=cp_ready):
                        def fin():
                            b6 = bacq(6)
                            T.wait(sV, cp_ready)
                            T.matmul(b6[:, 0:FQT], ones128[:, :], cp_sb[0:1, 0:FQT],
                                     start=True, stop=True).inc(sT)
                            cp_bcast = sT.n
                            for (a, bb, sa, sb_) in [(0, 4, 1.0, -1.0),
                                                     (1, 5, 1.0, -1.0),
                                                     (2, 6, -1.0, 1.0),
                                                     (3, 7, -1.0, 1.0)]:
                                olda = strv[:, a, t0:t0 + FQT]
                                oldb = strv[:, bb, t0:t0 + FQT]
                                V.waits(bank_free[0])
                                V.waits(bank_free[1])
                                V.wait(sT, cp_bcast)
                                V.scalar_tensor_tensor(psum[0][:, 0:FQT], oldb, sa,
                                                       b6[:, 0:FQT], ALU.mult,
                                                       ALU.mult).inc(sV)
                                V.scalar_tensor_tensor(psum[1][:, 0:FQT], olda, sb_,
                                                       b6[:, 0:FQT], ALU.mult,
                                                       ALU.mult).inc(sV)
                                V.wait(sV)
                                V.tensor_tensor(olda, olda, psum[0][:, 0:FQT],
                                                ALU.add).inc(sV)
                                V.tensor_tensor(oldb, oldb, psum[1][:, 0:FQT],
                                                ALU.add).inc(sV)
                                bank_free[0] = [(sV, sV.n)]
                                bank_free[1] = [(sV, sV.n)]
                            bank_free[6] = [(sV, sV.n)]
                            if cy == 2:
                                V.wait(sS, wo_S)
                                if "acc" in tap_marks:
                                    V.waits([tap_marks["acc"]])
                                for half in range(2):
                                    t = 2 * ti + half
                                    V.wait(sV)
                                    V.tensor_tensor(Cv[:, :, t * 128:(t + 1) * 128],
                                                    Cv[:, :, t * 128:(t + 1) * 128],
                                                    Bv[:, :, t * 128:(t + 1) * 128],
                                                    ALU.add).inc(sV)
                                    acc_marks[t] = sV.n
                        return fin

                    state["ffn_fin"] = mk_fin()
            if state.get("ffn_fin") is not None:
                state["ffn_fin"]()
                state["ffn_fin"] = None
            ffn_done = sV.n
            if "stream" in taps:
                Y.wait(sV, ffn_done)
                Y.dma_start(tap_h["stream"], B[:, :]).inc(tap_sems["stream"], 16)
                tap_marks["stream"] = (tap_sems["stream"], 16)

            for t in range(NTT):
                xre_t = B.ap().rearrange("p (c t) -> p c t", c=NCH)[:, :, t * 128:(t + 1) * 128]
                xsrc = bass.AP(x_h, t * 128 * 1024,
                               [[1024, 128], [128, 8], [1, 128]])
                Sc.wait(sV, acc_marks[t])
                if "stream" in tap_marks:
                    Sc.waits([tap_marks["stream"]])
                Sc.waits(state.get(("xre_prev", t % 2), []))
                Sc.dma_start(xre_t, xsrc).inc(sxr[t % 2], 16)
                xmark = (sxr[t % 2], sxr[t % 2].n)
                state[("xre_prev", t % 2)] = [xmark]
                for c in range(NCH):
                    bi = 2 + (t * NCH + c) % 2
                    b = bacq(bi)
                    T.wait(sV, acc_marks[t])
                    T.transpose(b[:, 0:128],
                                C[:, c * S + t * 128: c * S + (t + 1) * 128],
                                ident[:, :]).inc(sT)
                    V.wait(sT)
                    V.waits([xmark])
                    V.tensor_tensor(xre_t[:, c, :], b[:, 0:128], xre_t[:, c, :],
                                    ALU.add).inc(sV)
                    bank_free[bi] = [(sV, sV.n)]
                Y.wait(sV)
                Y.dma_start(bass.AP(out_h, t * 128 * 1024,
                                    [[1024, 128], [128, 8], [1, 128]]),
                            xre_t).inc(stp[t % 2], 16)
            Y.wait(stp[0], stp[0].n)
            Y.wait(stp[1], stp[1].n)
            for ts_ in tap_sems.values():
                Y.wait(ts_, ts_.n)
            return

        @block.sync
        def _(eng):
            emit("sync", eng)

        @block.tensor
        def _(eng):
            emit("tensor", eng)

        @block.vector
        def _(eng):
            emit("vector", eng)

        @block.scalar
        def _(eng):
            emit("scalar", eng)

        @block.gpsimd
        def _(eng):
            emit("gpsimd", eng)

    return nc


_NC_CACHE = {}


def _get_nc(taps=()):
    key = tuple(sorted(taps))
    if key not in _NC_CACHE:
        _NC_CACHE[key] = build_nc(taps)
    return _NC_CACHE[key]


def _prep_shared(inputs):
    f32 = np.float32
    g = lambda k: np.asarray(inputs[k], dtype=f32)
    shared = {
        "wqT": np.ascontiguousarray(g("L0_attn_Wq").T),
        "wkT": np.ascontiguousarray(g("L0_attn_Wk").T),
        "wvT": np.ascontiguousarray(g("L0_attn_Wv").T),
        "woT": np.ascontiguousarray(g("L0_attn_Wo").T),
        "w1": np.ascontiguousarray(g("L0_ffn_cn_w1")),
        "w2": np.ascontiguousarray(g("L0_ffn_cn_w2")),
        "w3": np.ascontiguousarray(g("L0_ffn_cn_w3")),
        "gw1": np.ascontiguousarray(g("L0_ffn_gc_w1")),
        "gw2": np.ascontiguousarray(g("L0_ffn_gc_w2")),
        "ident": np.eye(128, dtype=f32),
        "cnb1": np.ascontiguousarray(g("L0_ffn_cn_b1").reshape(8, 128).T),
        "cnb2": np.ascontiguousarray(g("L0_ffn_cn_b2").reshape(4, 128).T),
        "cnb3": np.ascontiguousarray(g("L0_ffn_cn_b3").reshape(2, 128).T),
        "gb1": np.ascontiguousarray(g("L0_ffn_gc_b1").reshape(4, 128).T),
        "gb2": np.ascontiguousarray(g("L0_ffn_gc_b2").reshape(2, 128).T),
        "sc3": np.stack([g("L0_ffn_cn_b4").reshape(()),
                         g("L0_ffn_gc_b3").reshape(()),
                         g("L0_ffn_adaptive_range").reshape(())]).reshape(1, 3)
        .astype(f32),
        "w4gc": np.concatenate([
            np.ascontiguousarray(g("L0_ffn_cn_w4").reshape(2, 128).T),
            np.ascontiguousarray(g("L0_ffn_gc_w3").reshape(2, 128).T)], axis=1)
        .astype(f32),
    }
    return shared


def kernel(**inputs):
    x = np.asarray(inputs["L0_attn_x"], dtype=np.float32)
    shared = _prep_shared(inputs)
    in_maps = []
    for i in range(8):
        m = dict(shared)
        m["x"] = np.ascontiguousarray(x[i])
        in_maps.append(m)
    nc = _get_nc()
    res = run_bass_kernel_spmd(nc, in_maps, core_ids=list(range(8)))
    return np.stack([np.asarray(res.results[i]["out"]) for i in range(8)], axis=0)


# version 14
# speedup vs baseline: 1.0076x; 1.0012x over previous
import numpy as np
import concourse.bass as bass
import concourse.mybir as mybir
from concourse import library_config
from concourse.bass_utils import run_bass_kernel_spmd
from contextlib import ExitStack

F32 = mybir.dt.float32
F16 = mybir.dt.float16
AF = mybir.ActivationFunctionType
ALU = mybir.AluOpType
AX = mybir.AxisListType

S = 2048
D = 1024
NCH = 8
NTT = 16
NKT = 16
QT = 512
NQT = 4
FQT = 256
NFQT = S // FQT
LN_EPS = 1e-5
ATT_EPS = 1e-6


class Sem:
    __slots__ = ("h", "n")

    def __init__(self, h):
        self.h = h
        self.n = 0


class Inst:
    __slots__ = ("raw",)

    def __init__(self, raw):
        self.raw = raw

    def inc(self, sem, k=1):
        sem.n += k
        if self.raw is not None:
            self.raw.then_inc(sem.h, k)
        return self


class Eng:

    def __init__(self, real):
        self.real = real

    def wait(self, sem, n=None):
        v = sem.n if n is None else n
        if self.real is not None and v > 0:
            self.real.wait_ge(sem.h, v)

    def waits(self, marks):
        for (sem, n) in marks:
            self.wait(sem, n)

    def __getattr__(self, op):
        real = self.real

        def call(*a, **k):
            return Inst(getattr(real, op)(*a, **k) if real is not None else None)

        return call


def build_nc(taps=()):
    taps = tuple(t for t in taps)
    nc = bass.Bass("TRN2", target_bir_lowering=False)

    x_h = nc.dram_tensor("x", [S, D], F32, kind="ExternalInput")
    wqT = nc.dram_tensor("wqT", [D, D], F32, kind="ExternalInput")
    wkT = nc.dram_tensor("wkT", [D, D], F32, kind="ExternalInput")
    wvT = nc.dram_tensor("wvT", [D, D], F32, kind="ExternalInput")
    woT = nc.dram_tensor("woT", [D, D], F32, kind="ExternalInput")
    w1 = nc.dram_tensor("w1", [D, D], F32, kind="ExternalInput")
    w2 = nc.dram_tensor("w2", [D, 512], F32, kind="ExternalInput")
    w3 = nc.dram_tensor("w3", [512, 256], F32, kind="ExternalInput")
    gw1 = nc.dram_tensor("gw1", [D, 512], F32, kind="ExternalInput")
    gw2 = nc.dram_tensor("gw2", [512, 256], F32, kind="ExternalInput")
    ident_s = nc.dram_tensor("ident", [128, 128], F32, kind="ExternalInput")
    cnb1_s = nc.dram_tensor("cnb1", [128, 8], F32, kind="ExternalInput")
    cnb2_s = nc.dram_tensor("cnb2", [128, 4], F32, kind="ExternalInput")
    cnb3_s = nc.dram_tensor("cnb3", [128, 2], F32, kind="ExternalInput")
    gb1_s = nc.dram_tensor("gb1", [128, 4], F32, kind="ExternalInput")
    gb2_s = nc.dram_tensor("gb2", [128, 2], F32, kind="ExternalInput")
    sc3_s = nc.dram_tensor("sc3", [1, 3], F32, kind="ExternalInput")
    w4gc_s = nc.dram_tensor("w4gc", [128, 4], F32, kind="ExternalInput")
    out_h = nc.dram_tensor("out", [S, D], F32, kind="ExternalOutput")
    tap_shapes = {"xnraw": [128, 16384], "attn": [128, 16384], "acc": [128, 16384],
                  "gcin": [128, 8], "gf": [1, 8], "stream": [128, 16384]}
    tap_h = {t: nc.dram_tensor("tap_" + t, tap_shapes[t], F32, kind="ExternalOutput").ap()
             for t in taps}

    with ExitStack() as ctx:
        sbt = lambda name, shape, dt=F32: ctx.enter_context(nc.sbuf_tensor(name, shape, dt))
        A = sbt("slotA", [128, 16384])
        B = sbt("slotB", [128, 16384])
        C = sbt("slotC", [128, 16384])
        A16 = A.bitcast(F16)
        B16 = B.bitcast(F16)
        C16 = C.bitcast(F16)
        wblk0 = sbt("wblk0", [128, 1024])
        wblk1 = sbt("wblk1", [128, 1024])
        ident = sbt("identb", [128, 128])
        cnb1 = sbt("cnb1b", [128, 8])
        cnb2 = sbt("cnb2b", [128, 4])
        cnb3 = sbt("cnb3b", [128, 2])
        gb1 = sbt("gb1b", [128, 4])
        gb2 = sbt("gb2b", [128, 2])
        sc3 = sbt("sc3b", [1, 3])
        w4gc = sbt("w4gcb", [128, 4])
        ones64 = sbt("ones64", [1, 64])
        ones128 = sbt("ones128", [1, 128])
        gcp = sbt("gcp", [128, 128])
        gcin = sbt("gcin", [128, 8])
        gch1 = sbt("gch1", [128, 4])
        gch2 = sbt("gch2", [128, 2])
        gfb = sbt("gfb", [1, 8])
        tf_sb = sbt("tf_sb", [1, FQT])
        cp_sb = sbt("cp_sb", [1, FQT])
        g3b = sbt("g3b", [128, 512])
        psum = [ctx.enter_context(nc.psum_tensor(f"p{i}", [128, 512], F32))
                for i in range(8)]
        sems = {nm: Sem(ctx.enter_context(nc.semaphore(nm)))
                for nm in ["sc_", "sx", "swb0", "swb1", "sf", "sxr0", "sxr1",
                           "st0", "st1", "sT", "sV", "sS", "sG"]}
        tap_sems = {t: Sem(ctx.enter_context(nc.semaphore("stp_" + t)))
                    for t in taps}
        block = ctx.enter_context(nc.Block())

        xn_hi = A16[:, 0:16384]
        xn_lo = A16[:, 16384:32768]
        w1hi = A16[:, 0:8192]
        w1lo = A16[:, 8192:16384]
        w2res = A[:, 8192:12288]
        s16h = A16[:, 24576:26624]
        s16l = A16[:, 26624:28672]
        g1 = A[:, 14336:16384]
        xre = [A[:, 0:1024], A[:, 1024:2048]]
        outt = [A[:, 2048:3072], A[:, 3072:4096]]
        ln_dump = C[:, 0:1024]
        ln_xnt = [C[:, 1024:2048], C[:, 2048:3072]]
        st_ = C[:, 3072:3248]
        ln_s1, ln_s2, ln_mu, ln_var, ln_m2, ln_std, ln_r0, ln_t, ln_rstd, ln_nmr = (
            st_[:, i * 16:(i + 1) * 16] for i in range(10))
        k_stack = C16[:, 0:4096]
        q_stack = C16[:, 4096:8192]
        qrep_hi2 = [C16[:, 8192:8704], C16[:, 29760:30272]]
        qrep_lo2 = [C16[:, 8704:9216], C16[:, 30272:30784]]
        wq16 = (C16[:, 9216:10240], C16[:, 10240:11264])
        wk16 = (C16[:, 11264:12288], C16[:, 12288:13312])
        wv16 = (C16[:, 13312:14336], C16[:, 14336:15360])
        vaugP = C[:, 7680:9760]
        vfm = [C[:, 9760:10272], C[:, 10272:10784]]
        sig = [C[:, 10784:11296], C[:, 11296:11808], C[:, 15392:15904]]
        n_sb2 = [C[:, 11808:12320], C[:, 12832:13344]]
        recip = C[:, 12320:12832]

        def emit(which, eng):
            T = Eng(eng if which == "tensor" else None)
            V = Eng(eng if which == "vector" else None)
            Sc = Eng(eng if which == "scalar" else None)
            G = Eng(eng if which == "gpsimd" else None)
            Y = Eng(eng if which == "sync" else None)
            sc_, sx, sf = sems["sc_"], sems["sx"], sems["sf"]
            swb = [sems["swb0"], sems["swb1"]]
            sxr = [sems["sxr0"], sems["sxr1"]]
            stp = [sems["st0"], sems["st1"]]
            sT, sV, sS = sems["sT"], sems["sV"], sems["sS"]
            sG = sems["sG"]
            for s_ in list(sems.values()) + list(tap_sems.values()):
                s_.n = 0
            bank_free = {i: [] for i in range(8)}

            def bacq(i):
                T.waits(bank_free[i])
                return psum[i]

            Y.dma_start(B.ap().rearrange("p (t n) -> p t n", t=16),
                        bass.AP(x_h, 0, [[1024, 128], [128 * 1024, 16], [1, 1024]])
                        ).inc(sx, 16)
            for dst, src in [(ident, ident_s), (cnb1, cnb1_s), (cnb2, cnb2_s),
                             (cnb3, cnb3_s), (gb1, gb1_s), (gb2, gb2_s),
                             (sc3, sc3_s), (w4gc, w4gc_s)]:
                Y.dma_start(dst[:, :], src.ap()).inc(sc_, 16)
            consts_n = sc_.n
            V.memset(gfb[:, :], 0.0).inc(sV)
            V.memset(ones64[:, :], 1.0).inc(sV)
            V.memset(ones128[:, :], 1.0).inc(sV)
            V.memset(vaugP[:, :], 1.0).inc(sV)

            V.wait(sx, 16)
            Sc.wait(sx, 16)
            for t in range(NTT):
                xt = B[:, t * 1024:(t + 1) * 1024]
                V.reduce_sum(ln_s1[:, t:t + 1], xt, axis=AX.X).inc(sV)
                if t > 0:
                    Sc.wait(sS)
                Sc.activation(ln_dump[:, :], xt, AF.Square,
                              accum_out=ln_s2[:, t:t + 1]).inc(sS)
            V.wait(sS)
            V.wait(sV)
            V.tensor_scalar(ln_mu[:, :], ln_s1[:, :], 1.0 / D, None, ALU.mult).inc(sV)
            V.tensor_scalar(ln_var[:, :], ln_s2[:, :], 1.0 / D, None, ALU.mult).inc(sV)
            V.wait(sV)
            V.tensor_tensor(ln_m2[:, :], ln_mu[:, :], ln_mu[:, :], ALU.mult).inc(sV)
            V.wait(sV)
            V.tensor_tensor(ln_var[:, :], ln_var[:, :], ln_m2[:, :], ALU.subtract).inc(sV)
            V.wait(sV)
            V.tensor_scalar(ln_var[:, :], ln_var[:, :], LN_EPS, None, ALU.add).inc(sV)
            Sc.wait(sV)
            Sc.activation(ln_std[:, :], ln_var[:, :], AF.Sqrt).inc(sS)
            V.wait(sS)
            V.reciprocal(ln_r0[:, :], ln_std[:, :]).inc(sV)
            V.wait(sV)
            V.tensor_tensor(ln_t[:, :], ln_r0[:, :], ln_r0[:, :], ALU.mult).inc(sV)
            V.wait(sV)
            V.tensor_tensor(ln_t[:, :], ln_t[:, :], ln_var[:, :], ALU.mult).inc(sV)
            V.wait(sV)
            V.tensor_scalar(ln_t[:, :], ln_t[:, :], -0.5, 1.5, ALU.mult, ALU.add).inc(sV)
            V.wait(sV)
            V.tensor_tensor(ln_rstd[:, :], ln_r0[:, :], ln_t[:, :], ALU.mult).inc(sV)
            V.wait(sV)
            V.scalar_tensor_tensor(ln_nmr[:, :], ln_mu[:, :], -1.0, ln_rstd[:, :],
                                   ALU.mult, ALU.mult).inc(sV)
            ln_done = sV.n

            wblks = [wblk0, wblk1]
            wblk_free = [[], []]
            nblk = [0]

            def load_block(src_handle, col, nk, row_len):
                i = nblk[0] % 2
                buf = wblks[i]
                Y.waits(wblk_free[i])
                ap = bass.AP(src_handle, col * 128,
                             [[row_len, 128], [128 * row_len, nk], [1, 128]])
                Y.dma_start(buf.ap()[:, 0:nk * 128].rearrange("p (kc o) -> p kc o", kc=nk),
                            ap).inc(swb[i], 16)
                nblk[0] += 1
                return buf, i, (swb[i], swb[i].n)

            def split16(buf, i, wready, hi, lo, width=1024):
                V.waits([wready])
                V.tensor_copy(hi[:, 0:width], buf[:, 0:width]).inc(sV)
                V.wait(sV)
                V.tensor_tensor(lo[:, 0:width], buf[:, 0:width], hi[:, 0:width],
                                ALU.subtract).inc(sV)
                wblk_free[i] = [(sV, sV.n)]
                return sV.n

            state = {"g": 0, "pending": None, "qrep_free": [[], []],
                     "qrep_done": [0, 0], "qrep_next": None, "carry": None,
                     "tail": None, "sig_free": [[], [], []]}
            def load_split_pair(pair, guard_mark):
                bq, iq, rq = load_block(wqT, pair, 8, D)
                V.wait(sT, guard_mark)
                sqm = split16(bq, iq, rq, *wq16)
                bk, ik, rk = load_block(wkT, pair, 8, D)
                skm = split16(bk, ik, rk, *wk16)
                bv, iv, rv = load_block(wvT, pair, 8, D)
                svm = split16(bv, iv, rv, *wv16)
                return [sqm, skm, svm]

            state["splits"] = load_split_pair(0, 0)
            xnt_free = [[], []]
            xn_marks = []
            for t in range(NTT):
                xt = B[:, t * 1024:(t + 1) * 1024]
                xnt = ln_xnt[t % 2]
                Sc.wait(sV, ln_done)
                Sc.waits(xnt_free[t % 2])
                Sc.activation(xnt[:, :], xt, AF.Identity,
                              bias=ln_nmr[:, t:t + 1], scale=ln_rstd[:, t:t + 1]).inc(sS)
                xnt_ready = sS.n
                for c in range(NCH):
                    bi = 6 + (t * NCH + c) % 2
                    b = bacq(bi)
                    T.wait(sS, xnt_ready)
                    T.wait(sc_, consts_n)
                    T.transpose(b[:, 0:128], xnt[:, c * 128:(c + 1) * 128],
                                ident[:, :]).inc(sT)
                    hi_sl = xn_hi[:, c * S + t * 128: c * S + (t + 1) * 128]
                    lo_sl = xn_lo[:, c * S + t * 128: c * S + (t + 1) * 128]
                    Sc.wait(sT)
                    Sc.copy(hi_sl, b[:, 0:128]).inc(sS)
                    V.wait(sS)
                    V.tensor_tensor(lo_sl, b[:, 0:128], hi_sl, ALU.subtract).inc(sV)
                    V.reduce_sum(gcp[:, c * NTT + t: c * NTT + t + 1], b[:, 0:128],
                                 axis=AX.X).inc(sV)
                    bank_free[bi] = [(sV, sV.n)]
                xnt_free[t % 2] = [(sT, sT.n)]
                xn_marks.append((sV, sV.n))
            V.wait(sV)
            V.reduce_sum(gcin[:, 0:NCH],
                         gcp.ap().rearrange("p (c t) -> p c t", c=NCH),
                         axis=AX.X).inc(sV)
            V.wait(sV)
            V.tensor_scalar(gcin[:, 0:NCH], gcin[:, 0:NCH], 1.0 / S, None,
                            ALU.mult).inc(sV)
            gcin_done = sV.n
            tap_marks = {}
            if "xnraw" in taps:
                Y.wait(sV, gcin_done)
                Y.dma_start(tap_h["xnraw"], A[:, :]).inc(tap_sems["xnraw"], 16)
                tap_marks["xnraw"] = (tap_sems["xnraw"], 16)

            for oc in range(4):
                buf, ib, wready = load_block(gw1, oc, 8, 512)
                b = bacq(7)
                T.waits([wready])
                T.wait(sV, gcin_done)
                for kc in range(NCH):
                    inst = T.matmul(b[:, 0:1], buf[:, kc * 128:(kc + 1) * 128],
                                    gcin[:, kc:kc + 1], start=(kc == 0),
                                    stop=(kc == NCH - 1))
                    if kc == NCH - 1:
                        inst.inc(sT)
                Sc.wait(sT)
                Sc.activation(gch1[:, oc:oc + 1], b[:, 0:1], AF.Gelu,
                              bias=gb1[:, oc:oc + 1]).inc(sS)
                bank_free[7] = [(sS, sS.n)]
                wblk_free[ib] = [(sT, sT.n)]
            for oc in range(2):
                buf, ib, wready = load_block(gw2, oc, 4, 256)
                b = bacq(7)
                T.waits([wready])
                T.wait(sS)
                for kc in range(4):
                    inst = T.matmul(b[:, 0:1], buf[:, kc * 128:(kc + 1) * 128],
                                    gch1[:, kc:kc + 1], start=(kc == 0), stop=(kc == 3))
                    if kc == 3:
                        inst.inc(sT)
                Sc.wait(sT)
                Sc.activation(gch2[:, oc:oc + 1], b[:, 0:1], AF.Gelu,
                              bias=gb2[:, oc:oc + 1]).inc(sS)
                bank_free[7] = [(sS, sS.n)]
                wblk_free[ib] = [(sT, sT.n)]
            b = bacq(7)
            T.wait(sS)
            T.wait(sc_, consts_n)
            for kc in range(2):
                inst = T.matmul(b[0:1, 0:1], w4gc[:, 2 + kc:3 + kc],
                                gch2[:, kc:kc + 1], start=(kc == 0), stop=(kc == 1))
                if kc == 1:
                    inst.inc(sT)
            Sc.wait(sT)
            Sc.activation(gfb[0:1, 0:1], b[0:1, 0:1], AF.Sigmoid,
                          bias=sc3[0:1, 1:2]).inc(sS)
            bank_free[7] = [(sS, sS.n)]
            V.wait(sS)
            V.tensor_scalar(gfb[0:1, 1:2], sc3[0:1, 2:3], 0.6, None, ALU.mult).inc(sV)
            V.tensor_scalar(gfb[0:1, 3:4], sc3[0:1, 2:3], 1.4, None, ALU.mult).inc(sV)
            V.tensor_scalar(gfb[0:1, 4:5], sc3[0:1, 2:3], -1.0, 0.1, ALU.mult,
                            ALU.add).inc(sV)
            V.wait(sV)
            V.tensor_tensor(gfb[0:1, 5:6], gfb[0:1, 0:1], gfb[0:1, 3:4], ALU.mult).inc(sV)
            V.wait(sV)
            V.tensor_tensor(gfb[0:1, 2:3], gfb[0:1, 4:5], gfb[0:1, 5:6], ALU.add).inc(sV)
            kconst_done = sV.n
            if "gcin" in taps:
                Y.wait(sV, gcin_done)
                Y.dma_start(tap_h["gcin"], gcin[:, :]).inc(tap_sems["gcin"], 16)
            if "gf" in taps:
                Y.wait(sV, kconst_done)
                Y.dma_start(tap_h["gf"], gfb[:, :]).inc(tap_sems["gf"], 16)

            for pair in range(NCH):
                splits = state["splits"]
                w16 = [wq16, wk16, wv16]

                for proj in range(3):
                    if proj == 1 and state["pending"] is not None:
                        state["pending"]()
                        state["pending"] = None
                    whi, wlo = w16[proj]
                    for qt in range(NQT):
                        bi = [4, 5, 0, 1][(proj * NQT + qt) % 4]
                        b = bacq(bi)
                        T.wait(sV, splits[proj])
                        if pair == 0:
                            T.waits([xn_marks[4 * qt + 3]])
                        passes = []
                        for kc in range(NCH):
                            xh = xn_hi[:, kc * S + qt * QT: kc * S + qt * QT + QT]
                            xl = xn_lo[:, kc * S + qt * QT: kc * S + qt * QT + QT]
                            lh = whi[:, kc * 128:(kc + 1) * 128]
                            ll = wlo[:, kc * 128:(kc + 1) * 128]
                            passes += [(lh, xh), (lh, xl), (ll, xh)]
                        n = len(passes)
                        for ip, (lT, rhs) in enumerate(passes):
                            inst = T.matmul(b[:, :], lT, rhs, start=(ip == 0),
                                            stop=(ip == n - 1))
                            if ip == n - 1:
                                inst.inc(sT)
                        mmdone = sT.n
                        if proj < 2:
                            stk = q_stack if proj == 0 else k_stack
                            dA = stk[:, qt * QT: qt * QT + QT]
                            dB = stk[:, S + qt * QT: S + qt * QT + QT]
                            Sc.wait(sT, mmdone)
                            Sc.copy(dA[0:64, :], b[0:64, :]).inc(sS)
                            Sc.copy(dB[0:64, :], b[64:128, :]).inc(sS)
                            V.wait(sS)
                            V.tensor_tensor(dA[64:128, :], b[0:64, :], dA[0:64, :],
                                            ALU.subtract).inc(sV)
                            V.tensor_tensor(dB[64:128, :], b[64:128, :], dB[0:64, :],
                                            ALU.subtract).inc(sV)
                            bank_free[bi] = [(sV, sV.n)]
                        else:
                            vf = vfm[qt % 2]
                            V.wait(sT, mmdone)
                            V.tensor_copy(vf[:, :], b[:, :]).inc(sV)
                            bank_free[bi] = [(sV, sV.n)]
                            vf_ready = (sV, sV.n)
                            for j in range(4):
                                tau = qt * 4 + j
                                ti_ = 6 + tau % 2
                                tb = bacq(ti_)
                                T.waits([vf_ready])
                                T.transpose(tb[:, 0:128], vf[:, j * 128:(j + 1) * 128],
                                            ident[:, :]).inc(sT)
                                Sc.wait(sT)
                                Sc.copy(bass.AP(vaugP.tensor, vaugP.offset + tau * 130,
                                                [[16384, 128], [65, 2], [1, 64]]),
                                        tb[:, 0:128].rearrange("p (h d) -> p h d", h=2)
                                        ).inc(sS)
                                bank_free[ti_] = [(sS, sS.n)]
                att_S = sS.n
                state["proj_done_mark"] = sT.n

                def mk_qrep(hi_, qc, g):
                    par = g % 2

                    def build():
                        V.wait(sS, att_S)
                        V.wait(sV)
                        V.waits(state["qrep_free"][par])
                        so = hi_ * S
                        qh = q_stack[0:64, so + qc * QT: so + qc * QT + QT]
                        ql = q_stack[64:128, so + qc * QT: so + qc * QT + QT]
                        V.tensor_copy(qrep_hi2[par][0:64, :], qh).inc(sV)
                        V.tensor_copy(qrep_hi2[par][64:128, :], qh).inc(sV)
                        V.tensor_copy(qrep_lo2[par][0:64, :], ql).inc(sV)
                        V.tensor_copy(qrep_lo2[par][64:128, :], ql).inc(sV)
                        state["qrep_done"][par] = sV.n
                    return build

                for hi_ in range(2):
                    stk_off = hi_ * S
                    for qc in range(NQT):
                        g = state["g"]
                        state["g"] += 1
                        par = g % 2
                        qrep_hi, qrep_lo = qrep_hi2[par], qrep_lo2[par]
                        n_sb = n_sb2[par]
                        wvb = [2, 3, 6][g % 3]
                        if state["qrep_next"] is None:
                            mk_qrep(hi_, qc, g)()
                        if (hi_, qc) == (0, 1) and pair + 1 < NCH:
                            state["splits"] = load_split_pair(
                                pair + 1, state["proj_done_mark"])
                        if qc < NQT - 1:
                            state["qrep_next"] = mk_qrep(hi_, qc + 1, g + 1)
                        elif hi_ == 0:
                            state["qrep_next"] = mk_qrep(1, 0, g + 1)
                        else:
                            state["qrep_next"] = None
                        qrep_done = state["qrep_done"][par]
                        b2 = bacq(wvb)
                        sig_marks = {}
                        wv_marks = {}

                        def emit_scores(kt, qrep_done=qrep_done):
                            bsc = bacq(kt % 2)
                            lT = k_stack[:, stk_off + kt * 128: stk_off + (kt + 1) * 128]
                            T.wait(sV, qrep_done)
                            T.matmul(bsc[:, :], lT, qrep_hi[:, :], start=True, stop=False)
                            T.matmul(bsc[:, :], lT, qrep_lo[:, :], start=False,
                                     stop=True).inc(sT)
                            return sT.n

                        def emit_sig(kt, sc_mark, g=g):
                            bufi = (g * NKT + kt) % 3
                            sgb = sig[bufi]
                            Sc.wait(sT, sc_mark)
                            Sc.waits(state["sig_free"][bufi])
                            Sc.activation(sgb[:, :], psum[kt % 2][:, :], AF.Sigmoid,
                                          scale=0.125).inc(sS)
                            bank_free[kt % 2] = [(sS, sS.n)]
                            sig_marks[kt] = sS.n

                        def emit_wv(kt, g=g, b2=b2, hi_=hi_, sig_marks=sig_marks,
                                    wv_marks=wv_marks):
                            bufi = (g * NKT + kt) % 3
                            T.wait(sS, sig_marks[kt])
                            lhs = bass.AP(vaugP.tensor,
                                          vaugP.offset + kt * 130 + hi_ * 65,
                                          [[16384, 128], [1, 65]])
                            T.matmul(b2[0:65, :], lhs,
                                     sig[bufi][:, :], start=(kt == 0),
                                     stop=(kt == NKT - 1)).inc(sT)
                            wv_marks[kt] = sT.n
                            state["sig_free"][bufi] = [(sT, sT.n)]

                        def mk_tail(g=g, b2=b2, n_sb=n_sb, wvb=wvb, hi_=hi_, qc=qc,
                                    pair=pair, wv_marks=wv_marks):
                            dst = B[hi_ * 64: hi_ * 64 + 64,
                                    pair * S + qc * QT: pair * S + qc * QT + QT]

                            def tail_norm():
                                V.wait(sT, wv_marks[NKT - 1])
                                V.tensor_scalar(n_sb[0:1, 0:512], b2[64:65, :],
                                                ATT_EPS, None, ALU.add).inc(sV)
                                nx = sV.n

                                def pending():
                                    b3 = bacq(4)
                                    T.wait(sV, nx)
                                    T.matmul(b3[0:64, :], ones64[:, :],
                                             n_sb[0:1, 0:512], start=True,
                                             stop=True).inc(sT)
                                    V.wait(sT)
                                    V.wait(sV)
                                    V.reciprocal(recip[0:64, 0:512],
                                                 b3[0:64, :]).inc(sV)
                                    bank_free[4] = [(sV, sV.n)]
                                    V.wait(sV)
                                    V.tensor_tensor(dst, b2[0:64, :],
                                                    recip[0:64, 0:512],
                                                    ALU.mult).inc(sV)
                                    bank_free[wvb] = [(sV, sV.n)]
                                state["pending"] = pending
                            return tail_norm

                        carry = state["carry"]
                        for kt in range(NKT):
                            sc_mark = emit_scores(kt)
                            if kt <= 1 and carry is not None:
                                carry(kt)
                            emit_sig(kt, sc_mark)
                            if kt == 1 and state["tail"] is not None:
                                state["tail"]()
                                state["tail"] = None
                            if kt >= 2:
                                emit_wv(kt - 2)
                            if kt == 4 and state["qrep_next"] is not None:
                                state["qrep_next"]()
                            if kt == 6 and state["pending"] is not None:
                                state["pending"]()
                                state["pending"] = None
                        state["qrep_free"][par] = [(sT, sT.n)]
                        if (hi_, qc) == (1, NQT - 1):
                            emit_wv(NKT - 2)
                            emit_wv(NKT - 1)
                            if state["pending"] is not None:
                                state["pending"]()
                                state["pending"] = None
                            mk_tail()()
                            state["carry"] = None
                            state["tail"] = None
                        else:
                            def mk_carry(emit_wv=emit_wv):
                                def carry_fn(k):
                                    emit_wv(NKT - 2 + k)
                                return carry_fn
                            state["carry"] = mk_carry()
                            state["tail"] = mk_tail()
            if state["pending"] is not None:
                state["pending"]()
                state["pending"] = None
            attn_V = sV.n
            if "attn" in taps:
                Y.wait(sV, attn_V)
                Y.dma_start(tap_h["attn"], B[:, :]).inc(tap_sems["attn"], 16)
                tap_marks["attn"] = (tap_sems["attn"], 16)

            for oc in range(NCH):
                buf, ib, wready = load_block(woT, oc, 8, D)
                for qc in range(NQT):
                    bi = 4 + qc % 2
                    b = bacq(bi)
                    T.waits([wready])
                    T.wait(sV, attn_V)
                    for kc in range(NCH):
                        rhs = B[:, kc * S + qc * QT: kc * S + qc * QT + QT]
                        inst = T.matmul(b[:, :], buf[:, kc * 128:(kc + 1) * 128], rhs,
                                        start=(kc == 0), stop=(kc == NCH - 1))
                        if kc == NCH - 1:
                            inst.inc(sT)
                    Sc.wait(sT)
                    Sc.copy(C[:, oc * S + qc * QT: oc * S + qc * QT + QT],
                            b[:, :]).inc(sS)
                    bank_free[bi] = [(sS, sS.n)]
                wblk_free[ib] = [(sT, sT.n)]
            wo_T, wo_S = sT.n, sS.n
            if "acc" in taps:
                Y.wait(sS, wo_S)
                Y.dma_start(tap_h["acc"], C[:, :]).inc(tap_sems["acc"], 16)
                tap_marks["acc"] = (tap_sems["acc"], 16)

            V.wait(sT, wo_T)
            if "attn" in tap_marks:
                V.waits([tap_marks["attn"]])
            Bv = B.ap().rearrange("p (c t) -> p c t", c=NCH)
            xhv = xn_hi.rearrange("p (c t) -> p c t", c=NCH)
            xlv = xn_lo.rearrange("p (c t) -> p c t", c=NCH)
            V.tensor_tensor(Bv[:, :, 0:FQT], xhv[:, :, 0:FQT], xlv[:, :, 0:FQT],
                            ALU.add).inc(sV)
            sinit0 = (sV, sV.n)
            G.wait(sT, wo_T)
            if "attn" in tap_marks:
                G.waits([tap_marks["attn"]])
            sinit = []
            for c in range(4):
                E_ = V if c < 2 else G
                sm = sV if c < 2 else sG
                for cc in (c, c + 4):
                    E_.tensor_tensor(B[:, cc * S + FQT:(cc + 1) * S],
                                     xn_hi[:, cc * S + FQT:(cc + 1) * S],
                                     xn_lo[:, cc * S + FQT:(cc + 1) * S],
                                     ALU.add).inc(sm)
                sinit.append((sm, sm.n))
            sinit_g = sG.n
            stream_ready = sV.n
            if "xnraw" in tap_marks:
                V.waits([tap_marks["xnraw"]])
            w1_marks = []
            for oc in range(NCH):
                buf, ib, wready = load_block(w1, oc, 8, D)
                E_ = V if oc < 4 else G
                sm = sV if oc < 4 else sG
                E_.waits([sinit0, sinit[oc // 2]])
                E_.wait(sG, sinit_g)
                E_.waits([wready])
                hi = w1hi.rearrange("p (oc w) -> p oc w", oc=NCH)[:, oc, :]
                lo = w1lo.rearrange("p (oc w) -> p oc w", oc=NCH)[:, oc, :]
                E_.tensor_copy(hi, buf[:, 0:1024]).inc(sm)
                E_.wait(sm)
                E_.tensor_tensor(lo, buf[:, 0:1024], hi, ALU.subtract).inc(sm)
                wblk_free[ib] = [(sm, sm.n)]
                w1_marks.append((sm, sm.n))
            for oc in range(4):
                Y.waits([sinit0, sinit[oc]])
                ap = bass.AP(w2, oc * 128, [[512, 128], [128 * 512, 8], [1, 128]])
                Y.dma_start(w2res.rearrange("p (oc w) -> p oc w", oc=4)[:, oc, :]
                            .rearrange("p (kc o) -> p kc o", kc=8), ap).inc(sf, 16)
            for oc in range(2):
                Y.waits(wblk_free[0])
                Y.waits(wblk_free[1])
                ap = bass.AP(w3, oc * 128, [[256, 128], [128 * 256, 4], [1, 128]])
                Y.dma_start(wblk0.ap()[:, oc * 512:(oc + 1) * 512]
                            .rearrange("p (kc o) -> p kc o", kc=4), ap).inc(sf, 16)
            ffn_w_ready = sf.n
            w1_ready = sV.n

            strv = Bv
            Cv = C.ap().rearrange("p (c t) -> p c t", c=NCH)
            acc_marks = [0] * NTT

            def emit_s16(cy, ti, first=False):
                t0 = ti * FQT
                if first:
                    V.waits([sinit0])
                else:
                    V.wait(sT, state["cn1_done"])
                    V.wait(sV)
                    V.wait(sG, sinit_g)
                V.tensor_copy(s16h.rearrange("p (c t) -> p c t", c=NCH),
                              strv[:, :, t0:t0 + FQT]).inc(sV)
                V.wait(sV)
                V.tensor_tensor(s16l.rearrange("p (c t) -> p c t", c=NCH),
                                strv[:, :, t0:t0 + FQT],
                                s16h.rearrange("p (c t) -> p c t", c=NCH),
                                ALU.subtract).inc(sV)
                state["s16_ready"] = sV.n

            def emit_cn1(cy, ti):
                for oc in range(NCH):
                    bi = 4 + oc % 2
                    b = psum[bi]
                    T.waits(bank_free[bi])
                    T.waits([w1_marks[oc]])
                    T.wait(sV, state["s16_ready"])
                    passes = []
                    for kc in range(NCH):
                        lh = w1hi[:, oc * 1024 + kc * 128: oc * 1024 + (kc + 1) * 128]
                        ll = w1lo[:, oc * 1024 + kc * 128: oc * 1024 + (kc + 1) * 128]
                        rh = s16h[:, kc * FQT:(kc + 1) * FQT]
                        rl = s16l[:, kc * FQT:(kc + 1) * FQT]
                        passes += [(lh, rh), (lh, rl), (ll, rh)]
                    n = len(passes)
                    for ip, (lT, rhs) in enumerate(passes):
                        inst = T.matmul(b[:, 0:FQT], lT, rhs, start=(ip == 0),
                                        stop=(ip == n - 1))
                        if ip == n - 1:
                            inst.inc(sT)
                    Sc.wait(sT)
                    Sc.wait(sc_, consts_n)
                    Sc.wait(sG, sinit_g)
                    Sc.wait(sV, stream_ready)
                    Sc.activation(g1[:, oc * FQT:(oc + 1) * FQT], b[:, 0:FQT],
                                  AF.Gelu, bias=cnb1[:, oc:oc + 1]).inc(sS)
                    bank_free[bi] = [(sS, sS.n)]
                state["g1_ready"] = sS.n
                state["cn1_done"] = sT.n

            state["g2_guard"] = list(wblk_free[1])
            emit_s16(0, 0, first=True)
            emit_cn1(0, 0)
            for cy in range(3):
                for ti in range(NFQT):
                    t0 = ti * FQT
                    g1_ready = state["g1_ready"]
                    if state.get("ffn_fin") is not None:
                        state["ffn_fin"]()
                        state["ffn_fin"] = None

                    for oc in range(4):
                        bi = 4 + oc % 2
                        b = bacq(bi)
                        T.wait(sf, ffn_w_ready)
                        T.wait(sS, g1_ready)
                        for kc in range(NCH):
                            inst = T.matmul(b[:, 0:FQT],
                                            w2res[:, oc * 1024 + kc * 128:
                                                  oc * 1024 + (kc + 1) * 128],
                                            g1[:, kc * FQT:(kc + 1) * FQT],
                                            start=(kc == 0), stop=(kc == NCH - 1))
                            if kc == NCH - 1:
                                inst.inc(sT)
                        Sc.wait(sT)
                        Sc.waits(state.get("g2_guard", []))
                        Sc.activation(wblk1[:, oc * FQT:(oc + 1) * FQT], b[:, 0:FQT],
                                      AF.Gelu, bias=cnb2[:, oc:oc + 1]).inc(sS)
                        bank_free[bi] = [(sS, sS.n)]
                    g2_ready = sS.n

                    for oc in range(2):
                        bi = 4 + oc % 2
                        b = bacq(bi)
                        T.wait(sS, g2_ready)
                        for kc in range(4):
                            inst = T.matmul(b[:, 0:FQT],
                                            wblk0[:, oc * 512 + kc * 128:
                                                  oc * 512 + (kc + 1) * 128],
                                            wblk1[:, kc * FQT:(kc + 1) * FQT],
                                            start=(kc == 0), stop=(kc == 3))
                            if kc == 3:
                                inst.inc(sT)
                        Sc.wait(sT)
                        Sc.activation(g3b[:, oc * FQT:(oc + 1) * FQT], b[:, 0:FQT],
                                      AF.Gelu, bias=cnb3[:, oc:oc + 1]).inc(sS)
                        bank_free[bi] = [(sS, sS.n)]
                    g3_ready = sS.n

                    nxt = (cy, ti + 1) if ti + 1 < NFQT else                           ((cy + 1, 0) if cy < 2 else None)
                    if nxt is not None:
                        emit_s16(*nxt)
                        emit_cn1(*nxt)

                    b7 = bacq(7)
                    T.wait(sS, g3_ready)
                    T.wait(sc_, consts_n)
                    for kc in range(2):
                        inst = T.matmul(b7[0:1, 0:FQT], w4gc[:, kc:kc + 1],
                                        g3b[:, kc * FQT:(kc + 1) * FQT],
                                        start=(kc == 0), stop=(kc == 1))
                        if kc == 1:
                            inst.inc(sT)
                    Sc.wait(sT)
                    Sc.wait(sV, kconst_done)
                    Sc.wait(sV)
                    Sc.activation(tf_sb[0:1, 0:FQT], b7[0:1, 0:FQT], AF.Sigmoid,
                                  bias=sc3[0:1, 0:1]).inc(sS)
                    bank_free[7] = [(sS, sS.n)]
                    V.wait(sS)
                    V.wait(sT)
                    V.tensor_scalar(cp_sb[0:1, 0:FQT], tf_sb[0:1, 0:FQT],
                                    gfb[0:1, 1:2], gfb[0:1, 2:3],
                                    ALU.mult, ALU.add).inc(sV)
                    cp_ready = sV.n

                    def mk_fin(cy=cy, ti=ti, t0=t0, cp_ready=cp_ready):
                        def fin():
                            b6 = bacq(6)
                            T.wait(sV, cp_ready)
                            T.matmul(b6[:, 0:FQT], ones128[:, :], cp_sb[0:1, 0:FQT],
                                     start=True, stop=True).inc(sT)
                            cp_bcast = sT.n
                            for (a, bb, sa, sb_) in [(0, 4, 1.0, -1.0),
                                                     (1, 5, 1.0, -1.0),
                                                     (2, 6, -1.0, 1.0),
                                                     (3, 7, -1.0, 1.0)]:
                                olda = strv[:, a, t0:t0 + FQT]
                                oldb = strv[:, bb, t0:t0 + FQT]
                                V.waits(bank_free[0])
                                V.waits(bank_free[1])
                                V.wait(sT, cp_bcast)
                                V.scalar_tensor_tensor(psum[0][:, 0:FQT], oldb, sa,
                                                       b6[:, 0:FQT], ALU.mult,
                                                       ALU.mult).inc(sV)
                                V.scalar_tensor_tensor(psum[1][:, 0:FQT], olda, sb_,
                                                       b6[:, 0:FQT], ALU.mult,
                                                       ALU.mult).inc(sV)
                                V.wait(sV)
                                V.tensor_tensor(olda, olda, psum[0][:, 0:FQT],
                                                ALU.add).inc(sV)
                                V.tensor_tensor(oldb, oldb, psum[1][:, 0:FQT],
                                                ALU.add).inc(sV)
                                bank_free[0] = [(sV, sV.n)]
                                bank_free[1] = [(sV, sV.n)]
                            bank_free[6] = [(sV, sV.n)]
                            if cy == 2:
                                V.wait(sS, wo_S)
                                if "acc" in tap_marks:
                                    V.waits([tap_marks["acc"]])
                                for half in range(2):
                                    t = 2 * ti + half
                                    V.wait(sV)
                                    V.tensor_tensor(Cv[:, :, t * 128:(t + 1) * 128],
                                                    Cv[:, :, t * 128:(t + 1) * 128],
                                                    Bv[:, :, t * 128:(t + 1) * 128],
                                                    ALU.add).inc(sV)
                                    acc_marks[t] = sV.n
                        return fin

                    state["ffn_fin"] = mk_fin()
            if state.get("ffn_fin") is not None:
                state["ffn_fin"]()
                state["ffn_fin"] = None
            ffn_done = sV.n
            if "stream" in taps:
                Y.wait(sV, ffn_done)
                Y.dma_start(tap_h["stream"], B[:, :]).inc(tap_sems["stream"], 16)
                tap_marks["stream"] = (tap_sems["stream"], 16)

            for t in range(NTT):
                xre_t = B.ap().rearrange("p (c t) -> p c t", c=NCH)[:, :, t * 128:(t + 1) * 128]
                xsrc = bass.AP(x_h, t * 128 * 1024,
                               [[1024, 128], [128, 8], [1, 128]])
                Sc.wait(sV, acc_marks[t])
                if "stream" in tap_marks:
                    Sc.waits([tap_marks["stream"]])
                Sc.waits(state.get(("xre_prev", t % 2), []))
                Sc.dma_start(xre_t, xsrc).inc(sxr[t % 2], 16)
                xmark = (sxr[t % 2], sxr[t % 2].n)
                state[("xre_prev", t % 2)] = [xmark]
                for c in range(NCH):
                    bi = 2 + (t * NCH + c) % 2
                    b = bacq(bi)
                    T.wait(sV, acc_marks[t])
                    T.transpose(b[:, 0:128],
                                C[:, c * S + t * 128: c * S + (t + 1) * 128],
                                ident[:, :]).inc(sT)
                    V.wait(sT)
                    V.waits([xmark])
                    V.tensor_tensor(xre_t[:, c, :], b[:, 0:128], xre_t[:, c, :],
                                    ALU.add).inc(sV)
                    bank_free[bi] = [(sV, sV.n)]
                Y.wait(sV)
                Y.dma_start(bass.AP(out_h, t * 128 * 1024,
                                    [[1024, 128], [128, 8], [1, 128]]),
                            xre_t).inc(stp[t % 2], 16)
            Y.wait(stp[0], stp[0].n)
            Y.wait(stp[1], stp[1].n)
            for ts_ in tap_sems.values():
                Y.wait(ts_, ts_.n)
            return

        @block.sync
        def _(eng):
            emit("sync", eng)

        @block.tensor
        def _(eng):
            emit("tensor", eng)

        @block.vector
        def _(eng):
            emit("vector", eng)

        @block.scalar
        def _(eng):
            emit("scalar", eng)

        @block.gpsimd
        def _(eng):
            emit("gpsimd", eng)

    return nc


_NC_CACHE = {}


def _get_nc(taps=()):
    key = tuple(sorted(taps))
    if key not in _NC_CACHE:
        _NC_CACHE[key] = build_nc(taps)
    return _NC_CACHE[key]


def _prep_shared(inputs):
    f32 = np.float32
    g = lambda k: np.asarray(inputs[k], dtype=f32)
    shared = {
        "wqT": np.ascontiguousarray(g("L0_attn_Wq").T),
        "wkT": np.ascontiguousarray(g("L0_attn_Wk").T),
        "wvT": np.ascontiguousarray(g("L0_attn_Wv").T),
        "woT": np.ascontiguousarray(g("L0_attn_Wo").T),
        "w1": np.ascontiguousarray(g("L0_ffn_cn_w1")),
        "w2": np.ascontiguousarray(g("L0_ffn_cn_w2")),
        "w3": np.ascontiguousarray(g("L0_ffn_cn_w3")),
        "gw1": np.ascontiguousarray(g("L0_ffn_gc_w1")),
        "gw2": np.ascontiguousarray(g("L0_ffn_gc_w2")),
        "ident": np.eye(128, dtype=f32),
        "cnb1": np.ascontiguousarray(g("L0_ffn_cn_b1").reshape(8, 128).T),
        "cnb2": np.ascontiguousarray(g("L0_ffn_cn_b2").reshape(4, 128).T),
        "cnb3": np.ascontiguousarray(g("L0_ffn_cn_b3").reshape(2, 128).T),
        "gb1": np.ascontiguousarray(g("L0_ffn_gc_b1").reshape(4, 128).T),
        "gb2": np.ascontiguousarray(g("L0_ffn_gc_b2").reshape(2, 128).T),
        "sc3": np.stack([g("L0_ffn_cn_b4").reshape(()),
                         g("L0_ffn_gc_b3").reshape(()),
                         g("L0_ffn_adaptive_range").reshape(())]).reshape(1, 3)
        .astype(f32),
        "w4gc": np.concatenate([
            np.ascontiguousarray(g("L0_ffn_cn_w4").reshape(2, 128).T),
            np.ascontiguousarray(g("L0_ffn_gc_w3").reshape(2, 128).T)], axis=1)
        .astype(f32),
    }
    return shared


def kernel(**inputs):
    x = np.asarray(inputs["L0_attn_x"], dtype=np.float32)
    shared = _prep_shared(inputs)
    in_maps = []
    for i in range(8):
        m = dict(shared)
        m["x"] = np.ascontiguousarray(x[i])
        in_maps.append(m)
    nc = _get_nc()
    res = run_bass_kernel_spmd(nc, in_maps, core_ids=list(range(8)))
    return np.stack([np.asarray(res.results[i]["out"]) for i in range(8)], axis=0)


# version 15
# speedup vs baseline: 1.0199x; 1.0122x over previous
import numpy as np
import concourse.bass as bass
import concourse.mybir as mybir
from concourse import library_config
from concourse.bass_utils import run_bass_kernel_spmd
from contextlib import ExitStack

F32 = mybir.dt.float32
F16 = mybir.dt.float16
AF = mybir.ActivationFunctionType
ALU = mybir.AluOpType
AX = mybir.AxisListType

S = 2048
D = 1024
NCH = 8
NTT = 16
NKT = 16
QT = 512
NQT = 4
FQT = 256
NFQT = S // FQT
LN_EPS = 1e-5
ATT_EPS = 1e-6


class Sem:
    __slots__ = ("h", "n")

    def __init__(self, h):
        self.h = h
        self.n = 0


class Inst:
    __slots__ = ("raw",)

    def __init__(self, raw):
        self.raw = raw

    def inc(self, sem, k=1):
        sem.n += k
        if self.raw is not None:
            self.raw.then_inc(sem.h, k)
        return self


class Eng:

    def __init__(self, real):
        self.real = real

    def wait(self, sem, n=None):
        v = sem.n if n is None else n
        if self.real is not None and v > 0:
            self.real.wait_ge(sem.h, v)

    def waits(self, marks):
        for (sem, n) in marks:
            self.wait(sem, n)

    def __getattr__(self, op):
        real = self.real

        def call(*a, **k):
            return Inst(getattr(real, op)(*a, **k) if real is not None else None)

        return call


def build_nc(taps=()):
    taps = tuple(t for t in taps)
    nc = bass.Bass("TRN2", target_bir_lowering=False)

    x_h = nc.dram_tensor("x", [S, D], F32, kind="ExternalInput")
    wqT = nc.dram_tensor("wqT", [D, D], F32, kind="ExternalInput")
    wkT = nc.dram_tensor("wkT", [D, D], F32, kind="ExternalInput")
    wvT = nc.dram_tensor("wvT", [D, D], F32, kind="ExternalInput")
    woT = nc.dram_tensor("woT", [D, D], F32, kind="ExternalInput")
    w1 = nc.dram_tensor("w1", [D, D], F32, kind="ExternalInput")
    w2 = nc.dram_tensor("w2", [D, 512], F32, kind="ExternalInput")
    w3 = nc.dram_tensor("w3", [512, 256], F32, kind="ExternalInput")
    gw1 = nc.dram_tensor("gw1", [D, 512], F32, kind="ExternalInput")
    gw2 = nc.dram_tensor("gw2", [512, 256], F32, kind="ExternalInput")
    ident_s = nc.dram_tensor("ident", [128, 128], F32, kind="ExternalInput")
    cnb1_s = nc.dram_tensor("cnb1", [128, 8], F32, kind="ExternalInput")
    cnb2_s = nc.dram_tensor("cnb2", [128, 4], F32, kind="ExternalInput")
    cnb3_s = nc.dram_tensor("cnb3", [128, 2], F32, kind="ExternalInput")
    gb1_s = nc.dram_tensor("gb1", [128, 4], F32, kind="ExternalInput")
    gb2_s = nc.dram_tensor("gb2", [128, 2], F32, kind="ExternalInput")
    sc3_s = nc.dram_tensor("sc3", [1, 3], F32, kind="ExternalInput")
    w4gc_s = nc.dram_tensor("w4gc", [128, 4], F32, kind="ExternalInput")
    out_h = nc.dram_tensor("out", [S, D], F32, kind="ExternalOutput")
    tap_shapes = {"xnraw": [128, 16384], "attn": [128, 16384], "acc": [128, 16384],
                  "gcin": [128, 8], "gf": [1, 8], "stream": [128, 16384]}
    tap_h = {t: nc.dram_tensor("tap_" + t, tap_shapes[t], F32, kind="ExternalOutput").ap()
             for t in taps}

    with ExitStack() as ctx:
        sbt = lambda name, shape, dt=F32: ctx.enter_context(nc.sbuf_tensor(name, shape, dt))
        A = sbt("slotA", [128, 16384])
        B = sbt("slotB", [128, 16384])
        C = sbt("slotC", [128, 16384])
        A16 = A.bitcast(F16)
        B16 = B.bitcast(F16)
        C16 = C.bitcast(F16)
        wblk0 = sbt("wblk0", [128, 1024])
        wblk1 = sbt("wblk1", [128, 1024])
        ident = sbt("identb", [128, 128])
        cnb1 = sbt("cnb1b", [128, 8])
        cnb2 = sbt("cnb2b", [128, 4])
        cnb3 = sbt("cnb3b", [128, 2])
        gb1 = sbt("gb1b", [128, 4])
        gb2 = sbt("gb2b", [128, 2])
        sc3 = sbt("sc3b", [1, 3])
        w4gc = sbt("w4gcb", [128, 4])
        ones64 = sbt("ones64", [1, 64])
        ones128 = sbt("ones128", [1, 128])
        gcp = sbt("gcp", [128, 128])
        gcin = sbt("gcin", [128, 8])
        gch1 = sbt("gch1", [128, 4])
        gch2 = sbt("gch2", [128, 2])
        gfb = sbt("gfb", [1, 8])
        tf_sb = sbt("tf_sb", [1, FQT])
        cp_sb = sbt("cp_sb", [1, FQT])
        g3b = sbt("g3b", [128, 512])
        psum = [ctx.enter_context(nc.psum_tensor(f"p{i}", [128, 512], F32))
                for i in range(8)]
        sems = {nm: Sem(ctx.enter_context(nc.semaphore(nm)))
                for nm in ["sc_", "sx", "swb0", "swb1", "sf", "sxr0", "sxr1",
                           "st0", "st1", "sT", "sV", "sS", "sG"]}
        tap_sems = {t: Sem(ctx.enter_context(nc.semaphore("stp_" + t)))
                    for t in taps}
        block = ctx.enter_context(nc.Block())

        xn_hi = A16[:, 0:16384]
        xn_lo = A16[:, 16384:32768]
        w1hi = A16[:, 0:8192]
        w1lo = A16[:, 8192:16384]
        w2res = A[:, 8192:12288]
        s16h = A16[:, 24576:26624]
        s16l = A16[:, 26624:28672]
        g1 = A[:, 14336:16384]
        xre = [A[:, 0:1024], A[:, 1024:2048]]
        outt = [A[:, 2048:3072], A[:, 3072:4096]]
        ln_dump = C[:, 0:1024]
        ln_xnt = [C[:, 1024:2048], C[:, 2048:3072]]
        st_ = C[:, 3072:3248]
        ln_s1, ln_s2, ln_mu, ln_var, ln_m2, ln_std, ln_r0, ln_t, ln_rstd, ln_nmr = (
            st_[:, i * 16:(i + 1) * 16] for i in range(10))
        k_stack = C16[:, 0:4096]
        q_stack = C16[:, 4096:8192]
        qrep_hi2 = [C16[:, 8192:8704], C16[:, 29760:30272]]
        qrep_lo2 = [C16[:, 8704:9216], C16[:, 30272:30784]]
        wq16 = (C16[:, 9216:10240], C16[:, 10240:11264])
        wk16 = (C16[:, 11264:12288], C16[:, 12288:13312])
        wv16 = (C16[:, 13312:14336], C16[:, 14336:15360])
        vaugP = C[:, 7680:9760]
        vfm = [C[:, 9760:10272], C[:, 10272:10784]]
        sig = [C[:, 10784:11296], C[:, 11296:11808], C[:, 15392:15904]]
        n_sb2 = [C[:, 11808:12320], C[:, 12832:13344]]
        recip = C[:, 12320:12832]

        def emit(which, eng):
            T = Eng(eng if which == "tensor" else None)
            V = Eng(eng if which == "vector" else None)
            Sc = Eng(eng if which == "scalar" else None)
            G = Eng(eng if which == "gpsimd" else None)
            Y = Eng(eng if which == "sync" else None)
            sc_, sx, sf = sems["sc_"], sems["sx"], sems["sf"]
            swb = [sems["swb0"], sems["swb1"]]
            sxr = [sems["sxr0"], sems["sxr1"]]
            stp = [sems["st0"], sems["st1"]]
            sT, sV, sS = sems["sT"], sems["sV"], sems["sS"]
            sG = sems["sG"]
            for s_ in list(sems.values()) + list(tap_sems.values()):
                s_.n = 0
            bank_free = {i: [] for i in range(8)}

            def bacq(i):
                T.waits(bank_free[i])
                return psum[i]

            Y.dma_start(B.ap().rearrange("p (t n) -> p t n", t=16),
                        bass.AP(x_h, 0, [[1024, 128], [128 * 1024, 16], [1, 1024]])
                        ).inc(sx, 16)
            for dst, src in [(ident, ident_s), (cnb1, cnb1_s), (cnb2, cnb2_s),
                             (cnb3, cnb3_s), (gb1, gb1_s), (gb2, gb2_s),
                             (sc3, sc3_s), (w4gc, w4gc_s)]:
                Y.dma_start(dst[:, :], src.ap()).inc(sc_, 16)
            consts_n = sc_.n
            V.memset(gfb[:, :], 0.0).inc(sV)
            V.memset(ones64[:, :], 1.0).inc(sV)
            V.memset(ones128[:, :], 1.0).inc(sV)
            V.memset(vaugP[:, :], 1.0).inc(sV)

            V.wait(sx, 16)
            Sc.wait(sx, 16)
            for t in range(NTT):
                xt = B[:, t * 1024:(t + 1) * 1024]
                V.reduce_sum(ln_s1[:, t:t + 1], xt, axis=AX.X).inc(sV)
                if t > 0:
                    Sc.wait(sS)
                Sc.activation(ln_dump[:, :], xt, AF.Square,
                              accum_out=ln_s2[:, t:t + 1]).inc(sS)
            V.wait(sS)
            V.wait(sV)
            V.tensor_scalar(ln_mu[:, :], ln_s1[:, :], 1.0 / D, None, ALU.mult).inc(sV)
            V.tensor_scalar(ln_var[:, :], ln_s2[:, :], 1.0 / D, None, ALU.mult).inc(sV)
            V.wait(sV)
            V.tensor_tensor(ln_m2[:, :], ln_mu[:, :], ln_mu[:, :], ALU.mult).inc(sV)
            V.wait(sV)
            V.tensor_tensor(ln_var[:, :], ln_var[:, :], ln_m2[:, :], ALU.subtract).inc(sV)
            V.wait(sV)
            V.tensor_scalar(ln_var[:, :], ln_var[:, :], LN_EPS, None, ALU.add).inc(sV)
            Sc.wait(sV)
            Sc.activation(ln_std[:, :], ln_var[:, :], AF.Sqrt).inc(sS)
            V.wait(sS)
            V.reciprocal(ln_r0[:, :], ln_std[:, :]).inc(sV)
            V.wait(sV)
            V.tensor_tensor(ln_t[:, :], ln_r0[:, :], ln_r0[:, :], ALU.mult).inc(sV)
            V.wait(sV)
            V.tensor_tensor(ln_t[:, :], ln_t[:, :], ln_var[:, :], ALU.mult).inc(sV)
            V.wait(sV)
            V.tensor_scalar(ln_t[:, :], ln_t[:, :], -0.5, 1.5, ALU.mult, ALU.add).inc(sV)
            V.wait(sV)
            V.tensor_tensor(ln_rstd[:, :], ln_r0[:, :], ln_t[:, :], ALU.mult).inc(sV)
            V.wait(sV)
            V.scalar_tensor_tensor(ln_nmr[:, :], ln_mu[:, :], -1.0, ln_rstd[:, :],
                                   ALU.mult, ALU.mult).inc(sV)
            ln_done = sV.n

            wblks = [wblk0, wblk1]
            wblk_free = [[], []]
            nblk = [0]

            def load_block(src_handle, col, nk, row_len):
                i = nblk[0] % 2
                buf = wblks[i]
                Y.waits(wblk_free[i])
                ap = bass.AP(src_handle, col * 128,
                             [[row_len, 128], [128 * row_len, nk], [1, 128]])
                Y.dma_start(buf.ap()[:, 0:nk * 128].rearrange("p (kc o) -> p kc o", kc=nk),
                            ap).inc(swb[i], 16)
                nblk[0] += 1
                return buf, i, (swb[i], swb[i].n)

            def split16(buf, i, wready, hi, lo, width=1024):
                V.waits([wready])
                V.tensor_copy(hi[:, 0:width], buf[:, 0:width]).inc(sV)
                V.wait(sV)
                V.tensor_tensor(lo[:, 0:width], buf[:, 0:width], hi[:, 0:width],
                                ALU.subtract).inc(sV)
                wblk_free[i] = [(sV, sV.n)]
                return sV.n

            state = {"g": 0, "pending": None, "qrep_free": [[], []],
                     "qrep_done": [0, 0], "qrep_next": None, "carry": None,
                     "tail": None, "sig_free": [[], [], []]}
            def load_split_pair(pair, guard_mark):
                bq, iq, rq = load_block(wqT, pair, 8, D)
                V.wait(sT, guard_mark)
                sqm = split16(bq, iq, rq, *wq16)
                bk, ik, rk = load_block(wkT, pair, 8, D)
                skm = split16(bk, ik, rk, *wk16)
                bv, iv, rv = load_block(wvT, pair, 8, D)
                svm = split16(bv, iv, rv, *wv16)
                return [sqm, skm, svm]

            state["splits"] = load_split_pair(0, 0)
            xnt_free = [[], []]
            xn_marks = []
            for t in range(NTT):
                xt = B[:, t * 1024:(t + 1) * 1024]
                xnt = ln_xnt[t % 2]
                Sc.wait(sV, ln_done)
                Sc.waits(xnt_free[t % 2])
                Sc.activation(xnt[:, :], xt, AF.Identity,
                              bias=ln_nmr[:, t:t + 1], scale=ln_rstd[:, t:t + 1]).inc(sS)
                xnt_ready = sS.n
                for c in range(NCH):
                    bi = 6 + (t * NCH + c) % 2
                    b = bacq(bi)
                    T.wait(sS, xnt_ready)
                    T.wait(sc_, consts_n)
                    T.transpose(b[:, 0:128], xnt[:, c * 128:(c + 1) * 128],
                                ident[:, :]).inc(sT)
                    hi_sl = xn_hi[:, c * S + t * 128: c * S + (t + 1) * 128]
                    lo_sl = xn_lo[:, c * S + t * 128: c * S + (t + 1) * 128]
                    Sc.wait(sT)
                    Sc.copy(hi_sl, b[:, 0:128]).inc(sS)
                    V.wait(sS)
                    V.tensor_tensor(lo_sl, b[:, 0:128], hi_sl, ALU.subtract).inc(sV)
                    V.reduce_sum(gcp[:, c * NTT + t: c * NTT + t + 1], b[:, 0:128],
                                 axis=AX.X).inc(sV)
                    bank_free[bi] = [(sV, sV.n)]
                xnt_free[t % 2] = [(sT, sT.n)]
                xn_marks.append((sV, sV.n))
            V.wait(sV)
            V.reduce_sum(gcin[:, 0:NCH],
                         gcp.ap().rearrange("p (c t) -> p c t", c=NCH),
                         axis=AX.X).inc(sV)
            V.wait(sV)
            V.tensor_scalar(gcin[:, 0:NCH], gcin[:, 0:NCH], 1.0 / S, None,
                            ALU.mult).inc(sV)
            gcin_done = sV.n
            tap_marks = {}
            if "xnraw" in taps:
                Y.wait(sV, gcin_done)
                Y.dma_start(tap_h["xnraw"], A[:, :]).inc(tap_sems["xnraw"], 16)
                tap_marks["xnraw"] = (tap_sems["xnraw"], 16)

            for oc in range(4):
                buf, ib, wready = load_block(gw1, oc, 8, 512)
                b = bacq(7)
                T.waits([wready])
                T.wait(sV, gcin_done)
                for kc in range(NCH):
                    inst = T.matmul(b[:, 0:1], buf[:, kc * 128:(kc + 1) * 128],
                                    gcin[:, kc:kc + 1], start=(kc == 0),
                                    stop=(kc == NCH - 1))
                    if kc == NCH - 1:
                        inst.inc(sT)
                Sc.wait(sT)
                Sc.activation(gch1[:, oc:oc + 1], b[:, 0:1], AF.Gelu,
                              bias=gb1[:, oc:oc + 1]).inc(sS)
                bank_free[7] = [(sS, sS.n)]
                wblk_free[ib] = [(sT, sT.n)]
            for oc in range(2):
                buf, ib, wready = load_block(gw2, oc, 4, 256)
                b = bacq(7)
                T.waits([wready])
                T.wait(sS)
                for kc in range(4):
                    inst = T.matmul(b[:, 0:1], buf[:, kc * 128:(kc + 1) * 128],
                                    gch1[:, kc:kc + 1], start=(kc == 0), stop=(kc == 3))
                    if kc == 3:
                        inst.inc(sT)
                Sc.wait(sT)
                Sc.activation(gch2[:, oc:oc + 1], b[:, 0:1], AF.Gelu,
                              bias=gb2[:, oc:oc + 1]).inc(sS)
                bank_free[7] = [(sS, sS.n)]
                wblk_free[ib] = [(sT, sT.n)]
            b = bacq(7)
            T.wait(sS)
            T.wait(sc_, consts_n)
            for kc in range(2):
                inst = T.matmul(b[0:1, 0:1], w4gc[:, 2 + kc:3 + kc],
                                gch2[:, kc:kc + 1], start=(kc == 0), stop=(kc == 1))
                if kc == 1:
                    inst.inc(sT)
            Sc.wait(sT)
            Sc.activation(gfb[0:1, 0:1], b[0:1, 0:1], AF.Sigmoid,
                          bias=sc3[0:1, 1:2]).inc(sS)
            bank_free[7] = [(sS, sS.n)]
            V.wait(sS)
            V.tensor_scalar(gfb[0:1, 1:2], sc3[0:1, 2:3], 0.6, None, ALU.mult).inc(sV)
            V.tensor_scalar(gfb[0:1, 3:4], sc3[0:1, 2:3], 1.4, None, ALU.mult).inc(sV)
            V.tensor_scalar(gfb[0:1, 4:5], sc3[0:1, 2:3], -1.0, 0.1, ALU.mult,
                            ALU.add).inc(sV)
            V.wait(sV)
            V.tensor_tensor(gfb[0:1, 5:6], gfb[0:1, 0:1], gfb[0:1, 3:4], ALU.mult).inc(sV)
            V.wait(sV)
            V.tensor_tensor(gfb[0:1, 2:3], gfb[0:1, 4:5], gfb[0:1, 5:6], ALU.add).inc(sV)
            kconst_done = sV.n
            if "gcin" in taps:
                Y.wait(sV, gcin_done)
                Y.dma_start(tap_h["gcin"], gcin[:, :]).inc(tap_sems["gcin"], 16)
            if "gf" in taps:
                Y.wait(sV, kconst_done)
                Y.dma_start(tap_h["gf"], gfb[:, :]).inc(tap_sems["gf"], 16)

            for pair in range(NCH):
                splits = state["splits"]
                w16 = [wq16, wk16, wv16]

                for proj in range(3):
                    if proj == 1 and state["pending"] is not None:
                        state["pending"]()
                        state["pending"] = None
                    whi, wlo = w16[proj]
                    for qt in range(NQT):
                        bi = [4, 5, 0, 1][(proj * NQT + qt) % 4]
                        b = bacq(bi)
                        T.wait(sV, splits[proj])
                        if pair == 0:
                            T.waits([xn_marks[4 * qt + 3]])
                        passes = []
                        for kc in range(NCH):
                            xh = xn_hi[:, kc * S + qt * QT: kc * S + qt * QT + QT]
                            xl = xn_lo[:, kc * S + qt * QT: kc * S + qt * QT + QT]
                            lh = whi[:, kc * 128:(kc + 1) * 128]
                            ll = wlo[:, kc * 128:(kc + 1) * 128]
                            passes += [(lh, xh), (lh, xl), (ll, xh)]
                        n = len(passes)
                        for ip, (lT, rhs) in enumerate(passes):
                            inst = T.matmul(b[:, :], lT, rhs, start=(ip == 0),
                                            stop=(ip == n - 1))
                            if ip == n - 1:
                                inst.inc(sT)
                        mmdone = sT.n
                        if proj < 2:
                            stk = q_stack if proj == 0 else k_stack
                            dA = stk[:, qt * QT: qt * QT + QT]
                            dB = stk[:, S + qt * QT: S + qt * QT + QT]
                            Sc.wait(sT, mmdone)
                            Sc.copy(dA[0:64, :], b[0:64, :]).inc(sS)
                            Sc.copy(dB[0:64, :], b[64:128, :]).inc(sS)
                            V.wait(sS)
                            V.tensor_tensor(dA[64:128, :], b[0:64, :], dA[0:64, :],
                                            ALU.subtract).inc(sV)
                            V.tensor_tensor(dB[64:128, :], b[64:128, :], dB[0:64, :],
                                            ALU.subtract).inc(sV)
                            bank_free[bi] = [(sV, sV.n)]
                        else:
                            vf = vfm[qt % 2]
                            V.wait(sT, mmdone)
                            V.tensor_copy(vf[:, :], b[:, :]).inc(sV)
                            bank_free[bi] = [(sV, sV.n)]
                            vf_ready = (sV, sV.n)

                            def mk_vtr(qt=qt, vf=vf, vf_ready=vf_ready):
                                def vtr():
                                    for j in range(4):
                                        tau = qt * 4 + j
                                        ti_ = 6 + tau % 2
                                        tb = bacq(ti_)
                                        T.waits([vf_ready])
                                        T.transpose(tb[:, 0:128],
                                                    vf[:, j * 128:(j + 1) * 128],
                                                    ident[:, :]).inc(sT)
                                        Sc.wait(sT)
                                        Sc.copy(bass.AP(vaugP.tensor,
                                                        vaugP.offset + tau * 130,
                                                        [[16384, 128], [65, 2], [1, 64]]),
                                                tb[:, 0:128].rearrange(
                                                    "p (h d) -> p h d", h=2)
                                                ).inc(sS)
                                        bank_free[ti_] = [(sS, sS.n)]
                                return vtr
                            if state.get("vtr") is not None:
                                state["vtr"]()
                            state["vtr"] = mk_vtr()
                if state.get("vtr") is not None:
                    state["vtr"]()
                    state["vtr"] = None
                att_S = sS.n
                state["proj_done_mark"] = sT.n

                def mk_qrep(hi_, qc, g):
                    par = g % 2

                    def build():
                        V.wait(sS, att_S)
                        V.wait(sV)
                        V.waits(state["qrep_free"][par])
                        so = hi_ * S
                        qh = q_stack[0:64, so + qc * QT: so + qc * QT + QT]
                        ql = q_stack[64:128, so + qc * QT: so + qc * QT + QT]
                        V.tensor_copy(qrep_hi2[par][0:64, :], qh).inc(sV)
                        V.tensor_copy(qrep_hi2[par][64:128, :], qh).inc(sV)
                        V.tensor_copy(qrep_lo2[par][0:64, :], ql).inc(sV)
                        V.tensor_copy(qrep_lo2[par][64:128, :], ql).inc(sV)
                        state["qrep_done"][par] = sV.n
                    return build

                for hi_ in range(2):
                    stk_off = hi_ * S
                    for qc in range(NQT):
                        g = state["g"]
                        state["g"] += 1
                        par = g % 2
                        qrep_hi, qrep_lo = qrep_hi2[par], qrep_lo2[par]
                        n_sb = n_sb2[par]
                        wvb = [2, 3, 6][g % 3]
                        if state["qrep_next"] is None:
                            mk_qrep(hi_, qc, g)()
                        if (hi_, qc) == (0, 1) and pair + 1 < NCH:
                            state["splits"] = load_split_pair(
                                pair + 1, state["proj_done_mark"])
                        if qc < NQT - 1:
                            state["qrep_next"] = mk_qrep(hi_, qc + 1, g + 1)
                        elif hi_ == 0:
                            state["qrep_next"] = mk_qrep(1, 0, g + 1)
                        else:
                            state["qrep_next"] = None
                        qrep_done = state["qrep_done"][par]
                        b2 = bacq(wvb)
                        sig_marks = {}
                        wv_marks = {}

                        def emit_scores(kt, qrep_done=qrep_done):
                            bsc = bacq(kt % 2)
                            lT = k_stack[:, stk_off + kt * 128: stk_off + (kt + 1) * 128]
                            T.wait(sV, qrep_done)
                            T.matmul(bsc[:, :], lT, qrep_hi[:, :], start=True, stop=False)
                            T.matmul(bsc[:, :], lT, qrep_lo[:, :], start=False,
                                     stop=True).inc(sT)
                            return sT.n

                        def emit_sig(kt, sc_mark, g=g):
                            bufi = (g * NKT + kt) % 3
                            sgb = sig[bufi]
                            Sc.wait(sT, sc_mark)
                            Sc.waits(state["sig_free"][bufi])
                            Sc.activation(sgb[:, :], psum[kt % 2][:, :], AF.Sigmoid,
                                          scale=0.125).inc(sS)
                            bank_free[kt % 2] = [(sS, sS.n)]
                            sig_marks[kt] = sS.n

                        def emit_wv(kt, g=g, b2=b2, hi_=hi_, sig_marks=sig_marks,
                                    wv_marks=wv_marks):
                            bufi = (g * NKT + kt) % 3
                            T.wait(sS, sig_marks[kt])
                            lhs = bass.AP(vaugP.tensor,
                                          vaugP.offset + kt * 130 + hi_ * 65,
                                          [[16384, 128], [1, 65]])
                            T.matmul(b2[0:65, :], lhs,
                                     sig[bufi][:, :], start=(kt == 0),
                                     stop=(kt == NKT - 1)).inc(sT)
                            wv_marks[kt] = sT.n
                            state["sig_free"][bufi] = [(sT, sT.n)]

                        def mk_tail(g=g, b2=b2, n_sb=n_sb, wvb=wvb, hi_=hi_, qc=qc,
                                    pair=pair, wv_marks=wv_marks):
                            dst = B[hi_ * 64: hi_ * 64 + 64,
                                    pair * S + qc * QT: pair * S + qc * QT + QT]

                            def tail_norm():
                                V.wait(sT, wv_marks[NKT - 1])
                                V.tensor_scalar(n_sb[0:1, 0:512], b2[64:65, :],
                                                ATT_EPS, None, ALU.add).inc(sV)
                                nx = sV.n

                                def pending():
                                    b3 = bacq(4)
                                    T.wait(sV, nx)
                                    T.matmul(b3[0:64, :], ones64[:, :],
                                             n_sb[0:1, 0:512], start=True,
                                             stop=True).inc(sT)
                                    V.wait(sT)
                                    V.wait(sV)
                                    V.reciprocal(recip[0:64, 0:512],
                                                 b3[0:64, :]).inc(sV)
                                    bank_free[4] = [(sV, sV.n)]
                                    V.wait(sV)
                                    V.tensor_tensor(dst, b2[0:64, :],
                                                    recip[0:64, 0:512],
                                                    ALU.mult).inc(sV)
                                    bank_free[wvb] = [(sV, sV.n)]
                                state["pending"] = pending
                            return tail_norm

                        carry = state["carry"]
                        for kt in range(NKT):
                            sc_mark = emit_scores(kt)
                            if kt <= 1 and carry is not None:
                                carry(kt)
                            emit_sig(kt, sc_mark)
                            if kt == 1 and state["tail"] is not None:
                                state["tail"]()
                                state["tail"] = None
                            if kt >= 2:
                                emit_wv(kt - 2)
                            if kt == 4 and state["qrep_next"] is not None:
                                state["qrep_next"]()
                            if kt == 6 and state["pending"] is not None:
                                state["pending"]()
                                state["pending"] = None
                        state["qrep_free"][par] = [(sT, sT.n)]
                        if (hi_, qc) == (1, NQT - 1):
                            emit_wv(NKT - 2)
                            emit_wv(NKT - 1)
                            if state["pending"] is not None:
                                state["pending"]()
                                state["pending"] = None
                            mk_tail()()
                            state["carry"] = None
                            state["tail"] = None
                        else:
                            def mk_carry(emit_wv=emit_wv):
                                def carry_fn(k):
                                    emit_wv(NKT - 2 + k)
                                return carry_fn
                            state["carry"] = mk_carry()
                            state["tail"] = mk_tail()
            if state["pending"] is not None:
                state["pending"]()
                state["pending"] = None
            attn_V = sV.n
            if "attn" in taps:
                Y.wait(sV, attn_V)
                Y.dma_start(tap_h["attn"], B[:, :]).inc(tap_sems["attn"], 16)
                tap_marks["attn"] = (tap_sems["attn"], 16)

            for oc in range(NCH):
                buf, ib, wready = load_block(woT, oc, 8, D)
                for qc in range(NQT):
                    bi = 4 + qc % 2
                    b = bacq(bi)
                    T.waits([wready])
                    T.wait(sV, attn_V)
                    for kc in range(NCH):
                        rhs = B[:, kc * S + qc * QT: kc * S + qc * QT + QT]
                        inst = T.matmul(b[:, :], buf[:, kc * 128:(kc + 1) * 128], rhs,
                                        start=(kc == 0), stop=(kc == NCH - 1))
                        if kc == NCH - 1:
                            inst.inc(sT)
                    Sc.wait(sT)
                    Sc.copy(C[:, oc * S + qc * QT: oc * S + qc * QT + QT],
                            b[:, :]).inc(sS)
                    bank_free[bi] = [(sS, sS.n)]
                wblk_free[ib] = [(sT, sT.n)]
            wo_T, wo_S = sT.n, sS.n
            if "acc" in taps:
                Y.wait(sS, wo_S)
                Y.dma_start(tap_h["acc"], C[:, :]).inc(tap_sems["acc"], 16)
                tap_marks["acc"] = (tap_sems["acc"], 16)

            V.wait(sT, wo_T)
            if "attn" in tap_marks:
                V.waits([tap_marks["attn"]])
            Bv = B.ap().rearrange("p (c t) -> p c t", c=NCH)
            xhv = xn_hi.rearrange("p (c t) -> p c t", c=NCH)
            xlv = xn_lo.rearrange("p (c t) -> p c t", c=NCH)
            V.tensor_tensor(Bv[:, :, 0:FQT], xhv[:, :, 0:FQT], xlv[:, :, 0:FQT],
                            ALU.add).inc(sV)
            sinit0 = (sV, sV.n)
            G.wait(sT, wo_T)
            if "attn" in tap_marks:
                G.waits([tap_marks["attn"]])
            sinit = []
            for c in range(4):
                E_ = V if c < 2 else G
                sm = sV if c < 2 else sG
                for cc in (c, c + 4):
                    E_.tensor_tensor(B[:, cc * S + FQT:(cc + 1) * S],
                                     xn_hi[:, cc * S + FQT:(cc + 1) * S],
                                     xn_lo[:, cc * S + FQT:(cc + 1) * S],
                                     ALU.add).inc(sm)
                sinit.append((sm, sm.n))
            sinit_g = sG.n
            stream_ready = sV.n
            if "xnraw" in tap_marks:
                V.waits([tap_marks["xnraw"]])
            w1_marks = []
            for oc in range(NCH):
                buf, ib, wready = load_block(w1, oc, 8, D)
                E_ = V if oc < 4 else G
                sm = sV if oc < 4 else sG
                E_.waits([sinit0, sinit[oc // 2]])
                E_.wait(sG, sinit_g)
                E_.waits([wready])
                hi = w1hi.rearrange("p (oc w) -> p oc w", oc=NCH)[:, oc, :]
                lo = w1lo.rearrange("p (oc w) -> p oc w", oc=NCH)[:, oc, :]
                E_.tensor_copy(hi, buf[:, 0:1024]).inc(sm)
                E_.wait(sm)
                E_.tensor_tensor(lo, buf[:, 0:1024], hi, ALU.subtract).inc(sm)
                wblk_free[ib] = [(sm, sm.n)]
                w1_marks.append((sm, sm.n))
            for oc in range(4):
                Y.waits([sinit0, sinit[oc]])
                ap = bass.AP(w2, oc * 128, [[512, 128], [128 * 512, 8], [1, 128]])
                Y.dma_start(w2res.rearrange("p (oc w) -> p oc w", oc=4)[:, oc, :]
                            .rearrange("p (kc o) -> p kc o", kc=8), ap).inc(sf, 16)
            for oc in range(2):
                Y.waits(wblk_free[0])
                Y.waits(wblk_free[1])
                ap = bass.AP(w3, oc * 128, [[256, 128], [128 * 256, 4], [1, 128]])
                Y.dma_start(wblk0.ap()[:, oc * 512:(oc + 1) * 512]
                            .rearrange("p (kc o) -> p kc o", kc=4), ap).inc(sf, 16)
            ffn_w_ready = sf.n
            w1_ready = sV.n

            strv = Bv
            Cv = C.ap().rearrange("p (c t) -> p c t", c=NCH)
            acc_marks = [0] * NTT

            def emit_s16(cy, ti, first=False):
                t0 = ti * FQT
                if first:
                    V.waits([sinit0])
                else:
                    V.wait(sT, state["cn1_done"])
                    V.wait(sV)
                    V.wait(sG, sinit_g)
                V.tensor_copy(s16h.rearrange("p (c t) -> p c t", c=NCH),
                              strv[:, :, t0:t0 + FQT]).inc(sV)
                V.wait(sV)
                V.tensor_tensor(s16l.rearrange("p (c t) -> p c t", c=NCH),
                                strv[:, :, t0:t0 + FQT],
                                s16h.rearrange("p (c t) -> p c t", c=NCH),
                                ALU.subtract).inc(sV)
                state["s16_ready"] = sV.n

            def emit_cn1(cy, ti):
                for oc in range(NCH):
                    bi = 4 + oc % 2
                    b = psum[bi]
                    T.waits(bank_free[bi])
                    T.waits([w1_marks[oc]])
                    T.wait(sV, state["s16_ready"])
                    passes = []
                    for kc in range(NCH):
                        lh = w1hi[:, oc * 1024 + kc * 128: oc * 1024 + (kc + 1) * 128]
                        ll = w1lo[:, oc * 1024 + kc * 128: oc * 1024 + (kc + 1) * 128]
                        rh = s16h[:, kc * FQT:(kc + 1) * FQT]
                        rl = s16l[:, kc * FQT:(kc + 1) * FQT]
                        passes += [(lh, rh), (lh, rl), (ll, rh)]
                    n = len(passes)
                    for ip, (lT, rhs) in enumerate(passes):
                        inst = T.matmul(b[:, 0:FQT], lT, rhs, start=(ip == 0),
                                        stop=(ip == n - 1))
                        if ip == n - 1:
                            inst.inc(sT)
                    Sc.wait(sT)
                    Sc.wait(sc_, consts_n)
                    Sc.wait(sG, sinit_g)
                    Sc.wait(sV, stream_ready)
                    Sc.activation(g1[:, oc * FQT:(oc + 1) * FQT], b[:, 0:FQT],
                                  AF.Gelu, bias=cnb1[:, oc:oc + 1]).inc(sS)
                    bank_free[bi] = [(sS, sS.n)]
                state["g1_ready"] = sS.n
                state["cn1_done"] = sT.n

            state["g2_guard"] = list(wblk_free[1])
            emit_s16(0, 0, first=True)
            emit_cn1(0, 0)
            for cy in range(3):
                for ti in range(NFQT):
                    t0 = ti * FQT
                    g1_ready = state["g1_ready"]
                    if state.get("ffn_fin") is not None:
                        state["ffn_fin"]()
                        state["ffn_fin"] = None

                    for oc in range(4):
                        bi = 4 + oc % 2
                        b = bacq(bi)
                        T.wait(sf, ffn_w_ready)
                        T.wait(sS, g1_ready)
                        for kc in range(NCH):
                            inst = T.matmul(b[:, 0:FQT],
                                            w2res[:, oc * 1024 + kc * 128:
                                                  oc * 1024 + (kc + 1) * 128],
                                            g1[:, kc * FQT:(kc + 1) * FQT],
                                            start=(kc == 0), stop=(kc == NCH - 1))
                            if kc == NCH - 1:
                                inst.inc(sT)
                        Sc.wait(sT)
                        Sc.waits(state.get("g2_guard", []))
                        Sc.activation(wblk1[:, oc * FQT:(oc + 1) * FQT], b[:, 0:FQT],
                                      AF.Gelu, bias=cnb2[:, oc:oc + 1]).inc(sS)
                        bank_free[bi] = [(sS, sS.n)]
                    g2_ready = sS.n

                    for oc in range(2):
                        bi = 4 + oc % 2
                        b = bacq(bi)
                        T.wait(sS, g2_ready)
                        for kc in range(4):
                            inst = T.matmul(b[:, 0:FQT],
                                            wblk0[:, oc * 512 + kc * 128:
                                                  oc * 512 + (kc + 1) * 128],
                                            wblk1[:, kc * FQT:(kc + 1) * FQT],
                                            start=(kc == 0), stop=(kc == 3))
                            if kc == 3:
                                inst.inc(sT)
                        Sc.wait(sT)
                        Sc.activation(g3b[:, oc * FQT:(oc + 1) * FQT], b[:, 0:FQT],
                                      AF.Gelu, bias=cnb3[:, oc:oc + 1]).inc(sS)
                        bank_free[bi] = [(sS, sS.n)]
                    g3_ready = sS.n

                    nxt = (cy, ti + 1) if ti + 1 < NFQT else                           ((cy + 1, 0) if cy < 2 else None)
                    if nxt is not None:
                        emit_s16(*nxt)
                        emit_cn1(*nxt)

                    b7 = bacq(7)
                    T.wait(sS, g3_ready)
                    T.wait(sc_, consts_n)
                    for kc in range(2):
                        inst = T.matmul(b7[0:1, 0:FQT], w4gc[:, kc:kc + 1],
                                        g3b[:, kc * FQT:(kc + 1) * FQT],
                                        start=(kc == 0), stop=(kc == 1))
                        if kc == 1:
                            inst.inc(sT)
                    Sc.wait(sT)
                    Sc.wait(sV, kconst_done)
                    Sc.wait(sV)
                    Sc.activation(tf_sb[0:1, 0:FQT], b7[0:1, 0:FQT], AF.Sigmoid,
                                  bias=sc3[0:1, 0:1]).inc(sS)
                    bank_free[7] = [(sS, sS.n)]
                    V.wait(sS)
                    V.wait(sT)
                    V.tensor_scalar(cp_sb[0:1, 0:FQT], tf_sb[0:1, 0:FQT],
                                    gfb[0:1, 1:2], gfb[0:1, 2:3],
                                    ALU.mult, ALU.add).inc(sV)
                    cp_ready = sV.n

                    def mk_fin(cy=cy, ti=ti, t0=t0, cp_ready=cp_ready):
                        def fin():
                            b6 = bacq(6)
                            T.wait(sV, cp_ready)
                            T.matmul(b6[:, 0:FQT], ones128[:, :], cp_sb[0:1, 0:FQT],
                                     start=True, stop=True).inc(sT)
                            cp_bcast = sT.n
                            for (a, bb, sa, sb_) in [(0, 4, 1.0, -1.0),
                                                     (1, 5, 1.0, -1.0),
                                                     (2, 6, -1.0, 1.0),
                                                     (3, 7, -1.0, 1.0)]:
                                olda = strv[:, a, t0:t0 + FQT]
                                oldb = strv[:, bb, t0:t0 + FQT]
                                V.waits(bank_free[0])
                                V.waits(bank_free[1])
                                V.wait(sT, cp_bcast)
                                V.scalar_tensor_tensor(psum[0][:, 0:FQT], oldb, sa,
                                                       b6[:, 0:FQT], ALU.mult,
                                                       ALU.mult).inc(sV)
                                V.scalar_tensor_tensor(psum[1][:, 0:FQT], olda, sb_,
                                                       b6[:, 0:FQT], ALU.mult,
                                                       ALU.mult).inc(sV)
                                V.wait(sV)
                                V.tensor_tensor(olda, olda, psum[0][:, 0:FQT],
                                                ALU.add).inc(sV)
                                V.tensor_tensor(oldb, oldb, psum[1][:, 0:FQT],
                                                ALU.add).inc(sV)
                                bank_free[0] = [(sV, sV.n)]
                                bank_free[1] = [(sV, sV.n)]
                            bank_free[6] = [(sV, sV.n)]
                            if cy == 2:
                                V.wait(sS, wo_S)
                                if "acc" in tap_marks:
                                    V.waits([tap_marks["acc"]])
                                for half in range(2):
                                    t = 2 * ti + half
                                    V.wait(sV)
                                    V.tensor_tensor(Cv[:, :, t * 128:(t + 1) * 128],
                                                    Cv[:, :, t * 128:(t + 1) * 128],
                                                    Bv[:, :, t * 128:(t + 1) * 128],
                                                    ALU.add).inc(sV)
                                    acc_marks[t] = sV.n
                        return fin

                    state["ffn_fin"] = mk_fin()
            if state.get("ffn_fin") is not None:
                state["ffn_fin"]()
                state["ffn_fin"] = None
            ffn_done = sV.n
            if "stream" in taps:
                Y.wait(sV, ffn_done)
                Y.dma_start(tap_h["stream"], B[:, :]).inc(tap_sems["stream"], 16)
                tap_marks["stream"] = (tap_sems["stream"], 16)

            for t in range(NTT):
                xre_t = B.ap().rearrange("p (c t) -> p c t", c=NCH)[:, :, t * 128:(t + 1) * 128]
                xsrc = bass.AP(x_h, t * 128 * 1024,
                               [[1024, 128], [128, 8], [1, 128]])
                Sc.wait(sV, acc_marks[t])
                if "stream" in tap_marks:
                    Sc.waits([tap_marks["stream"]])
                Sc.waits(state.get(("xre_prev", t % 2), []))
                Sc.dma_start(xre_t, xsrc).inc(sxr[t % 2], 16)
                xmark = (sxr[t % 2], sxr[t % 2].n)
                state[("xre_prev", t % 2)] = [xmark]
                for c in range(NCH):
                    bi = 2 + (t * NCH + c) % 2
                    b = bacq(bi)
                    T.wait(sV, acc_marks[t])
                    T.transpose(b[:, 0:128],
                                C[:, c * S + t * 128: c * S + (t + 1) * 128],
                                ident[:, :]).inc(sT)
                    V.wait(sT)
                    V.waits([xmark])
                    V.tensor_tensor(xre_t[:, c, :], b[:, 0:128], xre_t[:, c, :],
                                    ALU.add).inc(sV)
                    bank_free[bi] = [(sV, sV.n)]
                Y.wait(sV)
                Y.dma_start(bass.AP(out_h, t * 128 * 1024,
                                    [[1024, 128], [128, 8], [1, 128]]),
                            xre_t).inc(stp[t % 2], 16)
            Y.wait(stp[0], stp[0].n)
            Y.wait(stp[1], stp[1].n)
            for ts_ in tap_sems.values():
                Y.wait(ts_, ts_.n)
            return

        @block.sync
        def _(eng):
            emit("sync", eng)

        @block.tensor
        def _(eng):
            emit("tensor", eng)

        @block.vector
        def _(eng):
            emit("vector", eng)

        @block.scalar
        def _(eng):
            emit("scalar", eng)

        @block.gpsimd
        def _(eng):
            emit("gpsimd", eng)

    return nc


_NC_CACHE = {}


def _get_nc(taps=()):
    key = tuple(sorted(taps))
    if key not in _NC_CACHE:
        _NC_CACHE[key] = build_nc(taps)
    return _NC_CACHE[key]


def _prep_shared(inputs):
    f32 = np.float32
    g = lambda k: np.asarray(inputs[k], dtype=f32)
    shared = {
        "wqT": np.ascontiguousarray(g("L0_attn_Wq").T),
        "wkT": np.ascontiguousarray(g("L0_attn_Wk").T),
        "wvT": np.ascontiguousarray(g("L0_attn_Wv").T),
        "woT": np.ascontiguousarray(g("L0_attn_Wo").T),
        "w1": np.ascontiguousarray(g("L0_ffn_cn_w1")),
        "w2": np.ascontiguousarray(g("L0_ffn_cn_w2")),
        "w3": np.ascontiguousarray(g("L0_ffn_cn_w3")),
        "gw1": np.ascontiguousarray(g("L0_ffn_gc_w1")),
        "gw2": np.ascontiguousarray(g("L0_ffn_gc_w2")),
        "ident": np.eye(128, dtype=f32),
        "cnb1": np.ascontiguousarray(g("L0_ffn_cn_b1").reshape(8, 128).T),
        "cnb2": np.ascontiguousarray(g("L0_ffn_cn_b2").reshape(4, 128).T),
        "cnb3": np.ascontiguousarray(g("L0_ffn_cn_b3").reshape(2, 128).T),
        "gb1": np.ascontiguousarray(g("L0_ffn_gc_b1").reshape(4, 128).T),
        "gb2": np.ascontiguousarray(g("L0_ffn_gc_b2").reshape(2, 128).T),
        "sc3": np.stack([g("L0_ffn_cn_b4").reshape(()),
                         g("L0_ffn_gc_b3").reshape(()),
                         g("L0_ffn_adaptive_range").reshape(())]).reshape(1, 3)
        .astype(f32),
        "w4gc": np.concatenate([
            np.ascontiguousarray(g("L0_ffn_cn_w4").reshape(2, 128).T),
            np.ascontiguousarray(g("L0_ffn_gc_w3").reshape(2, 128).T)], axis=1)
        .astype(f32),
    }
    return shared


def kernel(**inputs):
    x = np.asarray(inputs["L0_attn_x"], dtype=np.float32)
    shared = _prep_shared(inputs)
    in_maps = []
    for i in range(8):
        m = dict(shared)
        m["x"] = np.ascontiguousarray(x[i])
        in_maps.append(m)
    nc = _get_nc()
    res = run_bass_kernel_spmd(nc, in_maps, core_ids=list(range(8)))
    return np.stack([np.asarray(res.results[i]["out"]) for i in range(8)], axis=0)


# version 16
# speedup vs baseline: 1.0570x; 1.0364x over previous
import numpy as np
import concourse.bass as bass
import concourse.mybir as mybir
from concourse import library_config
from concourse.bass_utils import run_bass_kernel_spmd
from contextlib import ExitStack

F32 = mybir.dt.float32
F16 = mybir.dt.float16
AF = mybir.ActivationFunctionType
ALU = mybir.AluOpType
AX = mybir.AxisListType

S = 2048
D = 1024
NCH = 8
NTT = 16
NKT = 16
QT = 512
NQT = 4
FQT = 256
NFQT = S // FQT
LN_EPS = 1e-5
ATT_EPS = 1e-6


class Sem:
    __slots__ = ("h", "n")

    def __init__(self, h):
        self.h = h
        self.n = 0


class Inst:
    __slots__ = ("raw",)

    def __init__(self, raw):
        self.raw = raw

    def inc(self, sem, k=1):
        sem.n += k
        if self.raw is not None:
            self.raw.then_inc(sem.h, k)
        return self


class Eng:

    def __init__(self, real):
        self.real = real

    def wait(self, sem, n=None):
        v = sem.n if n is None else n
        if self.real is not None and v > 0:
            self.real.wait_ge(sem.h, v)

    def waits(self, marks):
        for (sem, n) in marks:
            self.wait(sem, n)

    def __getattr__(self, op):
        real = self.real

        def call(*a, **k):
            return Inst(getattr(real, op)(*a, **k) if real is not None else None)

        return call


def build_nc(taps=()):
    taps = tuple(t for t in taps)
    nc = bass.Bass("TRN2", target_bir_lowering=False)

    x_h = nc.dram_tensor("x", [S, D], F32, kind="ExternalInput")
    wqT = nc.dram_tensor("wqT", [D, D], F32, kind="ExternalInput")
    wkT = nc.dram_tensor("wkT", [D, D], F32, kind="ExternalInput")
    wvT = nc.dram_tensor("wvT", [D, D], F32, kind="ExternalInput")
    woT = nc.dram_tensor("woT", [D, D], F32, kind="ExternalInput")
    w1 = nc.dram_tensor("w1", [D, D], F32, kind="ExternalInput")
    w2 = nc.dram_tensor("w2", [D, 512], F32, kind="ExternalInput")
    w3 = nc.dram_tensor("w3", [512, 256], F32, kind="ExternalInput")
    gw1 = nc.dram_tensor("gw1", [D, 512], F32, kind="ExternalInput")
    gw2 = nc.dram_tensor("gw2", [512, 256], F32, kind="ExternalInput")
    ident_s = nc.dram_tensor("ident", [128, 128], F32, kind="ExternalInput")
    cnb1_s = nc.dram_tensor("cnb1", [128, 8], F32, kind="ExternalInput")
    cnb2_s = nc.dram_tensor("cnb2", [128, 4], F32, kind="ExternalInput")
    cnb3_s = nc.dram_tensor("cnb3", [128, 2], F32, kind="ExternalInput")
    gb1_s = nc.dram_tensor("gb1", [128, 4], F32, kind="ExternalInput")
    gb2_s = nc.dram_tensor("gb2", [128, 2], F32, kind="ExternalInput")
    sc3_s = nc.dram_tensor("sc3", [1, 3], F32, kind="ExternalInput")
    w4gc_s = nc.dram_tensor("w4gc", [128, 4], F32, kind="ExternalInput")
    out_h = nc.dram_tensor("out", [S, D], F32, kind="ExternalOutput")
    tap_shapes = {"xnraw": [128, 16384], "attn": [128, 16384], "acc": [128, 16384],
                  "gcin": [128, 8], "gf": [1, 8], "stream": [128, 16384]}
    tap_h = {t: nc.dram_tensor("tap_" + t, tap_shapes[t], F32, kind="ExternalOutput").ap()
             for t in taps}

    with ExitStack() as ctx:
        sbt = lambda name, shape, dt=F32: ctx.enter_context(nc.sbuf_tensor(name, shape, dt))
        A = sbt("slotA", [128, 16384])
        B = sbt("slotB", [128, 16384])
        C = sbt("slotC", [128, 16384])
        A16 = A.bitcast(F16)
        B16 = B.bitcast(F16)
        C16 = C.bitcast(F16)
        wblk0 = sbt("wblk0", [128, 1024])
        wblk1 = sbt("wblk1", [128, 1024])
        ident = sbt("identb", [128, 128])
        cnb1 = sbt("cnb1b", [128, 8])
        cnb2 = sbt("cnb2b", [128, 4])
        cnb3 = sbt("cnb3b", [128, 2])
        gb1 = sbt("gb1b", [128, 4])
        gb2 = sbt("gb2b", [128, 2])
        sc3 = sbt("sc3b", [1, 3])
        w4gc = sbt("w4gcb", [128, 4])
        ones64 = sbt("ones64", [1, 64])
        ones128 = sbt("ones128", [1, 128])
        gcp = sbt("gcp", [128, 128])
        gcin = sbt("gcin", [128, 8])
        gch1 = sbt("gch1", [128, 4])
        gch2 = sbt("gch2", [128, 2])
        gfb = sbt("gfb", [1, 8])
        tf_sb = sbt("tf_sb", [1, FQT])
        cp_sb = sbt("cp_sb", [1, FQT])
        g3b = sbt("g3b", [128, 512])
        psum = [ctx.enter_context(nc.psum_tensor(f"p{i}", [128, 512], F32))
                for i in range(8)]
        sems = {nm: Sem(ctx.enter_context(nc.semaphore(nm)))
                for nm in ["sc_", "sx", "swb0", "swb1", "sf", "sxr0", "sxr1",
                           "st0", "st1", "sT", "sV", "sS", "sG"]}
        tap_sems = {t: Sem(ctx.enter_context(nc.semaphore("stp_" + t)))
                    for t in taps}
        block = ctx.enter_context(nc.Block())

        xn_hi = A16[:, 0:16384]
        xn_lo = A16[:, 16384:32768]
        w1hi = A16[:, 0:8192]
        w1lo = A16[:, 8192:16384]
        w2res = A[:, 8192:12288]
        s16h = A16[:, 24576:26624]
        s16l = A16[:, 26624:28672]
        g1 = A[:, 14336:16384]
        xre = [A[:, 0:1024], A[:, 1024:2048]]
        outt = [A[:, 2048:3072], A[:, 3072:4096]]
        ln_dump = C[:, 0:1024]
        ln_xnt = [C[:, 1024:2048], C[:, 2048:3072]]
        st_ = C[:, 3072:3248]
        ln_s1, ln_s2, ln_mu, ln_var, ln_m2, ln_std, ln_r0, ln_t, ln_rstd, ln_nmr = (
            st_[:, i * 16:(i + 1) * 16] for i in range(10))
        k_stack = C16[:, 0:4096]
        q_stack = C16[:, 4096:8192]
        qrep_hi2 = [C16[:, 8192:8704], C16[:, 29760:30272]]
        qrep_lo2 = [C16[:, 8704:9216], C16[:, 30272:30784]]
        wq16 = (C16[:, 9216:10240], C16[:, 10240:11264])
        wk16 = (C16[:, 11264:12288], C16[:, 12288:13312])
        wv16 = (C16[:, 13312:14336], C16[:, 14336:15360])
        vaugP = C[:, 7680:9760]
        vfm = [C[:, 9760:10272], C[:, 10272:10784]]
        sig = [C[:, 10784:11296], C[:, 11296:11808], C[:, 15392:15904]]
        n_sb2 = [C[:, 11808:12320], C[:, 12832:13344]]
        recip = C[:, 12320:12832]

        def emit(which, eng):
            T = Eng(eng if which == "tensor" else None)
            V = Eng(eng if which == "vector" else None)
            Sc = Eng(eng if which == "scalar" else None)
            G = Eng(eng if which == "gpsimd" else None)
            Y = Eng(eng if which == "sync" else None)
            sc_, sx, sf = sems["sc_"], sems["sx"], sems["sf"]
            swb = [sems["swb0"], sems["swb1"]]
            sxr = [sems["sxr0"], sems["sxr1"]]
            stp = [sems["st0"], sems["st1"]]
            sT, sV, sS = sems["sT"], sems["sV"], sems["sS"]
            sG = sems["sG"]
            for s_ in list(sems.values()) + list(tap_sems.values()):
                s_.n = 0
            bank_free = {i: [] for i in range(8)}
            state = {}

            def bacq(i):
                T.waits(bank_free[i])
                return psum[i]

            Y.dma_start(B.ap().rearrange("p (t n) -> p t n", t=16),
                        bass.AP(x_h, 0, [[1024, 128], [128 * 1024, 16], [1, 1024]])
                        ).inc(sx, 16)
            for dst, src in [(ident, ident_s), (cnb1, cnb1_s), (cnb2, cnb2_s),
                             (cnb3, cnb3_s), (gb1, gb1_s), (gb2, gb2_s),
                             (sc3, sc3_s), (w4gc, w4gc_s)]:
                Y.dma_start(dst[:, :], src.ap()).inc(sc_, 16)
            consts_n = sc_.n
            V.memset(gfb[:, :], 0.0).inc(sV)
            V.memset(ones64[:, :], 1.0).inc(sV)
            V.memset(ones128[:, :], 1.0).inc(sV)
            V.memset(vaugP[:, :], 1.0).inc(sV)

            V.wait(sx, 16)
            Sc.wait(sx, 16)
            for t in range(NTT):
                xt = B[:, t * 1024:(t + 1) * 1024]
                V.reduce_sum(ln_s1[:, t:t + 1], xt, axis=AX.X).inc(sV)
                if t > 0:
                    Sc.wait(sS)
                Sc.activation(ln_dump[:, :], xt, AF.Square,
                              accum_out=ln_s2[:, t:t + 1]).inc(sS)
            V.wait(sS)
            V.wait(sV)
            V.tensor_scalar(ln_mu[:, :], ln_s1[:, :], 1.0 / D, None, ALU.mult).inc(sV)
            V.tensor_scalar(ln_var[:, :], ln_s2[:, :], 1.0 / D, None, ALU.mult).inc(sV)
            V.wait(sV)
            V.tensor_tensor(ln_m2[:, :], ln_mu[:, :], ln_mu[:, :], ALU.mult).inc(sV)
            V.wait(sV)
            V.tensor_tensor(ln_var[:, :], ln_var[:, :], ln_m2[:, :], ALU.subtract).inc(sV)
            V.wait(sV)
            V.tensor_scalar(ln_var[:, :], ln_var[:, :], LN_EPS, None, ALU.add).inc(sV)
            Sc.wait(sV)
            Sc.activation(ln_std[:, :], ln_var[:, :], AF.Sqrt).inc(sS)
            V.wait(sS)
            V.reciprocal(ln_r0[:, :], ln_std[:, :]).inc(sV)
            V.wait(sV)
            V.tensor_tensor(ln_t[:, :], ln_r0[:, :], ln_r0[:, :], ALU.mult).inc(sV)
            V.wait(sV)
            V.tensor_tensor(ln_t[:, :], ln_t[:, :], ln_var[:, :], ALU.mult).inc(sV)
            V.wait(sV)
            V.tensor_scalar(ln_t[:, :], ln_t[:, :], -0.5, 1.5, ALU.mult, ALU.add).inc(sV)
            V.wait(sV)
            V.tensor_tensor(ln_rstd[:, :], ln_r0[:, :], ln_t[:, :], ALU.mult).inc(sV)
            V.wait(sV)
            V.scalar_tensor_tensor(ln_nmr[:, :], ln_mu[:, :], -1.0, ln_rstd[:, :],
                                   ALU.mult, ALU.mult).inc(sV)
            ln_done = sV.n

            wblks = [wblk0, wblk1]
            wblk_free = [[], []]
            nblk = [0]

            def load_block(src_handle, col, nk, row_len):
                i = nblk[0] % 2
                buf = wblks[i]
                Y.waits(wblk_free[i])
                ap = bass.AP(src_handle, col * 128,
                             [[row_len, 128], [128 * row_len, nk], [1, 128]])
                Y.dma_start(buf.ap()[:, 0:nk * 128].rearrange("p (kc o) -> p kc o", kc=nk),
                            ap).inc(swb[i], 16)
                nblk[0] += 1
                return buf, i, (swb[i], swb[i].n)

            def split16(buf, i, wready, hi, lo, width=1024):
                V.waits([wready])
                V.tensor_copy(hi[:, 0:width], buf[:, 0:width]).inc(sV)
                V.wait(sV)
                V.tensor_tensor(lo[:, 0:width], buf[:, 0:width], hi[:, 0:width],
                                ALU.subtract).inc(sV)
                wblk_free[i] = [(sV, sV.n)]
                return sV.n

            state.update({"g": 0, "pending": None, "qrep_free": [[], []],
                          "qrep_done": [0, 0], "qrep_next": None, "carry": None,
                          "tail": None, "sig_free": [[], [], []]})
            def load_split_pair(pair, guard_mark):
                bq, iq, rq = load_block(wqT, pair, 8, D)
                V.wait(sT, guard_mark)
                sqm = split16(bq, iq, rq, *wq16)
                bk, ik, rk = load_block(wkT, pair, 8, D)
                skm = split16(bk, ik, rk, *wk16)
                bv, iv, rv = load_block(wvT, pair, 8, D)
                svm = split16(bv, iv, rv, *wv16)
                return [sqm, skm, svm]

            state["splits"] = load_split_pair(0, 0)
            xnt_free = [[], []]
            xn_marks = []
            def emit_xnt(t):
                xt = B[:, t * 1024:(t + 1) * 1024]
                Sc.wait(sV, ln_done)
                Sc.waits(xnt_free[t % 2])
                Sc.activation(ln_xnt[t % 2][:, :], xt, AF.Identity,
                              bias=ln_nmr[:, t:t + 1],
                              scale=ln_rstd[:, t:t + 1]).inc(sS)
                state[("xnt_ready", t)] = sS.n

            emit_xnt(0)
            emit_xnt(1)
            for t in range(NTT):
                xnt = ln_xnt[t % 2]
                xnt_ready = state[("xnt_ready", t)]
                for c in range(NCH):
                    bi = 6 + (t * NCH + c) % 2
                    b = bacq(bi)
                    T.wait(sS, xnt_ready)
                    T.wait(sc_, consts_n)
                    T.transpose(b[:, 0:128], xnt[:, c * 128:(c + 1) * 128],
                                ident[:, :]).inc(sT)
                    hi_sl = xn_hi[:, c * S + t * 128: c * S + (t + 1) * 128]
                    lo_sl = xn_lo[:, c * S + t * 128: c * S + (t + 1) * 128]
                    Sc.wait(sT)
                    Sc.copy(hi_sl, b[:, 0:128]).inc(sS)
                    V.wait(sS)
                    V.tensor_tensor(lo_sl, b[:, 0:128], hi_sl, ALU.subtract).inc(sV)
                    V.reduce_sum(gcp[:, c * NTT + t: c * NTT + t + 1], b[:, 0:128],
                                 axis=AX.X).inc(sV)
                    bank_free[bi] = [(sV, sV.n)]
                xnt_free[t % 2] = [(sT, sT.n)]
                if t + 2 < NTT:
                    emit_xnt(t + 2)
                xn_marks.append((sV, sV.n))
            V.wait(sV)
            V.reduce_sum(gcin[:, 0:NCH],
                         gcp.ap().rearrange("p (c t) -> p c t", c=NCH),
                         axis=AX.X).inc(sV)
            V.wait(sV)
            V.tensor_scalar(gcin[:, 0:NCH], gcin[:, 0:NCH], 1.0 / S, None,
                            ALU.mult).inc(sV)
            gcin_done = sV.n
            tap_marks = {}
            if "xnraw" in taps:
                Y.wait(sV, gcin_done)
                Y.dma_start(tap_h["xnraw"], A[:, :]).inc(tap_sems["xnraw"], 16)
                tap_marks["xnraw"] = (tap_sems["xnraw"], 16)

            for oc in range(4):
                buf, ib, wready = load_block(gw1, oc, 8, 512)
                b = bacq(7)
                T.waits([wready])
                T.wait(sV, gcin_done)
                for kc in range(NCH):
                    inst = T.matmul(b[:, 0:1], buf[:, kc * 128:(kc + 1) * 128],
                                    gcin[:, kc:kc + 1], start=(kc == 0),
                                    stop=(kc == NCH - 1))
                    if kc == NCH - 1:
                        inst.inc(sT)
                Sc.wait(sT)
                Sc.activation(gch1[:, oc:oc + 1], b[:, 0:1], AF.Gelu,
                              bias=gb1[:, oc:oc + 1]).inc(sS)
                bank_free[7] = [(sS, sS.n)]
                wblk_free[ib] = [(sT, sT.n)]
            for oc in range(2):
                buf, ib, wready = load_block(gw2, oc, 4, 256)
                b = bacq(7)
                T.waits([wready])
                T.wait(sS)
                for kc in range(4):
                    inst = T.matmul(b[:, 0:1], buf[:, kc * 128:(kc + 1) * 128],
                                    gch1[:, kc:kc + 1], start=(kc == 0), stop=(kc == 3))
                    if kc == 3:
                        inst.inc(sT)
                Sc.wait(sT)
                Sc.activation(gch2[:, oc:oc + 1], b[:, 0:1], AF.Gelu,
                              bias=gb2[:, oc:oc + 1]).inc(sS)
                bank_free[7] = [(sS, sS.n)]
                wblk_free[ib] = [(sT, sT.n)]
            b = bacq(7)
            T.wait(sS)
            T.wait(sc_, consts_n)
            for kc in range(2):
                inst = T.matmul(b[0:1, 0:1], w4gc[:, 2 + kc:3 + kc],
                                gch2[:, kc:kc + 1], start=(kc == 0), stop=(kc == 1))
                if kc == 1:
                    inst.inc(sT)
            Sc.wait(sT)
            Sc.activation(gfb[0:1, 0:1], b[0:1, 0:1], AF.Sigmoid,
                          bias=sc3[0:1, 1:2]).inc(sS)
            bank_free[7] = [(sS, sS.n)]
            V.wait(sS)
            V.tensor_scalar(gfb[0:1, 1:2], sc3[0:1, 2:3], 0.6, None, ALU.mult).inc(sV)
            V.tensor_scalar(gfb[0:1, 3:4], sc3[0:1, 2:3], 1.4, None, ALU.mult).inc(sV)
            V.tensor_scalar(gfb[0:1, 4:5], sc3[0:1, 2:3], -1.0, 0.1, ALU.mult,
                            ALU.add).inc(sV)
            V.wait(sV)
            V.tensor_tensor(gfb[0:1, 5:6], gfb[0:1, 0:1], gfb[0:1, 3:4], ALU.mult).inc(sV)
            V.wait(sV)
            V.tensor_tensor(gfb[0:1, 2:3], gfb[0:1, 4:5], gfb[0:1, 5:6], ALU.add).inc(sV)
            kconst_done = sV.n
            if "gcin" in taps:
                Y.wait(sV, gcin_done)
                Y.dma_start(tap_h["gcin"], gcin[:, :]).inc(tap_sems["gcin"], 16)
            if "gf" in taps:
                Y.wait(sV, kconst_done)
                Y.dma_start(tap_h["gf"], gfb[:, :]).inc(tap_sems["gf"], 16)

            for pair in range(NCH):
                splits = state["splits"]
                w16 = [wq16, wk16, wv16]

                for proj in range(3):
                    if proj == 1 and state["pending"] is not None:
                        state["pending"]()
                        state["pending"] = None
                    whi, wlo = w16[proj]
                    for qt in range(NQT):
                        bi = [4, 5, 0, 1][(proj * NQT + qt) % 4]
                        b = bacq(bi)
                        T.wait(sV, splits[proj])
                        if pair == 0:
                            T.waits([xn_marks[4 * qt + 3]])
                        passes = []
                        for kc in range(NCH):
                            xh = xn_hi[:, kc * S + qt * QT: kc * S + qt * QT + QT]
                            xl = xn_lo[:, kc * S + qt * QT: kc * S + qt * QT + QT]
                            lh = whi[:, kc * 128:(kc + 1) * 128]
                            ll = wlo[:, kc * 128:(kc + 1) * 128]
                            passes += [(lh, xh), (lh, xl), (ll, xh)]
                        n = len(passes)
                        for ip, (lT, rhs) in enumerate(passes):
                            inst = T.matmul(b[:, :], lT, rhs, start=(ip == 0),
                                            stop=(ip == n - 1))
                            if ip == n - 1:
                                inst.inc(sT)
                        mmdone = sT.n
                        if proj < 2:
                            stk = q_stack if proj == 0 else k_stack
                            dA = stk[:, qt * QT: qt * QT + QT]
                            dB = stk[:, S + qt * QT: S + qt * QT + QT]
                            Sc.wait(sT, mmdone)
                            Sc.copy(dA[0:64, :], b[0:64, :]).inc(sS)
                            Sc.copy(dB[0:64, :], b[64:128, :]).inc(sS)
                            V.wait(sS)
                            V.tensor_tensor(dA[64:128, :], b[0:64, :], dA[0:64, :],
                                            ALU.subtract).inc(sV)
                            V.tensor_tensor(dB[64:128, :], b[64:128, :], dB[0:64, :],
                                            ALU.subtract).inc(sV)
                            bank_free[bi] = [(sV, sV.n)]
                        else:
                            vf = vfm[qt % 2]
                            V.wait(sT, mmdone)
                            V.tensor_copy(vf[:, :], b[:, :]).inc(sV)
                            bank_free[bi] = [(sV, sV.n)]
                            vf_ready = (sV, sV.n)

                            def mk_vtr(qt=qt, vf=vf, vf_ready=vf_ready):
                                def vtr():
                                    for j in range(4):
                                        tau = qt * 4 + j
                                        ti_ = 6 + tau % 2
                                        tb = bacq(ti_)
                                        T.waits([vf_ready])
                                        T.transpose(tb[:, 0:128],
                                                    vf[:, j * 128:(j + 1) * 128],
                                                    ident[:, :]).inc(sT)
                                        Sc.wait(sT)
                                        Sc.copy(bass.AP(vaugP.tensor,
                                                        vaugP.offset + tau * 130,
                                                        [[16384, 128], [65, 2], [1, 64]]),
                                                tb[:, 0:128].rearrange(
                                                    "p (h d) -> p h d", h=2)
                                                ).inc(sS)
                                        bank_free[ti_] = [(sS, sS.n)]
                                return vtr
                            if state.get("vtr") is not None:
                                state["vtr"]()
                            state["vtr"] = mk_vtr()
                if state.get("vtr") is not None:
                    state["vtr"]()
                    state["vtr"] = None
                att_S = sS.n
                state["proj_done_mark"] = sT.n

                def mk_qrep(hi_, qc, g):
                    par = g % 2

                    def build():
                        V.wait(sS, att_S)
                        V.wait(sV)
                        V.waits(state["qrep_free"][par])
                        so = hi_ * S
                        qh = q_stack[0:64, so + qc * QT: so + qc * QT + QT]
                        ql = q_stack[64:128, so + qc * QT: so + qc * QT + QT]
                        V.tensor_copy(qrep_hi2[par][0:64, :], qh).inc(sV)
                        V.tensor_copy(qrep_hi2[par][64:128, :], qh).inc(sV)
                        V.tensor_copy(qrep_lo2[par][0:64, :], ql).inc(sV)
                        V.tensor_copy(qrep_lo2[par][64:128, :], ql).inc(sV)
                        state["qrep_done"][par] = sV.n
                    return build

                for hi_ in range(2):
                    stk_off = hi_ * S
                    for qc in range(NQT):
                        g = state["g"]
                        state["g"] += 1
                        par = g % 2
                        qrep_hi, qrep_lo = qrep_hi2[par], qrep_lo2[par]
                        n_sb = n_sb2[par]
                        wvb = [2, 3, 6][g % 3]
                        if state["qrep_next"] is None:
                            mk_qrep(hi_, qc, g)()
                        if (hi_, qc) == (0, 1) and pair + 1 < NCH:
                            state["splits"] = load_split_pair(
                                pair + 1, state["proj_done_mark"])
                        if qc < NQT - 1:
                            state["qrep_next"] = mk_qrep(hi_, qc + 1, g + 1)
                        elif hi_ == 0:
                            state["qrep_next"] = mk_qrep(1, 0, g + 1)
                        else:
                            state["qrep_next"] = None
                        qrep_done = state["qrep_done"][par]
                        b2 = bacq(wvb)
                        sig_marks = {}
                        wv_marks = {}

                        def emit_scores(kt, qrep_done=qrep_done):
                            bsc = bacq(kt % 2)
                            lT = k_stack[:, stk_off + kt * 128: stk_off + (kt + 1) * 128]
                            T.wait(sV, qrep_done)
                            T.matmul(bsc[:, :], lT, qrep_hi[:, :], start=True, stop=False)
                            T.matmul(bsc[:, :], lT, qrep_lo[:, :], start=False,
                                     stop=True).inc(sT)
                            return sT.n

                        def emit_sig(kt, sc_mark, g=g):
                            bufi = (g * NKT + kt) % 3
                            sgb = sig[bufi]
                            Sc.wait(sT, sc_mark)
                            Sc.waits(state["sig_free"][bufi])
                            Sc.activation(sgb[:, :], psum[kt % 2][:, :], AF.Sigmoid,
                                          scale=0.125).inc(sS)
                            bank_free[kt % 2] = [(sS, sS.n)]
                            sig_marks[kt] = sS.n

                        def emit_wv(kt, g=g, b2=b2, hi_=hi_, sig_marks=sig_marks,
                                    wv_marks=wv_marks):
                            bufi = (g * NKT + kt) % 3
                            T.wait(sS, sig_marks[kt])
                            lhs = bass.AP(vaugP.tensor,
                                          vaugP.offset + kt * 130 + hi_ * 65,
                                          [[16384, 128], [1, 65]])
                            T.matmul(b2[0:65, :], lhs,
                                     sig[bufi][:, :], start=(kt == 0),
                                     stop=(kt == NKT - 1)).inc(sT)
                            wv_marks[kt] = sT.n
                            state["sig_free"][bufi] = [(sT, sT.n)]

                        def mk_tail(g=g, b2=b2, n_sb=n_sb, wvb=wvb, hi_=hi_, qc=qc,
                                    pair=pair, wv_marks=wv_marks):
                            dst = B[hi_ * 64: hi_ * 64 + 64,
                                    pair * S + qc * QT: pair * S + qc * QT + QT]

                            def tail_norm():
                                V.wait(sT, wv_marks[NKT - 1])
                                V.tensor_scalar(n_sb[0:1, 0:512], b2[64:65, :],
                                                ATT_EPS, None, ALU.add).inc(sV)
                                nx = sV.n

                                def pending():
                                    b3 = bacq(4)
                                    T.wait(sV, nx)
                                    T.matmul(b3[0:64, :], ones64[:, :],
                                             n_sb[0:1, 0:512], start=True,
                                             stop=True).inc(sT)
                                    V.wait(sT)
                                    V.wait(sV)
                                    V.reciprocal(recip[0:64, 0:512],
                                                 b3[0:64, :]).inc(sV)
                                    bank_free[4] = [(sV, sV.n)]
                                    V.wait(sV)
                                    V.tensor_tensor(dst, b2[0:64, :],
                                                    recip[0:64, 0:512],
                                                    ALU.mult).inc(sV)
                                    bank_free[wvb] = [(sV, sV.n)]
                                state["pending"] = pending
                            return tail_norm

                        carry = state["carry"]
                        for kt in range(NKT):
                            sc_mark = emit_scores(kt)
                            if kt <= 1 and carry is not None:
                                carry(kt)
                            emit_sig(kt, sc_mark)
                            if kt == 1 and state["tail"] is not None:
                                state["tail"]()
                                state["tail"] = None
                            if kt >= 2:
                                emit_wv(kt - 2)
                            if kt == 4 and state["qrep_next"] is not None:
                                state["qrep_next"]()
                            if kt == 6 and state["pending"] is not None:
                                state["pending"]()
                                state["pending"] = None
                        state["qrep_free"][par] = [(sT, sT.n)]
                        if (hi_, qc) == (1, NQT - 1):
                            emit_wv(NKT - 2)
                            emit_wv(NKT - 1)
                            if state["pending"] is not None:
                                state["pending"]()
                                state["pending"] = None
                            mk_tail()()
                            state["carry"] = None
                            state["tail"] = None
                        else:
                            def mk_carry(emit_wv=emit_wv):
                                def carry_fn(k):
                                    emit_wv(NKT - 2 + k)
                                return carry_fn
                            state["carry"] = mk_carry()
                            state["tail"] = mk_tail()
            if state["pending"] is not None:
                state["pending"]()
                state["pending"] = None
            attn_V = sV.n
            if "attn" in taps:
                Y.wait(sV, attn_V)
                Y.dma_start(tap_h["attn"], B[:, :]).inc(tap_sems["attn"], 16)
                tap_marks["attn"] = (tap_sems["attn"], 16)

            for oc in range(NCH):
                buf, ib, wready = load_block(woT, oc, 8, D)
                for qc in range(NQT):
                    bi = 4 + qc % 2
                    b = bacq(bi)
                    T.waits([wready])
                    T.wait(sV, attn_V)
                    for kc in range(NCH):
                        rhs = B[:, kc * S + qc * QT: kc * S + qc * QT + QT]
                        inst = T.matmul(b[:, :], buf[:, kc * 128:(kc + 1) * 128], rhs,
                                        start=(kc == 0), stop=(kc == NCH - 1))
                        if kc == NCH - 1:
                            inst.inc(sT)
                    Sc.wait(sT)
                    Sc.copy(C[:, oc * S + qc * QT: oc * S + qc * QT + QT],
                            b[:, :]).inc(sS)
                    bank_free[bi] = [(sS, sS.n)]
                wblk_free[ib] = [(sT, sT.n)]
            wo_T, wo_S = sT.n, sS.n
            if "acc" in taps:
                Y.wait(sS, wo_S)
                Y.dma_start(tap_h["acc"], C[:, :]).inc(tap_sems["acc"], 16)
                tap_marks["acc"] = (tap_sems["acc"], 16)

            V.wait(sT, wo_T)
            if "attn" in tap_marks:
                V.waits([tap_marks["attn"]])
            Bv = B.ap().rearrange("p (c t) -> p c t", c=NCH)
            xhv = xn_hi.rearrange("p (c t) -> p c t", c=NCH)
            xlv = xn_lo.rearrange("p (c t) -> p c t", c=NCH)
            V.tensor_tensor(Bv[:, :, 0:FQT], xhv[:, :, 0:FQT], xlv[:, :, 0:FQT],
                            ALU.add).inc(sV)
            sinit0 = (sV, sV.n)
            G.wait(sT, wo_T)
            if "attn" in tap_marks:
                G.waits([tap_marks["attn"]])
            sinit = []
            for c in range(4):
                E_ = V if c < 2 else G
                sm = sV if c < 2 else sG
                for cc in (c, c + 4):
                    E_.tensor_tensor(B[:, cc * S + FQT:(cc + 1) * S],
                                     xn_hi[:, cc * S + FQT:(cc + 1) * S],
                                     xn_lo[:, cc * S + FQT:(cc + 1) * S],
                                     ALU.add).inc(sm)
                sinit.append((sm, sm.n))
            sinit_g = sG.n
            stream_ready = sV.n
            if "xnraw" in tap_marks:
                V.waits([tap_marks["xnraw"]])
            w1_marks = []
            for oc in range(NCH):
                buf, ib, wready = load_block(w1, oc, 8, D)
                E_ = V if oc < 4 else G
                sm = sV if oc < 4 else sG
                E_.waits([sinit0, sinit[oc // 2]])
                E_.wait(sG, sinit_g)
                E_.waits([wready])
                hi = w1hi.rearrange("p (oc w) -> p oc w", oc=NCH)[:, oc, :]
                lo = w1lo.rearrange("p (oc w) -> p oc w", oc=NCH)[:, oc, :]
                E_.tensor_copy(hi, buf[:, 0:1024]).inc(sm)
                E_.wait(sm)
                E_.tensor_tensor(lo, buf[:, 0:1024], hi, ALU.subtract).inc(sm)
                wblk_free[ib] = [(sm, sm.n)]
                w1_marks.append((sm, sm.n))
            for oc in range(4):
                Y.waits([sinit0, sinit[oc]])
                ap = bass.AP(w2, oc * 128, [[512, 128], [128 * 512, 8], [1, 128]])
                Y.dma_start(w2res.rearrange("p (oc w) -> p oc w", oc=4)[:, oc, :]
                            .rearrange("p (kc o) -> p kc o", kc=8), ap).inc(sf, 16)
            for oc in range(2):
                Y.waits(wblk_free[0])
                Y.waits(wblk_free[1])
                ap = bass.AP(w3, oc * 128, [[256, 128], [128 * 256, 4], [1, 128]])
                Y.dma_start(wblk0.ap()[:, oc * 512:(oc + 1) * 512]
                            .rearrange("p (kc o) -> p kc o", kc=4), ap).inc(sf, 16)
            ffn_w_ready = sf.n
            w1_ready = sV.n

            strv = Bv
            Cv = C.ap().rearrange("p (c t) -> p c t", c=NCH)
            acc_marks = [0] * NTT

            def emit_s16(cy, ti, first=False):
                t0 = ti * FQT
                if first:
                    V.waits([sinit0])
                else:
                    V.wait(sT, state["cn1_done"])
                    V.wait(sV)
                    V.wait(sG, sinit_g)
                V.tensor_copy(s16h.rearrange("p (c t) -> p c t", c=NCH),
                              strv[:, :, t0:t0 + FQT]).inc(sV)
                V.wait(sV)
                V.tensor_tensor(s16l.rearrange("p (c t) -> p c t", c=NCH),
                                strv[:, :, t0:t0 + FQT],
                                s16h.rearrange("p (c t) -> p c t", c=NCH),
                                ALU.subtract).inc(sV)
                state["s16_ready"] = sV.n

            def emit_cn1(cy, ti, ocs=range(NCH)):
                for oc in ocs:
                    bi = 4 + oc % 2
                    b = psum[bi]
                    T.waits(bank_free[bi])
                    T.waits([w1_marks[oc]])
                    T.wait(sV, state["s16_ready"])
                    passes = []
                    for kc in range(NCH):
                        lh = w1hi[:, oc * 1024 + kc * 128: oc * 1024 + (kc + 1) * 128]
                        ll = w1lo[:, oc * 1024 + kc * 128: oc * 1024 + (kc + 1) * 128]
                        rh = s16h[:, kc * FQT:(kc + 1) * FQT]
                        rl = s16l[:, kc * FQT:(kc + 1) * FQT]
                        passes += [(lh, rh), (lh, rl), (ll, rh)]
                    n = len(passes)
                    for ip, (lT, rhs) in enumerate(passes):
                        inst = T.matmul(b[:, 0:FQT], lT, rhs, start=(ip == 0),
                                        stop=(ip == n - 1))
                        if ip == n - 1:
                            inst.inc(sT)
                    Sc.wait(sT)
                    Sc.wait(sc_, consts_n)
                    Sc.wait(sG, sinit_g)
                    Sc.wait(sV, stream_ready)
                    Sc.activation(g1[:, oc * FQT:(oc + 1) * FQT], b[:, 0:FQT],
                                  AF.Gelu, bias=cnb1[:, oc:oc + 1]).inc(sS)
                    bank_free[bi] = [(sS, sS.n)]
                if list(ocs)[-1] == NCH - 1:
                    state["g1_ready"] = sS.n
                    state["cn1_done"] = sT.n

            state["g2_guard"] = list(wblk_free[1])
            emit_s16(0, 0, first=True)
            emit_cn1(0, 0)
            for cy in range(3):
                for ti in range(NFQT):
                    t0 = ti * FQT
                    g1_ready = state["g1_ready"]
                    if state.get("ffn_fin") is not None:
                        state["ffn_fin"]()
                        state["ffn_fin"] = None

                    for oc in range(4):
                        bi = 4 + oc % 2
                        b = bacq(bi)
                        T.wait(sf, ffn_w_ready)
                        T.wait(sS, g1_ready)
                        for kc in range(NCH):
                            inst = T.matmul(b[:, 0:FQT],
                                            w2res[:, oc * 1024 + kc * 128:
                                                  oc * 1024 + (kc + 1) * 128],
                                            g1[:, kc * FQT:(kc + 1) * FQT],
                                            start=(kc == 0), stop=(kc == NCH - 1))
                            if kc == NCH - 1:
                                inst.inc(sT)
                        Sc.wait(sT)
                        Sc.waits(state.get("g2_guard", []))
                        Sc.activation(wblk1[:, oc * FQT:(oc + 1) * FQT], b[:, 0:FQT],
                                      AF.Gelu, bias=cnb2[:, oc:oc + 1]).inc(sS)
                        bank_free[bi] = [(sS, sS.n)]
                    g2_ready = sS.n

                    for oc in range(2):
                        bi = 4 + oc % 2
                        b = bacq(bi)
                        T.wait(sS, g2_ready)
                        for kc in range(4):
                            inst = T.matmul(b[:, 0:FQT],
                                            wblk0[:, oc * 512 + kc * 128:
                                                  oc * 512 + (kc + 1) * 128],
                                            wblk1[:, kc * FQT:(kc + 1) * FQT],
                                            start=(kc == 0), stop=(kc == 3))
                            if kc == 3:
                                inst.inc(sT)
                        Sc.wait(sT)
                        Sc.activation(g3b[:, oc * FQT:(oc + 1) * FQT], b[:, 0:FQT],
                                      AF.Gelu, bias=cnb3[:, oc:oc + 1]).inc(sS)
                        bank_free[bi] = [(sS, sS.n)]
                    g3_ready = sS.n

                    nxt = (cy, ti + 1) if ti + 1 < NFQT else                           ((cy + 1, 0) if cy < 2 else None)
                    if nxt is not None:
                        emit_s16(*nxt)
                        emit_cn1(*nxt, ocs=range(0, 1))

                    b7 = bacq(7)
                    T.wait(sS, g3_ready)
                    T.wait(sc_, consts_n)
                    for kc in range(2):
                        inst = T.matmul(b7[0:1, 0:FQT], w4gc[:, kc:kc + 1],
                                        g3b[:, kc * FQT:(kc + 1) * FQT],
                                        start=(kc == 0), stop=(kc == 1))
                        if kc == 1:
                            inst.inc(sT)
                    Sc.wait(sT)
                    Sc.wait(sV, kconst_done)
                    Sc.wait(sV)
                    Sc.activation(tf_sb[0:1, 0:FQT], b7[0:1, 0:FQT], AF.Sigmoid,
                                  bias=sc3[0:1, 0:1]).inc(sS)
                    bank_free[7] = [(sS, sS.n)]
                    V.wait(sS)
                    V.wait(sT)
                    V.tensor_scalar(cp_sb[0:1, 0:FQT], tf_sb[0:1, 0:FQT],
                                    gfb[0:1, 1:2], gfb[0:1, 2:3],
                                    ALU.mult, ALU.add).inc(sV)
                    cp_ready = sV.n
                    if nxt is not None:
                        emit_cn1(*nxt, ocs=range(1, NCH))

                    def mk_fin(cy=cy, ti=ti, t0=t0, cp_ready=cp_ready):
                        def fin():
                            b6 = bacq(6)
                            T.wait(sV, cp_ready)
                            T.matmul(b6[:, 0:FQT], ones128[:, :], cp_sb[0:1, 0:FQT],
                                     start=True, stop=True).inc(sT)
                            cp_bcast = sT.n
                            for (a, bb, sa, sb_) in [(0, 4, 1.0, -1.0),
                                                     (1, 5, 1.0, -1.0),
                                                     (2, 6, -1.0, 1.0),
                                                     (3, 7, -1.0, 1.0)]:
                                olda = strv[:, a, t0:t0 + FQT]
                                oldb = strv[:, bb, t0:t0 + FQT]
                                V.waits(bank_free[0])
                                V.waits(bank_free[1])
                                V.wait(sT, cp_bcast)
                                V.scalar_tensor_tensor(psum[0][:, 0:FQT], oldb, sa,
                                                       b6[:, 0:FQT], ALU.mult,
                                                       ALU.mult).inc(sV)
                                V.scalar_tensor_tensor(psum[1][:, 0:FQT], olda, sb_,
                                                       b6[:, 0:FQT], ALU.mult,
                                                       ALU.mult).inc(sV)
                                V.wait(sV)
                                V.tensor_tensor(olda, olda, psum[0][:, 0:FQT],
                                                ALU.add).inc(sV)
                                V.tensor_tensor(oldb, oldb, psum[1][:, 0:FQT],
                                                ALU.add).inc(sV)
                                bank_free[0] = [(sV, sV.n)]
                                bank_free[1] = [(sV, sV.n)]
                            bank_free[6] = [(sV, sV.n)]
                            if cy == 2:
                                V.wait(sS, wo_S)
                                if "acc" in tap_marks:
                                    V.waits([tap_marks["acc"]])
                                for half in range(2):
                                    t = 2 * ti + half
                                    V.wait(sV)
                                    V.tensor_tensor(Cv[:, :, t * 128:(t + 1) * 128],
                                                    Cv[:, :, t * 128:(t + 1) * 128],
                                                    Bv[:, :, t * 128:(t + 1) * 128],
                                                    ALU.add).inc(sV)
                                    acc_marks[t] = sV.n
                        return fin

                    state["ffn_fin"] = mk_fin()
            if state.get("ffn_fin") is not None:
                state["ffn_fin"]()
                state["ffn_fin"] = None
            ffn_done = sV.n
            if "stream" in taps:
                Y.wait(sV, ffn_done)
                Y.dma_start(tap_h["stream"], B[:, :]).inc(tap_sems["stream"], 16)
                tap_marks["stream"] = (tap_sems["stream"], 16)

            for t in range(NTT):
                xre_t = B.ap().rearrange("p (c t) -> p c t", c=NCH)[:, :, t * 128:(t + 1) * 128]
                xsrc = bass.AP(x_h, t * 128 * 1024,
                               [[1024, 128], [128, 8], [1, 128]])
                Sc.wait(sV, acc_marks[t])
                if "stream" in tap_marks:
                    Sc.waits([tap_marks["stream"]])
                Sc.waits(state.get(("xre_prev", t % 2), []))
                Sc.dma_start(xre_t, xsrc).inc(sxr[t % 2], 16)
                xmark = (sxr[t % 2], sxr[t % 2].n)
                state[("xre_prev", t % 2)] = [xmark]
                for c in range(NCH):
                    bi = 2 + (t * NCH + c) % 2
                    b = bacq(bi)
                    T.wait(sV, acc_marks[t])
                    T.transpose(b[:, 0:128],
                                C[:, c * S + t * 128: c * S + (t + 1) * 128],
                                ident[:, :]).inc(sT)
                    V.wait(sT)
                    V.waits([xmark])
                    V.tensor_tensor(xre_t[:, c, :], b[:, 0:128], xre_t[:, c, :],
                                    ALU.add).inc(sV)
                    bank_free[bi] = [(sV, sV.n)]
                Y.wait(sV)
                Y.dma_start(bass.AP(out_h, t * 128 * 1024,
                                    [[1024, 128], [128, 8], [1, 128]]),
                            xre_t).inc(stp[t % 2], 16)
            Y.wait(stp[0], stp[0].n)
            Y.wait(stp[1], stp[1].n)
            for ts_ in tap_sems.values():
                Y.wait(ts_, ts_.n)
            return

        @block.sync
        def _(eng):
            emit("sync", eng)

        @block.tensor
        def _(eng):
            emit("tensor", eng)

        @block.vector
        def _(eng):
            emit("vector", eng)

        @block.scalar
        def _(eng):
            emit("scalar", eng)

        @block.gpsimd
        def _(eng):
            emit("gpsimd", eng)

    return nc


_NC_CACHE = {}


def _get_nc(taps=()):
    key = tuple(sorted(taps))
    if key not in _NC_CACHE:
        _NC_CACHE[key] = build_nc(taps)
    return _NC_CACHE[key]


def _prep_shared(inputs):
    f32 = np.float32
    g = lambda k: np.asarray(inputs[k], dtype=f32)
    shared = {
        "wqT": np.ascontiguousarray(g("L0_attn_Wq").T),
        "wkT": np.ascontiguousarray(g("L0_attn_Wk").T),
        "wvT": np.ascontiguousarray(g("L0_attn_Wv").T),
        "woT": np.ascontiguousarray(g("L0_attn_Wo").T),
        "w1": np.ascontiguousarray(g("L0_ffn_cn_w1")),
        "w2": np.ascontiguousarray(g("L0_ffn_cn_w2")),
        "w3": np.ascontiguousarray(g("L0_ffn_cn_w3")),
        "gw1": np.ascontiguousarray(g("L0_ffn_gc_w1")),
        "gw2": np.ascontiguousarray(g("L0_ffn_gc_w2")),
        "ident": np.eye(128, dtype=f32),
        "cnb1": np.ascontiguousarray(g("L0_ffn_cn_b1").reshape(8, 128).T),
        "cnb2": np.ascontiguousarray(g("L0_ffn_cn_b2").reshape(4, 128).T),
        "cnb3": np.ascontiguousarray(g("L0_ffn_cn_b3").reshape(2, 128).T),
        "gb1": np.ascontiguousarray(g("L0_ffn_gc_b1").reshape(4, 128).T),
        "gb2": np.ascontiguousarray(g("L0_ffn_gc_b2").reshape(2, 128).T),
        "sc3": np.stack([g("L0_ffn_cn_b4").reshape(()),
                         g("L0_ffn_gc_b3").reshape(()),
                         g("L0_ffn_adaptive_range").reshape(())]).reshape(1, 3)
        .astype(f32),
        "w4gc": np.concatenate([
            np.ascontiguousarray(g("L0_ffn_cn_w4").reshape(2, 128).T),
            np.ascontiguousarray(g("L0_ffn_gc_w3").reshape(2, 128).T)], axis=1)
        .astype(f32),
    }
    return shared


def kernel(**inputs):
    x = np.asarray(inputs["L0_attn_x"], dtype=np.float32)
    shared = _prep_shared(inputs)
    in_maps = []
    for i in range(8):
        m = dict(shared)
        m["x"] = np.ascontiguousarray(x[i])
        in_maps.append(m)
    nc = _get_nc()
    res = run_bass_kernel_spmd(nc, in_maps, core_ids=list(range(8)))
    return np.stack([np.asarray(res.results[i]["out"]) for i in range(8)], axis=0)
